# version 32
# baseline (speedup 1.0000x reference)
import contextlib
import numpy as np
import concourse.bass as bass
import concourse.mybir as mybir
from concourse.bass_utils import run_bass_kernel_spmd

F32 = mybir.dt.float32
BF16 = mybir.dt.bfloat16
AF = mybir.ActivationFunctionType
ALU = mybir.AluOpType

N_CORES = 8
TOK = 2048
T = 512
NT = TOK // T
DM = 1024
DC = DM // 128
DH = 4096
HC = DH // 128
EPS = 1e-5

WSPECS = {
    "w1": (1024, 4096), "w2": (1024, 4096),
    "h1": (4096, 4096), "h2": (4096, 4096),
    "w3": (4096, 1024),
    "win": (1024, 2048), "wout": (1024, 1024),
}
FP8 = mybir.dt.float8e4
TERNARY = {"w1", "w2", "h1", "h2", "w3"}


def _panels(w):
    O, K = w.shape
    KC, OC = K // 128, O // 128
    wt = np.ascontiguousarray(np.asarray(w, dtype=np.float32).T)
    return np.ascontiguousarray(wt.reshape(KC, 128, OC, 128).transpose(2, 1, 0, 3))


def _coltile(v, noc):
    v = np.asarray(v, dtype=np.float32)
    return np.ascontiguousarray(v.reshape(noc, 128).T)


class _Sem:
    def __init__(self, h):
        self.h = h
        self.n = 0


class Build:
    ENGINES = ("sync", "scalar", "vector", "tensor", "gpsimd")

    def __init__(self, nc):
        self.nc = nc
        self.prog = {e: [] for e in self.ENGINES}
        self.sems = {}

    def sem(self, name, handle):
        self.sems[name] = _Sem(handle)
        return self.sems[name]

    def em(self, eng, fn):
        self.prog[eng].append(fn)

    def wait(self, eng, sem, m):
        if not m:
            return
        s = self.sems[sem]
        self.em(eng, lambda e, s=s, m=m: e.wait_ge(s.h, m))

    def inc(self, eng, build_fn, sem, k=1):
        s = self.sems[sem]
        s.n += k
        m = s.n
        self.em(eng, lambda e, f=build_fn, s=s, k=k: f(e).then_inc(s.h, k))
        return m

    def run(self, eng_obj, eng_name):
        for f in self.prog[eng_name]:
            f(eng_obj)


def build_nc(phases=("ffn1", "ssm", "ffn2"), tap_after=None):
    nc = bass.Bass("TRN2", target_bir_lowering=False, debug=False)
    dbg_tap = None
    dbg_small = None
    if tap_after is not None:
        dbg_tap = nc.dram_tensor("dbg_tap", [DM, TOK], F32, kind="ExternalOutput").ap()
        dbg_small = nc.dram_tensor("dbg_small", [64, 11], F32, kind="ExternalOutput").ap()
        dbg_u = nc.dram_tensor("dbg_u", [64, TOK], F32, kind="ExternalOutput").ap()
        dbg_h = nc.dram_tensor("dbg_h", [64, TOK], F32, kind="ExternalOutput").ap()
        dbg_st = nc.dram_tensor("dbg_st", [65, 2560], F32, kind="ExternalOutput").ap()

    xin = nc.dram_tensor("xin", [DM, TOK], F32, kind="ExternalInput").ap()
    yout = nc.dram_tensor("yout", [DM, TOK], F32, kind="ExternalOutput").ap()
    wf32, wbf = {}, {}
    for w, (K, O) in WSPECS.items():
        KC, OC = K // 128, O // 128
        wf32[w] = nc.dram_tensor("wf_" + w, [OC, 128, KC, 128], F32, kind="ExternalInput").ap()
        wdt = FP8 if w in TERNARY else BF16
        wbf[w] = nc.dram_tensor("wb_" + w, [OC, 128, KC, 128], wdt, kind="Internal").ap()
    dins = {}
    for nm, shp in [("d_s1", [128, 32]), ("d_s2", [128, 32]), ("d_sh1", [128, 32]),
                    ("d_sh2", [128, 32]), ("d_s3", [128, 8]), ("d_bout", [128, 8]),
                    ("d_gam", [128, 8]), ("d_bet", [128, 8]), ("d_binx", [128, 8]),
                    ("d_binz", [128, 8]), ("d_alog", [64, 1]), ("d_bxp", [64, 1]),
                    ("d_kvec", [64, 8]), ("d_wxp", [128, 512]), ("d_dmat", [64, 1024]),
                    ("d_eye8", [8, 8])]:
        dins[nm] = nc.dram_tensor(nm, shp, F32, kind="ExternalInput").ap()
    ag_in = nc.dram_tensor("ag_in", [64, 1], F32, kind="Internal").ap()
    ag_out = nc.dram_tensor("ag_out", [512, 1], F32, kind="Internal", addr_space="Shared").ap()
    ag_ydi = [nc.dram_tensor(f"ag_ydi{i}", [128, 8192], BF16, kind="Internal").ap() for i in range(2)]
    ag_ydo = [nc.dram_tensor(f"ag_ydo{i}", [1024, 8192], BF16, kind="Internal", addr_space="Shared").ap() for i in range(2)]
    ag_w8i = nc.dram_tensor("ag_w8i", [64, 1], F32, kind="Internal").ap()
    ag_w8o = nc.dram_tensor("ag_w8o", [512, 1], F32, kind="Internal", addr_space="Shared").ap()

    resid = nc.alloc_sbuf_tensor("resid", [128, DC * TOK], F32)
    harena = nc.alloc_sbuf_tensor("harena", [128, 24576], BF16)
    garena = nc.alloc_sbuf_tensor("garena", [128, 32768], BF16)
    wpan = nc.alloc_sbuf_tensor("wpan", [128, 8192], BF16)
    harena_f = harena.bitcast(F32)
    garena_f = garena.bitcast(F32)
    wpan_f = wpan.bitcast(F32)
    wpan_f8 = wpan.bitcast(FP8)

    C = {}
    for nm, shp, dt in [("s1", [128, 32], F32), ("s2", [128, 32], F32),
                        ("sh1", [128, 32], F32), ("sh2", [128, 32], F32),
                        ("s3h", [128, 8], F32), ("bout", [128, 8], F32),
                        ("gam", [128, 8], F32), ("bet", [128, 8], F32),
                        ("binx", [128, 8], F32), ("binz", [128, 8], F32),
                        ("alog", [64, 1], F32), ("bxp", [64, 1], F32),
                        ("kvec", [64, 8], F32), ("texp", [64, 1], F32),
                        ("acol", [64, 1], F32), ("wq", [64, 8], F32),
                        ("fsb", [64, 8], F32), ("ccol", [64, 1], F32),
                        ("fsb8", [8, 64], F32), ("wqe", [64, 8], F32),
                        ("eye8", [8, 8], F32),
                        ("wxpf", [128, 512], F32), ("wxpb", [128, 512], BF16),
                        ("dmat", [64, 1024], F32), ("eps", [65, 1], F32), ("eps2", [65, 1], F32),
                        ("onesb", [128, 1], BF16), ("onesr", [65, 128], F32)]:
        C[nm] = nc.alloc_sbuf_tensor("c_" + nm, shp, dt)

    ps = [nc.alloc_psum_tensor(f"ps{i}", [128, 512], F32) for i in range(4)]
    psb0 = nc.alloc_psum_tensor("psb0", [128, 512], F32)
    psb1 = nc.alloc_psum_tensor("psb1", [128, 512], F32)
    psr0 = nc.alloc_psum_tensor("psr0", [128, 512], F32)
    psr1 = nc.alloc_psum_tensor("psr1", [128, 512], F32)

    resid3 = resid.reshape([128, DC, TOK])

    def resid_ap(c, t0, n=T):
        return resid[:, c * TOK + t0: c * TOK + t0 + n]

    def h_ap(oc):
        return harena[:, oc * T:(oc + 1) * T]

    def g_ap(oc):
        return garena[:, oc * T:(oc + 1) * T]

    xn_ap = g_ap

    def xb_ap(c):
        return garena[:, 4096 + c * T: 4096 + (c + 1) * T]

    def x2_ap(c):
        return garena[:, 8192 + c * T: 8192 + (c + 1) * T]

    xb3_ffn = garena.reshape([128, 64, T])[:, 8:16, :]

    def xbA_ap(c):
        return garena[:, 16384 + c * T: 16384 + (c + 1) * T]

    def x2A_ap(c):
        return garena[:, 20480 + c * T: 20480 + (c + 1) * T]

    xb3A = garena.reshape([128, 64, T])[:, 32:40, :]

    h48 = harena.reshape([128, 48, T])

    def xn0_ap(c):
        return harena[:, 20480 + c * T: 20480 + (c + 1) * T]

    def g0_ap(oc):
        if oc < 8:
            return garena[:, 8192 + oc * T: 8192 + (oc + 1) * T]
        if oc < 16:
            return garena[:, 20480 + (oc - 8) * T: 20480 + (oc - 7) * T]
        return harena[:, 16384 + (oc - 16) * T: 16384 + (oc - 15) * T]

    def xb0_ap(c):
        return harena[:, c * T:(c + 1) * T]

    xb3_0 = h48[:, 0:8, :]

    def xbL1_ap(c):
        return harena[:, 16384 + c * T: 16384 + (c + 1) * T]

    def x2L1_ap(c):
        return harena[:, 20480 + c * T: 20480 + (c + 1) * T]

    xb3L1 = h48[:, 32:40, :]

    garBv = garena[:, 0:8192].rearrange("p (k two t) -> p k two t", two=2, t=T)

    st_mu = garena_f[0:1, 12288:12800]
    st_msq = garena_f[0:1, 12800:13312]
    st_musq = garena_f[0:1, 13312:13824]
    st_std = garena_f[0:1, 13824:14336]
    st_rstd = garena_f[0:1, 14336:14848]
    tmpc = garena_f[:, 14848:15360]
    st_rstd2 = garena_f[0:1, 15360:15872]
    tmps = [garena[:, 30720:31232], garena[:, 31232:31744]]

    def xln_ap(c):
        return harena_f[:, c * T:(c + 1) * T]

    def xn2p_ap(par, c):
        base = 8192 if par == 0 else 16384
        return harena[:, base + c * T: base + (c + 1) * T]

    def xn2_ap(c):
        return harena[:, 8192 + c * T: 8192 + (c + 1) * T]

    def xp_ap(c):
        return harena[:, 12288 + c * T: 12288 + (c + 1) * T]

    def xbS_ap(c):
        return harena[:, 16384 + c * T: 16384 + (c + 1) * T]

    def x2S_ap(c):
        return harena[:, 20480 + c * T: 20480 + (c + 1) * T]

    xb3_ssm = harena.reshape([128, 48, T])[:, 32:40, :]
    x23_ssm = harena.reshape([128, 48, T])[:, 40:48, :]

    ST_ROW = [32, 64, 0, 32]

    def st_row(t, k):
        r = ST_ROW[t]
        base = 12288 + k * 512
        return garena_f[r:r + 1, base: base + 512]

    def z_ap(tile, c):
        base = (tile * 8 + c) * T
        return garena_f[:, base: base + T]

    def u_ap(t0, n):
        return wpan_f[0:64, 2048 + t0: 2048 + t0 + n]

    def hs_ap(t0, n):
        return harena_f[0:64, t0: t0 + n]

    def yd_ap(tile, kc):
        base = 4096 + kc * TOK + tile * T
        return harena[:, base: base + T]

    yd2r = harena.reshape([128, 12, 2, 1024])

    def yd2_rhs(kc, tile):
        if tile < 2:
            return yd2r[:, 2 + kc, :, tile * 256:(tile + 1) * 256]
        return garBv[:, kc, :, (tile - 2) * 256:(tile - 1) * 256]

    resid_scr = resid.rearrange("p (c j two) -> p c two j", c=DC, two=2)

    def resid_int(c, tile):
        return resid_scr[:, c, :, tile * 256:(tile + 1) * 256]

    SLOT_RANGE = {}
    for _s in range(8):
        SLOT_RANGE[("k2", _s)] = (_s * 2048, (_s + 1) * 2048)
    for _s in range(4):
        SLOT_RANGE[("k4", _s)] = (_s * 4096, (_s + 1) * 4096)

    def byte_slot(key, nbytes, dt):
        lo, hi = SLOT_RANGE[key]
        assert nbytes <= hi - lo
        if dt == FP8:
            return wpan_f8[:, lo: lo + nbytes]
        return wpan[:, lo // 2: (lo + nbytes) // 2]

    def wdt_of(w):
        return FP8 if w in TERNARY else BF16

    B = Build(nc)
    stack = contextlib.ExitStack()
    for snm in ("sS", "sG", "sT", "sA", "sV", "sC", "sD", "sP", "sF"):
        B.sem(snm, stack.enter_context(nc.semaphore(snm)))
    sT, sA, sV = B.sems["sT"], B.sems["sA"], B.sems["sV"]

    def gdma(out, in_, **kw):
        return B.inc("gpsimd", lambda e, o=out, i=in_, kw=kw: e.dma_start(out=o, in_=i, **kw), "sG", 16)

    def gfence(tot=None):
        if tot is None:
            tot = B.sems["sG"].n
        return B.inc("gpsimd", lambda e, tot=tot: e.wait_ge(B.sems["sG"].h, tot), "sF")

    xin3 = xin.rearrange("(c p) t -> p c t", p=128)
    m_x1 = B.inc("scalar", lambda e: e.dma_start(
        out=resid3[:, :, 0:T], in_=xin3[:, :, 0:T]), "sD", 16)
    m_x = B.inc("scalar", lambda e: e.dma_start(
        out=resid3[:, :, T:TOK], in_=xin3[:, :, T:TOK]), "sD", 16)
    for cn, dn in [("s1", "d_s1"), ("s2", "d_s2"), ("sh1", "d_sh1"), ("sh2", "d_sh2"),
                   ("bout", "d_bout"), ("gam", "d_gam"), ("bet", "d_bet"),
                   ("binx", "d_binx"), ("binz", "d_binz"), ("alog", "d_alog"),
                   ("bxp", "d_bxp"), ("kvec", "d_kvec"), ("wxpf", "d_wxp"),
                   ("dmat", "d_dmat"), ("eye8", "d_eye8")]:
        B.inc("scalar", lambda e, o=C[cn][:, :], i=dins[dn]: e.dma_start(out=o, in_=i), "sD", 16)
    m_consts = B.inc("scalar", lambda e: e.dma_start(out=C["s3h"][:, :], in_=dins["d_s3"]), "sD", 16)

    conv_gate = {"w1": None, "w2": None,
                 "h1": lambda g: 16 * (4 + 2 * g),
                 "h2": lambda g: 16 * (24 + 3 * g),
                 "w3": lambda g: 16 * 60,
                 "win": lambda g: 16 * 80,
                 "wout": lambda g: 16 * 90}
    conv_step = {w: 2 if w in ("w1", "w2") else 4 for w in WSPECS}
    conv_groups = []
    for oc0 in range(0, 32, 2):
        for w in ("w1", "w2"):
            conv_groups.append((w, oc0, 2, None))
    for w, (K, O) in WSPECS.items():
        if w in ("w1", "w2"):
            continue
        KC, OC = K // 128, O // 128
        step = min(4, OC)
        for gi, oc0 in enumerate(range(0, OC, step)):
            conv_groups.append((w, oc0, step, conv_gate[w](gi)))
    conv_m = {w: [] for w in WSPECS}
    pending = []
    for gidx, (w, oc0, step, gate) in enumerate(conv_groups):
        if gate is not None:
            B.wait("gpsimd", "sS", gate)
        gdma(wbf[w][oc0:oc0 + step], wf32[w][oc0:oc0 + step], max_dma_last_dim=8192)
        pending.append(w)
        if len(pending) >= (2 if gidx < 8 else 8) or gidx == len(conv_groups) - 1:
            m = gfence()
            for w2 in pending:
                conv_m[w2].append(m)
            pending = []

    mw1 = B.inc("gpsimd", lambda e: e.collective_compute(
        "AllGather", ALU.bypass,
        replica_groups=[[0, 1, 2, 3, 4, 5, 6, 7]],
        ins=[ag_w8i.opt()], outs=[ag_w8o.opt()]), "sC", 1)
    B.wait("gpsimd", "sC", mw1)

    B.wait("vector", "sD", m_consts)
    B.em("vector", lambda e: e.memset(C["eps"][:, :], EPS))
    B.em("vector", lambda e: e.memset(C["eps2"][:, :], EPS * EPS))
    B.em("vector", lambda e: e.memset(C["onesb"][:, :], 1.0))
    B.em("vector", lambda e: e.memset(C["onesr"][:, :], 1.0))
    B.em("vector", lambda e: e.tensor_scalar_mul(C["s3h"][:, :], C["s3h"][:, :], 0.5))
    m_setup = B.inc("vector", lambda e: e.tensor_copy(C["wxpb"][:, :], C["wxpf"][:, :]), "sV")
    B.wait("scalar", "sV", m_setup)
    m_texp = B.inc("scalar", lambda e: e.activation(C["texp"][:, :], C["alog"][:, :], AF.Exp), "sA")
    B.wait("vector", "sA", m_texp)
    m_negt = B.inc("vector", lambda e: e.tensor_scalar_mul(C["acol"][:, :], C["texp"][:, :], -1.0), "sV")
    B.wait("scalar", "sV", m_negt)
    m_acol = B.inc("scalar", lambda e: e.activation(C["acol"][:, :], C["acol"][:, :], AF.Exp), "sA")
    B.wait("vector", "sD", m_consts)
    m_wq0 = B.inc("vector", lambda e: e.tensor_scalar_mul(
        C["wqe"][:, :], C["kvec"][:, :], C["texp"][:, :]), "sV")
    B.wait("scalar", "sV", m_wq0)
    m_wqe = B.inc("scalar", lambda e: e.activation(
        C["wqe"][:, :], C["wqe"][:, :], AF.Exp, scale=-1.0), "sA")

    state = {
        "harena_free": ("sT", 0),
        "garena_free": ("sT", 0),
        "psb_free": ("sV", 0),
        "psr_free": ("sV", 0),
        "bank_free": {i: ("sV", 0) for i in range(4)},
        "slot_free": {},
        "tmps_free": {0: ("sV", 0), 1: ("sV", 0)},
        "resid_ready": ("sV", 0),
    }
    B.wait("vector", "sD", m_x1)

    def emit_ln(tile, kind, alt=True, defer_apply=False, scratch=None, xn_fn=None,
                copy_wait=False):
        t0 = tile * T
        xnf = xn_fn if xn_fn is not None else xn_ap
        if kind == "ffn":
            src_c = lambda c: resid_ap(c, t0)
            if scratch is not None:
                xbf, x2f, xb3 = scratch
            else:
                xbf, x2f, xb3 = (xbA_ap, x2A_ap, xb3A) if alt else (xb_ap, x2_ap, xb3_ffn)
        elif kind in ("outer", "ssm"):
            src_c = lambda c: resid_ap(c, t0)
            xbf, x2f, xb3 = xbS_ap, x2S_ap, xb3_ssm
        else:
            src_c = xln_ap
            xbf, x2f, xb3 = xbS_ap, x2S_ap, xb3_ssm

        if kind == "inner":
            for c in range(DC - 1):
                B.em("vector", lambda e, c=c: e.tensor_copy(xbf(c), src_c(c)))
            m_xb = B.inc("vector", lambda e: e.tensor_copy(xbf(DC - 1), src_c(DC - 1)), "sV")
        else:
            if copy_wait:
                B.wait("vector", *state["garena_free"])
            src3 = resid3[:, :, t0:t0 + T]
            m_xb = B.inc("vector", lambda e, s3=src3, d3=xb3: e.tensor_copy(d3, s3), "sV")
        B.wait("scalar", "sV", m_xb)
        for c in range(DC - 1):
            B.em("scalar", lambda e, c=c: e.activation(x2f(c), xbf(c), AF.Square))
        m_x2 = B.inc("scalar", lambda e: e.activation(x2f(DC - 1), xbf(DC - 1), AF.Square), "sA")
        B.wait("tensor", "sV", max(m_xb, state["psr_free"][1]))
        for c in range(DC):
            B.em("tensor", lambda e, c=c: e.matmul(
                psr0[0:1, :], C["onesb"][:, :], xbf(c), start=(c == 0), stop=(c == DC - 1)))
        B.wait("tensor", "sA", m_x2)
        for c in range(DC - 1):
            B.em("tensor", lambda e, c=c: e.matmul(
                psr1[0:1, :], C["onesb"][:, :], x2f(c), start=(c == 0), stop=False))
        m_red = B.inc("tensor", lambda e: e.matmul(
            psr1[0:1, :], C["onesb"][:, :], x2f(DC - 1), start=False, stop=True), "sT")
        B.wait("vector", "sT", m_red)
        B.em("vector", lambda e: e.tensor_scalar_mul(st_mu, psr0[0:1, :], 1.0 / DM))
        B.em("vector", lambda e: e.tensor_mul(st_musq, st_mu, st_mu))
        m_var = B.inc("vector", lambda e: e.scalar_tensor_tensor(
            st_msq, psr1[0:1, :], 1.0 / DM, st_musq, ALU.mult, ALU.subtract), "sV")
        state["psr_free"] = ("sV", m_var)
        B.wait("scalar", "sV", m_var)
        if kind in ("ssm", "inner"):
            B.em("scalar", lambda e: e.activation(st_std, st_msq, AF.Sqrt, bias=C["eps"][0:1, :]))
            m_std = B.inc("scalar", lambda e: e.activation(
                st_rstd2, st_msq, AF.Sqrt, bias=C["eps2"][0:1, :], scale=1.0 + EPS), "sA")
        else:
            m_std = B.inc("scalar", lambda e: e.activation(st_std, st_msq, AF.Sqrt, bias=C["eps"][0:1, :]), "sA")
        B.wait("tensor", "sA", m_std)
        B.wait("tensor", "sV", state["psb_free"][1])
        B.em("tensor", lambda e: e.matmul(psb0[:, :], C["onesr"][0:1, :], st_mu, start=True, stop=True))
        if kind == "ssm":
            B.em("tensor", lambda e: e.matmul(psb1[:, :], C["onesr"][0:1, :], st_std, start=True, stop=True))
            m_bc = B.inc("tensor", lambda e: e.matmul(psr0[:, :], C["onesr"][0:1, :], st_rstd2, start=True, stop=True), "sT")
        else:
            m_bc = B.inc("tensor", lambda e: e.matmul(psb1[:, :], C["onesr"][0:1, :], st_std, start=True, stop=True), "sT")
        if kind == "ssm":
            B.wait("vector", "sT", m_bc)
            B.em("vector", lambda e: e.reciprocal(tmpc, psr0[:, :]))
            B.em("vector", lambda e: e.tensor_mul(tmpc, tmpc, psb1[:, :]))
            m_bc = B.inc("vector", lambda e: e.reciprocal(psb1[:, :], psb1[:, :]), "sV")
            state["psr_free"] = ("sV", m_bc)
            bc_sem = "sV"
        else:
            bc_sem = "sT"
        if kind == "ffn":
            def apply_fn():
                B.wait("vector", *state["garena_free"])
                B.wait("vector", "sT", m_bc)
                B.em("vector", lambda e: e.reciprocal(psb1[:, :], psb1[:, :]))
                for c in range(DC):
                    B.em("vector", lambda e, c=c: e.tensor_sub(tmpc, src_c(c), psb0[:, :]))
                    if c == DC - 1:
                        m_apply = B.inc("vector", lambda e, c=c: e.tensor_mul(
                            xnf(c), tmpc, psb1[:, :]), "sV")
                    else:
                        B.em("vector", lambda e, c=c: e.tensor_mul(xnf(c), tmpc, psb1[:, :]))
                state["psb_free"] = ("sV", m_apply)
                return ("sV", m_apply)
            if defer_apply:
                return apply_fn
            return apply_fn() + (None,)
        if bc_sem == "sT":
            B.wait("vector", "sT", m_bc)
        deferred = None
        if False:
            pass
        elif kind == "ssm":
            m_apply = 0
            for q in range(4):
                cs = [q * 2, q * 2 + 1]
                for c in cs:
                    B.em("vector", lambda e, c=c: e.tensor_sub(xln_ap(c), src_c(c), psb0[:, :]))
                    if c == cs[-1]:
                        m_vmul = B.inc("vector", lambda e, c=c: e.tensor_mul(
                            xln_ap(c), xln_ap(c), psb1[:, :]), "sV")
                    else:
                        B.em("vector", lambda e, c=c: e.tensor_mul(xln_ap(c), xln_ap(c), psb1[:, :]))
                B.wait("gpsimd", "sV", m_vmul)
                for c in cs:
                    B.em("gpsimd", lambda e, c=c: e.tensor_mul(x2S_ap(c), xln_ap(c), tmpc))
                    if c == cs[-1]:
                        m_apply = B.inc("gpsimd", lambda e, c=c: e.tensor_scalar(
                            xn2_ap(c), x2S_ap(c), C["gam"][:, c:c + 1], C["bet"][:, c:c + 1],
                            ALU.mult, ALU.add), "sP")
                    else:
                        B.em("gpsimd", lambda e, c=c: e.tensor_scalar(
                            xn2_ap(c), x2S_ap(c), C["gam"][:, c:c + 1], C["bet"][:, c:c + 1],
                            ALU.mult, ALU.add))
            deferred = [
                (lambda e, c=c, t0=t0: e.tensor_add(resid_ap(c, t0), resid_ap(c, t0), xln_ap(c)))
                for c in range(DC)]
            state["psb_free"] = ("sV", m_vmul)
            state["psr_free"] = ("sV", m_vmul)
            return ("sP", m_apply, deferred)
        elif kind == "outer":
            for c in range(DC):
                B.em("vector", lambda e, c=c: e.tensor_sub(tmpc, src_c(c), psb0[:, :]))
                if c == DC - 1:
                    m_apply = B.inc("vector", lambda e, c=c: e.tensor_mul(
                        xln_ap(c), tmpc, psb1[:, :]), "sV")
                else:
                    B.em("vector", lambda e, c=c: e.tensor_mul(xln_ap(c), tmpc, psb1[:, :]))
            deferred = [
                (lambda e, c=c, t0=t0: e.tensor_add(resid_ap(c, t0), resid_ap(c, t0), xln_ap(c)))
                for c in range(DC)]
            ret_sem = "sV"
        else:
            for c in range(DC):
                B.em("vector", lambda e, c=c: e.tensor_sub(tmpc, src_c(c), psb0[:, :]))
                if c == DC - 1:
                    m_vmul = B.inc("vector", lambda e, c=c: e.tensor_mul(
                        x2S_ap(c), tmpc, psb1[:, :]), "sV")
                else:
                    B.em("vector", lambda e, c=c: e.tensor_mul(x2S_ap(c), tmpc, psb1[:, :]))
            B.wait("gpsimd", "sV", m_vmul)
            for c in range(DC - 1):
                B.em("gpsimd", lambda e, c=c: e.tensor_scalar(
                    xn2_ap(c), x2S_ap(c), C["gam"][:, c:c + 1], C["bet"][:, c:c + 1],
                    ALU.mult, ALU.add))
            m_apply = B.inc("gpsimd", lambda e: e.tensor_scalar(
                xn2_ap(DC - 1), x2S_ap(DC - 1), C["gam"][:, DC - 1:DC], C["bet"][:, DC - 1:DC],
                ALU.mult, ALU.add), "sP")
            ret_sem = "sP"
        state["psb_free"] = (ret_sem, m_apply) if ret_sem == "sV" else ("sV", m_vmul)
        return (ret_sem, m_apply, deferred)

    def _slot_overlaps(slot_key):
        lo, hi = SLOT_RANGE[slot_key]
        return [k for k, (l2, h2) in SLOT_RANGE.items() if l2 < hi and lo < h2]

    def load_group(wname, oc0, noc, KC, slot_key):
        dt = wdt_of(wname)
        n_el = noc * KC * 128
        slot_ap = byte_slot(slot_key, n_el * (1 if dt == FP8 else 2), dt)
        waits = {}
        for k in _slot_overlaps(slot_key):
            sem, m = state["slot_free"].get(k, ("sT", 0))
            waits[sem] = max(waits.get(sem, 0), m)
        for sem, m in waits.items():
            B.wait("sync", sem, m)
        B.wait("sync", "sF", conv_m[wname][(oc0 + noc - 1) // conv_step[wname]])
        src = wbf[wname][oc0:oc0 + noc].rearrange("n p k o -> p n k o")
        return B.inc("sync", lambda e, o=slot_ap, i=src: e.dma_start(
            out=o, in_=i), "sS", 16), slot_ap

    def stage(wname, KC, OC, group, nslots, rhs_fn, evac_fn, rhs_ready, inject=None):
        for gi, oc0 in enumerate(range(0, OC, group)):
            if inject is not None:
                inject(gi)
            slot_key = ("k4", gi % nslots)
            m_load, pan = load_group(wname, oc0, group, KC, slot_key)
            B.wait("tensor", "sS", m_load)
            B.wait("tensor", rhs_ready[0], rhs_ready[1])
            for j in range(group):
                oc = oc0 + j
                bank = oc % 4
                bsem, bm = state["bank_free"][bank]
                B.wait("tensor", bsem, bm)
                for kc in range(KC - 1):
                    off = (j * KC + kc) * 128
                    B.em("tensor", lambda e, p=pan, o=off, kc=kc, b=bank: e.matmul(
                        ps[b][:, :], p[:, o:o + 128], rhs_fn(kc),
                        start=(kc == 0), stop=False))
                off = (j * KC + KC - 1) * 128
                m_pe = B.inc("tensor", lambda e, p=pan, o=off, b=bank: e.matmul(
                    ps[b][:, :], p[:, o:o + 128], rhs_fn(KC - 1),
                    start=(KC == 1), stop=True), "sT")
                if j == group - 1:
                    state["slot_free"][slot_key] = ("sT", m_pe)
                state["bank_free"][bank] = evac_fn(oc, ps[bank][:, :], m_pe)

    def emit_w12(tile, m_ln, xn_fn=None):
        xnf = xn_fn if xn_fn is not None else xn_ap
        for gi, oc0 in enumerate(range(0, HC, 2)):
            ka, kb = ("k2", (2 * gi) % 8), ("k2", (2 * gi + 1) % 8)
            m_l1, pan_a = load_group("w1", oc0, 2, DC, ka)
            m_l2, pan_b = load_group("w2", oc0, 2, DC, kb)
            B.wait("tensor", "sS", m_l2)
            B.wait("tensor", m_ln[0], m_ln[1])
            for j in range(2):
                oc = oc0 + j
                bA = (oc % 2) * 2
                bB = bA + 1
                for b in (bA, bB):
                    bsem, bm = state["bank_free"][b]
                    B.wait("tensor", bsem, bm)
                for kc in range(DC - 1):
                    off = (j * DC + kc) * 128
                    B.em("tensor", lambda e, p=pan_a, o=off, kc=kc, b=bA: e.matmul(
                        ps[b][:, :], p[:, o:o + 128], xnf(kc),
                        start=(kc == 0), stop=False))
                off = (j * DC + DC - 1) * 128
                m_pe1 = B.inc("tensor", lambda e, p=pan_a, o=off, b=bA: e.matmul(
                    ps[b][:, :], p[:, o:o + 128], xnf(DC - 1),
                    start=False, stop=True), "sT")
                for kc in range(DC - 1):
                    off = (j * DC + kc) * 128
                    B.em("tensor", lambda e, p=pan_b, o=off, kc=kc, b=bB: e.matmul(
                        ps[b][:, :], p[:, o:o + 128], xnf(kc),
                        start=(kc == 0), stop=False))
                off = (j * DC + DC - 1) * 128
                m_pe2 = B.inc("tensor", lambda e, p=pan_b, o=off, b=bB: e.matmul(
                    ps[b][:, :], p[:, o:o + 128], xnf(DC - 1),
                    start=False, stop=True), "sT")
                if j == 1:
                    state["slot_free"][ka] = ("sT", m_pe1)
                    state["slot_free"][kb] = ("sT", m_pe2)
                B.wait("scalar", "sT", m_pe1)
                B.wait("scalar", *state["tmps_free"][oc % 2])
                m_silu = B.inc("scalar", lambda e, b=bA, oc=oc: e.activation(
                    tmps[oc % 2], ps[b][:, :], AF.Silu, scale=C["s1"][:, oc:oc + 1]), "sA")
                state["bank_free"][bA] = ("sA", m_silu)
                B.wait("vector", "sT", m_pe2)
                B.wait("vector", "sA", m_silu)
                m_h = B.inc("vector", lambda e, b=bB, oc=oc: e.scalar_tensor_tensor(
                    h_ap(oc), ps[b][:, :], C["s2"][:, oc:oc + 1], tmps[oc % 2],
                    ALU.mult, ALU.mult), "sV")
                state["bank_free"][bB] = ("sV", m_h)
                state["tmps_free"][oc % 2] = ("sV", m_h)
        return sV.n

    def ffn_tile(tile, m_ln, xn_fn=None, g_fn=None, pend_mk=None, pend_at="h1",
                 apply_now=True):
        g_fn = g_fn if g_fn is not None else g_ap
        t0 = tile * T
        B.wait("tensor", *state["harena_free"])
        m_h = emit_w12(tile, m_ln, xn_fn=xn_fn)

        def ev_gelu_g(oc, p, m_pe):
            B.wait("scalar", "sT", m_pe)
            m = B.inc("scalar", lambda e, p=p, oc=oc: e.activation(
                g_fn(oc), p, AF.Gelu, scale=C["sh1"][:, oc:oc + 1]), "sA")
            return ("sA", m)

        stage("h1", HC, HC, 1, 4, h_ap, ev_gelu_g, ("sV", m_h))
        m_h1 = sA.n
        state["harena_free"] = ("sT", sT.n)
        apply_fn = pend_mk() if (pend_mk and pend_at == "h1") else None

        def ev_gelu_h(oc, p, m_pe):
            B.wait("scalar", "sT", m_pe)
            m = B.inc("scalar", lambda e, p=p, oc=oc: e.activation(
                h_ap(oc), p, AF.Gelu, scale=C["sh2"][:, oc:oc + 1]), "sA")
            return ("sA", m)

        stage("h2", HC, HC, 1, 4, g_fn, ev_gelu_h, ("sA", m_h1))
        m_h2 = sA.n
        state["garena_free"] = ("sT", sT.n)
        if pend_mk and pend_at == "h2":
            apply_fn = pend_mk()
        res = apply_fn() if (apply_fn is not None and apply_now) else apply_fn

        def ev_w3(oc, p, m_pe):
            B.wait("vector", "sT", m_pe)
            m = B.inc("vector", lambda e, p=p, oc=oc, t0=t0: e.scalar_tensor_tensor(
                resid_ap(oc, t0), p, C["s3h"][:, oc:oc + 1], resid_ap(oc, t0),
                ALU.mult, ALU.add), "sV")
            return ("sV", m)

        stage("w3", HC, DC, 1, 4, h_ap, ev_w3, ("sA", m_h2))
        state["harena_free"] = ("sT", sT.n)
        return res

    def emit_ffn(ln0_m=None, after_tile=None):
        m_lns = {0: ln0_m if ln0_m is not None else emit_ln(0, "ffn")[:2]}
        B.wait("vector", "sD", m_x)
        for tile in range(NT):
            pm = (lambda t=tile: emit_ln(t + 1, "ffn", defer_apply=True)) \
                if tile + 1 < NT else None
            r = ffn_tile(tile, m_lns[tile], pend_mk=pm)
            if r is not None:
                m_lns[tile + 1] = r
            if after_tile is not None:
                after_tile(tile)
        state["resid_ready"] = ("sV", sV.n)

    ssmA = {"pend": None, "red": {}, "var": {}, "sq": {}}

    def ssm_partA_copy(tile):
        t0 = tile * T
        src3 = resid3[:, :, t0:t0 + T]
        B.wait("vector", "sT", ssmA["red"].get(tile - 1, 0))
        m_xb = B.inc("vector", lambda e, s3=src3: e.tensor_copy(xb3_ssm, s3), "sV")
        m_x2 = B.inc("vector", lambda e: e.tensor_mul(x23_ssm, xb3_ssm, xb3_ssm), "sV")
        ssmA["pend"] = (tile, m_xb, m_x2)

    def ssm_partA_reduce():
        if ssmA["pend"] is None:
            return
        tile, m_xb, m_x2 = ssmA["pend"]
        ssmA["pend"] = None
        R = ST_ROW[tile]
        B.wait("tensor", "sV", max(m_xb, state["psr_free"][1]))
        for c in range(DC):
            B.em("tensor", lambda e, c=c, R=R: e.matmul(
                psr0[R:R + 1, :], C["onesb"][:, :], xbS_ap(c), start=(c == 0), stop=(c == DC - 1)))
        B.wait("tensor", "sV", m_x2)
        for c in range(DC - 1):
            B.em("tensor", lambda e, c=c, R=R: e.matmul(
                psr1[R:R + 1, :], C["onesb"][:, :], x2S_ap(c), start=(c == 0), stop=False))
        m_red = B.inc("tensor", lambda e, R=R: e.matmul(
            psr1[R:R + 1, :], C["onesb"][:, :], x2S_ap(DC - 1), start=False, stop=True), "sT")
        ssmA["red"][tile] = m_red
        B.wait("vector", "sT", m_red)
        B.em("vector", lambda e, t=tile, R=R: e.tensor_scalar_mul(
            st_row(t, 0), psr0[R:R + 1, :], 1.0 / DM))
        B.em("vector", lambda e, t=tile: e.tensor_mul(st_row(t, 1), st_row(t, 0), st_row(t, 0)))
        m_var = B.inc("vector", lambda e, t=tile, R=R: e.scalar_tensor_tensor(
            st_row(t, 2), psr1[R:R + 1, :], 1.0 / DM, st_row(t, 1), ALU.mult, ALU.subtract), "sV")
        ssmA["var"][tile] = m_var
        state["psr_free"] = ("sV", m_var)

    def ssm_partA(tile):
        ssm_partA_reduce()
        ssm_partA_copy(tile)

    def ssm_sqrts(ts):
        B.wait("scalar", "sV", ssmA["var"][ts[-1]])
        for t in ts:
            R = ST_ROW[t]
            B.em("scalar", lambda e, t=t, R=R: e.activation(
                st_row(t, 3), st_row(t, 2), AF.Sqrt, bias=C["eps"][R:R + 1, :]))
            m = B.inc("scalar", lambda e, t=t, R=R: e.activation(
                st_row(t, 4), st_row(t, 2), AF.Sqrt, bias=C["eps2"][R:R + 1, :], scale=1.0 + EPS), "sA")
            ssmA["sq"][t] = m

    def emit_ffn2(m_ln0, finish_stage4):
        pm1 = lambda: emit_ln(1, "ffn", defer_apply=True,
                              scratch=(xbL1_ap, x2L1_ap, xb3L1), xn_fn=xbL1_ap,
                              copy_wait=True)
        apply1 = ffn_tile(0, m_ln0, xn_fn=xn0_ap, g_fn=g0_ap,
                          pend_mk=pm1, pend_at="h2", apply_now=False)
        m_lns = {1: apply1()}
        finish_stage4()
        for tile in range(1, NT):
            pm = (lambda t=tile: emit_ln(t + 1, "ffn", defer_apply=True)) \
                if tile + 1 < NT else None
            r = ffn_tile(tile, m_lns[tile], pend_mk=pm,
                         xn_fn=xbL1_ap if tile == 1 else None)
            if r is not None:
                m_lns[tile + 1] = r
        state["resid_ready"] = ("sV", sV.n)

    def emit_ssm(m_acol):
        m_inner_l = {}
        B.wait("gpsimd", "sD", m_consts)

        def ssm_bc(tile, m_prev_apply):
            R = ST_ROW[tile]
            B.wait("tensor", "sA", ssmA["sq"][tile])
            B.wait("tensor", "sV", state["psb_free"][1])
            B.wait("tensor", "sV", state["psr_free"][1])
            B.em("tensor", lambda e, t=tile, R=R: e.matmul(
                psb0[:, :], C["onesr"][R:R + 1, :], st_row(t, 0), start=True, stop=True))
            B.em("tensor", lambda e, t=tile, R=R: e.matmul(
                psb1[:, :], C["onesr"][R:R + 1, :], st_row(t, 3), start=True, stop=True))
            m_bcT = B.inc("tensor", lambda e, t=tile, R=R: e.matmul(
                psr0[:, :], C["onesr"][R:R + 1, :], st_row(t, 4), start=True, stop=True), "sT")
            B.wait("vector", "sP", m_prev_apply)
            B.wait("vector", "sT", m_bcT)
            B.em("vector", lambda e: e.reciprocal(tmpc, psr0[:, :]))
            B.em("vector", lambda e: e.tensor_mul(tmpc, tmpc, psb1[:, :]))
            m_fix = B.inc("vector", lambda e: e.reciprocal(psb1[:, :], psb1[:, :]), "sV")
            state["psr_free"] = ("sV", m_fix)

        def ssm_apply(tile):
            t0 = tile * T
            src_c = lambda c: resid_ap(c, t0)
            par = tile % 2
            m_apply = 0
            for q in range(4):
                cs = [q * 2, q * 2 + 1]
                for c in cs:
                    B.em("vector", lambda e, c=c: e.tensor_sub(xln_ap(c), src_c(c), psb0[:, :]))
                    if c == cs[-1]:
                        m_vmul = B.inc("vector", lambda e, c=c: e.tensor_mul(
                            xln_ap(c), xln_ap(c), psb1[:, :]), "sV")
                    else:
                        B.em("vector", lambda e, c=c: e.tensor_mul(xln_ap(c), xln_ap(c), psb1[:, :]))
                B.wait("gpsimd", "sV", m_vmul)
                for c in cs:
                    B.em("gpsimd", lambda e, c=c: e.tensor_mul(x2S_ap(c), xln_ap(c), tmpc))
                    if c == cs[-1]:
                        m_apply = B.inc("gpsimd", lambda e, c=c, par=par: e.tensor_scalar(
                            xn2p_ap(par, c), x2S_ap(c), C["gam"][:, c:c + 1], C["bet"][:, c:c + 1],
                            ALU.mult, ALU.add), "sP")
                    else:
                        B.em("gpsimd", lambda e, c=c, par=par: e.tensor_scalar(
                            xn2p_ap(par, c), x2S_ap(c), C["gam"][:, c:c + 1], C["bet"][:, c:c + 1],
                            ALU.mult, ALU.add))
                for c in cs:
                    B.em("vector", lambda e, c=c, t0=t0: e.tensor_add(
                        resid_ap(c, t0), resid_ap(c, t0), xln_ap(c)))
            state["psb_free"] = ("sV", m_vmul)
            return ("sP", m_apply, None)

        if not ssmA["var"]:
            for t in range(NT - 1):
                ssm_partA(t)
            ssm_partA_reduce()
            ssm_partA_copy(NT - 1)
        ssm_sqrts([0, 1, 2])
        if dbg_small is not None:
            B.wait("gpsimd", "sA", ssmA["sq"][2])
            gdma(dbg_st, garena_f[0:65, 12288:14848])
        B.wait("vector", *state["harena_free"])
        B.wait("vector", *state["garena_free"])
        ssm_bc(0, 0)
        ssm_partA_reduce()
        ssm_sqrts([3])
        m_inner_l[0] = ssm_apply(0)
        for tile in range(NT):
            t0 = tile * T
            m_inner = m_inner_l[tile]

            def inject(gi, t=tile + 1, prev=m_inner):
                if t < NT:
                    if gi == 0:
                        ssm_bc(t, prev[1])
                    elif gi == 2:
                        m_inner_l[t] = ssm_apply(t)

            def ev_xz(oc, p, m_pe, tile=tile):
                B.wait("scalar", "sT", m_pe)
                if oc < 8:
                    m = B.inc("scalar", lambda e, p=p, oc=oc: e.activation(
                        xp_ap(oc), p, AF.Silu, bias=C["binx"][:, oc:oc + 1]), "sA")
                else:
                    m = B.inc("scalar", lambda e, p=p, oc=oc, tile=tile: e.activation(
                        z_ap(tile, oc - 8), p, AF.Sigmoid, bias=C["binz"][:, oc - 8:oc - 7]), "sA")
                return ("sA", m)

            stage("win", DC, 16, 2, 2,
                  lambda c, par=tile % 2: xn2p_ap(par, c), ev_xz,
                  (m_inner[0], m_inner[1]), inject=inject)
            m_xp = sA.n
            bank = 0
            bsem, bm = state["bank_free"][bank]
            B.wait("tensor", bsem, bm)
            B.wait("tensor", "sA", m_xp)
            for kc in range(DC - 1):
                B.em("tensor", lambda e, kc=kc, b=bank: e.matmul(
                    ps[b][0:64, :], C["wxpb"][:, kc * 64:(kc + 1) * 64], xp_ap(kc),
                    start=(kc == 0), stop=False))
            m_upe = B.inc("tensor", lambda e, b=bank: e.matmul(
                ps[b][0:64, :], C["wxpb"][:, (DC - 1) * 64:DC * 64], xp_ap(DC - 1),
                start=False, stop=True), "sT")
            B.wait("vector", "sT", m_upe)
            m_u = B.inc("vector", lambda e, t0=t0, b=bank: e.tensor_scalar_add(
                u_ap(t0, T), ps[b][0:64, :], C["bxp"][:, :]), "sV")
            state["bank_free"][bank] = ("sV", m_u)

        B.wait("vector", "sA", m_acol)
        m_scan1 = B.inc("vector", lambda e: e.tensor_tensor_scan(
            hs_ap(0, TOK), C["acol"].broadcast_to([64, TOK])[:, :], u_ap(0, TOK),
            0.0, ALU.mult, ALU.add), "sV")
        B.wait("gpsimd", "sV", m_scan1)
        m_f = gdma(ag_in, hs_ap(TOK - 1, 1))
        B.wait("gpsimd", "sG", m_f)
        m_cc = B.inc("gpsimd", lambda e: e.collective_compute(
            "AllGather", ALU.bypass,
            replica_groups=[[0, 1, 2, 3, 4, 5, 6, 7]],
            ins=[ag_in.opt()], outs=[ag_out.opt()]), "sC", 1)
        B.wait("gpsimd", "sC", m_cc)
        m_fsb = gdma(C["fsb8"][:, :], ag_out.rearrange("(r p) one -> r (p one)", p=64))
        m_fsbF = gfence()
        B.wait("tensor", "sF", m_fsbF)
        B.wait("tensor", state["psr_free"][0], state["psr_free"][1])
        m_ftr = B.inc("tensor", lambda e: e.matmul(
            psr1[0:64, 0:8], C["fsb8"][:, :], C["eye8"][:, :],
            start=True, stop=True), "sT")
        B.wait("vector", "sT", m_ftr)
        B.wait("vector", "sA", m_wqe)
        m_wf = B.inc("vector", lambda e: e.tensor_mul(
            C["wq"][:, :], C["wqe"][:, :], psr1[0:64, 0:8]), "sV")
        B.wait("vector", "sV", m_wf)
        m_c = B.inc("vector", lambda e: e.reduce_sum(
            C["ccol"][:, :], C["wq"][:, :], axis=mybir.AxisListType.X), "sV")
        B.wait("vector", "sV", m_c)
        m_scan2 = B.inc("vector", lambda e: e.tensor_tensor_scan(
            hs_ap(0, TOK), C["acol"].broadcast_to([64, TOK])[:, :], u_ap(0, TOK),
            C["ccol"][:, :], ALU.mult, ALU.add), "sV")
        if dbg_small is not None:
            B.wait("gpsimd", "sV", m_scan2)
            gdma(dbg_u, u_ap(0, TOK))
            gdma(dbg_h, hs_ap(0, TOK))
            gdma(dbg_small[:, 0:1], C["ccol"][:, :], allow_slow_non_contiguous=True)
            gdma(dbg_small[:, 1:5], C["fsb"][:, 0:4], allow_slow_non_contiguous=True)
            gdma(dbg_small[:, 5:9], C["wq"][:, 0:4], allow_slow_non_contiguous=True)
            gdma(dbg_small[:, 9:10], C["acol"][:, :], allow_slow_non_contiguous=True)
            gdma(dbg_small[:, 10:11], C["texp"][:, :], allow_slow_non_contiguous=True)

        B.wait("vector", "sA", B.sems["sA"].n)
        m_cc_ag = []
        for ti, tile in enumerate((0, 2, 1, 3)):
            t0 = tile * T
            m_yd = 0
            for oc in range(DC):
                bank = (ti * DC + oc) % 4
                bsem, bm = state["bank_free"][bank]
                B.wait("tensor", bsem, bm)
                B.wait("tensor", "sV", m_scan2)
                m_pe = B.inc("tensor", lambda e, oc=oc, t0=t0, b=bank: e.matmul(
                    ps[b][:, :], C["dmat"][:, oc * 128:(oc + 1) * 128], hs_ap(t0, T),
                    start=True, stop=True), "sT")
                B.wait("vector", "sT", m_pe)
                m_yd = B.inc("vector", lambda e, tile=tile, oc=oc, b=bank: e.tensor_mul(
                    yd_ap(tile, oc), ps[b][:, :], z_ap(tile, oc)), "sV")
                state["bank_free"][bank] = ("sV", m_yd)
            B.wait("gpsimd", "sV", m_yd)
            m_agi = gdma(
                ag_ydi[tile % 2].rearrange("p (o w t) -> p o w t", o=8, w=2)[:, :, tile // 2, :],
                harena.reshape([128, 12, 2048])[:, 2:10, tile * T:(tile + 1) * T])
            if ti % 2 == 1:
                B.wait("gpsimd", "sG", m_agi)
                if m_cc_ag:
                    B.wait("gpsimd", "sC", m_cc_ag[-1])
                m_cc_ag.append(B.inc("gpsimd", lambda e, ab=tile % 2: e.collective_compute(
                    "AllGather", ALU.bypass,
                    replica_groups=[[0, 1, 2, 3, 4, 5, 6, 7]],
                    ins=[ag_ydi[ab].opt()], outs=[ag_ydo[ab].opt()]), "sC", 1))
        m_yd_all = sV.n

        sGs = B.sems["sG"]
        m_fs = []
        for ab in range(2):
            B.wait("gpsimd", "sC", m_cc_ag[ab])
            agv = ag_ydo[ab].rearrange("(r p) (o w t) -> r p o w t", p=128, o=8, w=2)

            def em_fetch(e, agv=agv, ab=ab):
                for arm in nc.switch_core_id(engines=[e]):
                    cid = arm.logical % 8
                    q0, q1, tb = cid // 2, 4 + cid // 2, cid % 2
                    for par, q in ((0, q0), (1, q1)):
                        if ab == 0:
                            dst = yd2r[:, 2:10, par, 0:512]
                        else:
                            dst = garBv[:, :, par, :]
                        e.dma_start(out=dst, in_=agv[q, :, :, tb, :]).then_inc(sGs.h, 16)

            B.em("gpsimd", em_fetch)
            sGs.n += 32
            m_fs.append(gfence())
        m_f1 = m_fs[-1]

        m_wo = []
        for gidx in range(2):
            B.wait("sync", "sV", m_yd_all)
            B.wait("sync", "sF", conv_m["wout"][gidx])
            src = wbf["wout"][gidx * 4:(gidx + 1) * 4].rearrange("n p k o -> p n k o")
            dst = garena[:, 12288 + gidx * 4096: 12288 + (gidx + 1) * 4096]
            m_wo.append(B.inc("sync", lambda e, o=dst, i=src: e.dma_start(
                out=o, in_=i), "sS", 16))

        def wout_sl(oc, kc):
            g, j = oc // 4, oc % 4
            base = 12288 + g * 4096 + (j * DC + kc) * 128
            return garena[:, base: base + 128]

        def stage4_tile(tile):
            for oc in range(DC):
                bank = (tile * DC + oc) % 4
                bsem, bm = state["bank_free"][bank]
                B.wait("tensor", bsem, bm)
                B.wait("tensor", "sS", m_wo[oc // 4])
                B.wait("tensor", "sF", m_fs[tile // 2])
                for kc in range(DC - 1):
                    B.em("tensor", lambda e, oc=oc, kc=kc, tile=tile, b=bank: e.matmul(
                        ps[b][:, :], wout_sl(oc, kc), yd2_rhs(kc, tile),
                        start=(kc == 0), stop=False))
                m_pe = B.inc("tensor", lambda e, oc=oc, tile=tile, b=bank: e.matmul(
                    ps[b][:, :], wout_sl(oc, DC - 1), yd2_rhs(DC - 1, tile),
                    start=False, stop=True), "sT")
                B.wait("vector", "sT", m_pe)
                m_y = B.inc("vector", lambda e, oc=oc, tile=tile, b=bank: e.scalar_tensor_tensor(
                    resid_int(oc, tile), ps[b][:, :], C["bout"][:, oc:oc + 1], resid_int(oc, tile),
                    ALU.add, ALU.add), "sV")
                state["bank_free"][bank] = ("sV", m_y)

        special = "ffn2" in phases and tap_after is None
        m_ln0_next = None
        B.wait("vector", "sP", state.get("m_adds", 0))
        stage4_tile(0)
        if special:
            m_ln0_next = emit_ln(0, "ffn", scratch=(xb0_ap, x2_ap, xb3_0),
                                 xn_fn=xn0_ap)[:2]
        stage4_tile(1)
        state["harena_free"] = ("sT", sT.n)
        for k in [("k4", 2), ("k4", 3), ("k2", 4), ("k2", 5), ("k2", 6), ("k2", 7)]:
            state["slot_free"][k] = ("sV", m_scan2)

        def finish_stage4():
            stage4_tile(2)
            stage4_tile(3)
            state["garena_free"] = ("sT", sT.n)
            state["resid_ready"] = ("sV", sV.n)

        if not special:
            state["garena_free"] = ("sT", sT.n)
            finish_stage4()
            return None, None
        return m_ln0_next, finish_stage4

    def maybe_tap(ph):
        if tap_after == ph:
            B.wait("gpsimd", *state["resid_ready"])
            gdma(dbg_tap.rearrange("(c p) t -> p c t", p=128), resid3[:, :, :])

    m_ln0_ffn2, fs4 = None, None
    if "ffn1" in phases:
        emit_ffn(after_tile=ssm_partA if "ssm" in phases else None)
        maybe_tap("ffn1")
    if "ssm" in phases:
        m_ln0_ffn2, fs4 = emit_ssm(m_acol)
        maybe_tap("ssm")
    if "ffn2" in phases:
        if fs4 is not None:
            emit_ffn2(m_ln0_ffn2, fs4)
        else:
            emit_ffn()

    B.wait("gpsimd", *state["resid_ready"])
    m_out = gdma(yout.rearrange("(c p) t -> p c t", p=128), resid3[:, :, :])
    B.wait("gpsimd", "sG", m_out)

    with nc.Block() as block:
        @block.sync
        def _(e):
            B.run(e, "sync")

        @block.scalar
        def _(e):
            B.run(e, "scalar")

        @block.vector
        def _(e):
            B.run(e, "vector")

        @block.tensor
        def _(e):
            B.run(e, "tensor")

        @block.gpsimd
        def _(e):
            B.run(e, "gpsimd")

    stack.close()
    return nc


def _mk_in_maps(inputs):
    x = np.asarray(inputs["L0_attn_x"], dtype=np.float32)
    b_in = np.asarray(inputs["L0_attn_b_in"], dtype=np.float32)
    base = dict(
        d_s1=_coltile(inputs["L0_ffn_w1_s"], 32),
        d_s2=_coltile(inputs["L0_ffn_w2_s"], 32),
        d_sh1=_coltile(inputs["L0_ffn_h1_s"], 32),
        d_sh2=_coltile(inputs["L0_ffn_h2_s"], 32),
        d_s3=_coltile(inputs["L0_ffn_w3_s"], 8),
        d_bout=_coltile(inputs["L0_attn_b_out"], 8),
        d_gam=_coltile(inputs["L0_attn_gamma"], 8),
        d_bet=_coltile(inputs["L0_attn_beta"], 8),
        d_binx=_coltile(b_in[:1024], 8),
        d_binz=_coltile(b_in[1024:], 8),
        d_alog=np.ascontiguousarray(np.asarray(inputs["L0_attn_A_log"], dtype=np.float32).reshape(64, 1)),
        d_bxp=np.ascontiguousarray(np.asarray(inputs["L0_attn_b_xp"], dtype=np.float32).reshape(64, 1)),
        d_wxp=np.ascontiguousarray(
            np.asarray(inputs["L0_attn_W_xp"], dtype=np.float32).T.reshape(8, 128, 64)
            .transpose(1, 0, 2).reshape(128, 512)),
        d_dmat=np.ascontiguousarray(np.asarray(inputs["L0_attn_Dmat"], dtype=np.float32)),
        d_eye8=np.eye(8, dtype=np.float32),
        wf_w1=_panels(inputs["L0_ffn_w1_w"]),
        wf_w2=_panels(inputs["L0_ffn_w2_w"]),
        wf_h1=_panels(inputs["L0_ffn_h1_w"]),
        wf_h2=_panels(inputs["L0_ffn_h2_w"]),
        wf_w3=_panels(inputs["L0_ffn_w3_w"]),
        wf_win=_panels(inputs["L0_attn_W_in"]),
        wf_wout=_panels(inputs["L0_attn_W_out"]),
    )
    in_maps = []
    for c in range(N_CORES):
        b, r = c // 4, c % 4
        m = dict(base)
        m["xin"] = np.ascontiguousarray(x[b, r * TOK:(r + 1) * TOK, :].T)
        kv = np.zeros((64, 8), dtype=np.float32)
        for q in range(8):
            qq = q - 4 * b
            kv[:, q] = 2048.0 * (r - 1 - qq) if (0 <= qq < r) else 1e9
        m["d_kvec"] = kv
        in_maps.append(m)
    return in_maps


def kernel(**inputs):
    in_maps = _mk_in_maps(inputs)
    nc = build_nc()
    res = run_bass_kernel_spmd(nc, in_maps, core_ids=list(range(N_CORES)))
    out = np.empty((2, 8192, 1024), dtype=np.float32)
    for c in range(N_CORES):
        b, r = c // 4, c % 4
        out[b, r * TOK:(r + 1) * TOK, :] = np.asarray(res.results[c]["yout"]).T
    return out


# version 33
# speedup vs baseline: 1.0265x; 1.0265x over previous
import contextlib
import numpy as np
import concourse.bass as bass
import concourse.mybir as mybir
from concourse.bass_utils import run_bass_kernel_spmd

F32 = mybir.dt.float32
BF16 = mybir.dt.bfloat16
AF = mybir.ActivationFunctionType
ALU = mybir.AluOpType

N_CORES = 8
TOK = 2048
T = 512
NT = TOK // T
DM = 1024
DC = DM // 128
DH = 4096
HC = DH // 128
EPS = 1e-5

WSPECS = {
    "w1": (1024, 4096), "w2": (1024, 4096),
    "h1": (4096, 4096), "h2": (4096, 4096),
    "w3": (4096, 1024),
    "win": (1024, 2048), "wout": (1024, 1024),
}
FP8 = mybir.dt.float8e4
TERNARY = {"w1", "w2", "h1", "h2", "w3"}


def _panels(w):
    O, K = w.shape
    KC, OC = K // 128, O // 128
    wt = np.ascontiguousarray(np.asarray(w, dtype=np.float32).T)
    return np.ascontiguousarray(wt.reshape(KC, 128, OC, 128).transpose(2, 1, 0, 3))


def _coltile(v, noc):
    v = np.asarray(v, dtype=np.float32)
    return np.ascontiguousarray(v.reshape(noc, 128).T)


class _Sem:
    def __init__(self, h):
        self.h = h
        self.n = 0


class Build:
    ENGINES = ("sync", "scalar", "vector", "tensor", "gpsimd")

    def __init__(self, nc):
        self.nc = nc
        self.prog = {e: [] for e in self.ENGINES}
        self.sems = {}

    def sem(self, name, handle):
        self.sems[name] = _Sem(handle)
        return self.sems[name]

    def em(self, eng, fn):
        self.prog[eng].append(fn)

    def wait(self, eng, sem, m):
        if not m:
            return
        s = self.sems[sem]
        self.em(eng, lambda e, s=s, m=m: e.wait_ge(s.h, m))

    def inc(self, eng, build_fn, sem, k=1):
        s = self.sems[sem]
        s.n += k
        m = s.n
        self.em(eng, lambda e, f=build_fn, s=s, k=k: f(e).then_inc(s.h, k))
        return m

    def run(self, eng_obj, eng_name):
        for f in self.prog[eng_name]:
            f(eng_obj)


def build_nc(phases=("ffn1", "ssm", "ffn2"), tap_after=None):
    nc = bass.Bass("TRN2", target_bir_lowering=False, debug=False)
    dbg_tap = None
    dbg_small = None
    if tap_after is not None:
        dbg_tap = nc.dram_tensor("dbg_tap", [DM, TOK], F32, kind="ExternalOutput").ap()
        dbg_small = nc.dram_tensor("dbg_small", [64, 11], F32, kind="ExternalOutput").ap()
        dbg_u = nc.dram_tensor("dbg_u", [64, TOK], F32, kind="ExternalOutput").ap()
        dbg_h = nc.dram_tensor("dbg_h", [64, TOK], F32, kind="ExternalOutput").ap()
        dbg_st = nc.dram_tensor("dbg_st", [65, 2560], F32, kind="ExternalOutput").ap()

    xin = nc.dram_tensor("xin", [DM, TOK], F32, kind="ExternalInput").ap()
    yout = nc.dram_tensor("yout", [DM, TOK], F32, kind="ExternalOutput").ap()
    wf32, wbf = {}, {}
    for w, (K, O) in WSPECS.items():
        KC, OC = K // 128, O // 128
        wf32[w] = nc.dram_tensor("wf_" + w, [OC, 128, KC, 128], F32, kind="ExternalInput").ap()
        wdt = FP8 if w in TERNARY else BF16
        wbf[w] = nc.dram_tensor("wb_" + w, [OC, 128, KC, 128], wdt, kind="Internal").ap()
    dins = {}
    for nm, shp in [("d_s1", [128, 32]), ("d_s2", [128, 32]), ("d_sh1", [128, 32]),
                    ("d_sh2", [128, 32]), ("d_s3", [128, 8]), ("d_bout", [128, 8]),
                    ("d_gam", [128, 8]), ("d_bet", [128, 8]), ("d_binx", [128, 8]),
                    ("d_binz", [128, 8]), ("d_alog", [64, 1]), ("d_bxp", [64, 1]),
                    ("d_kvec", [64, 8]), ("d_wxp", [128, 512]), ("d_dmat", [64, 1024]),
                    ("d_eye8", [8, 8])]:
        dins[nm] = nc.dram_tensor(nm, shp, F32, kind="ExternalInput").ap()
    ag_in = nc.dram_tensor("ag_in", [64, 1], F32, kind="Internal").ap()
    ag_out = nc.dram_tensor("ag_out", [512, 1], F32, kind="Internal", addr_space="Shared").ap()
    ag_ydi = [nc.dram_tensor(f"ag_ydi{i}", [128, 8192], BF16, kind="Internal").ap() for i in range(2)]
    ag_ydo = [nc.dram_tensor(f"ag_ydo{i}", [1024, 8192], BF16, kind="Internal", addr_space="Shared").ap() for i in range(2)]
    ag_w8i = nc.dram_tensor("ag_w8i", [64, 1], F32, kind="Internal").ap()
    ag_w8o = nc.dram_tensor("ag_w8o", [512, 1], F32, kind="Internal", addr_space="Shared").ap()

    resid = nc.alloc_sbuf_tensor("resid", [128, DC * TOK], F32)
    harena = nc.alloc_sbuf_tensor("harena", [128, 24576], BF16)
    garena = nc.alloc_sbuf_tensor("garena", [128, 32768], BF16)
    wpan = nc.alloc_sbuf_tensor("wpan", [128, 8192], BF16)
    harena_f = harena.bitcast(F32)
    garena_f = garena.bitcast(F32)
    wpan_f = wpan.bitcast(F32)
    wpan_f8 = wpan.bitcast(FP8)

    C = {}
    for nm, shp, dt in [("s1", [128, 32], F32), ("s2", [128, 32], F32),
                        ("sh1", [128, 32], F32), ("sh2", [128, 32], F32),
                        ("s3h", [128, 8], F32), ("bout", [128, 8], F32),
                        ("gam", [128, 8], F32), ("bet", [128, 8], F32),
                        ("binx", [128, 8], F32), ("binz", [128, 8], F32),
                        ("alog", [64, 1], F32), ("bxp", [64, 1], F32),
                        ("kvec", [64, 8], F32), ("texp", [64, 1], F32),
                        ("acol", [64, 1], F32), ("wq", [64, 8], F32),
                        ("fsb", [64, 8], F32), ("ccol", [64, 1], F32),
                        ("fsb8", [8, 64], F32), ("wqe", [64, 8], F32),
                        ("eye8", [8, 8], F32),
                        ("wxpf", [128, 512], F32), ("wxpb", [128, 512], BF16),
                        ("dmat", [64, 1024], F32), ("eps", [65, 1], F32), ("eps2", [65, 1], F32),
                        ("onesb", [128, 1], BF16), ("onesr", [65, 128], F32)]:
        C[nm] = nc.alloc_sbuf_tensor("c_" + nm, shp, dt)

    ps = [nc.alloc_psum_tensor(f"ps{i}", [128, 512], F32) for i in range(4)]
    psb0 = nc.alloc_psum_tensor("psb0", [128, 512], F32)
    psb1 = nc.alloc_psum_tensor("psb1", [128, 512], F32)
    psr0 = nc.alloc_psum_tensor("psr0", [128, 512], F32)
    psr1 = nc.alloc_psum_tensor("psr1", [128, 512], F32)

    resid3 = resid.reshape([128, DC, TOK])

    def resid_ap(c, t0, n=T):
        return resid[:, c * TOK + t0: c * TOK + t0 + n]

    def h_ap(oc):
        return harena[:, oc * T:(oc + 1) * T]

    def g_ap(oc):
        return garena[:, oc * T:(oc + 1) * T]

    xn_ap = g_ap

    def xb_ap(c):
        return garena[:, 4096 + c * T: 4096 + (c + 1) * T]

    def x2_ap(c):
        return garena[:, 8192 + c * T: 8192 + (c + 1) * T]

    xb3_ffn = garena.reshape([128, 64, T])[:, 8:16, :]

    def xbA_ap(c):
        return garena[:, 16384 + c * T: 16384 + (c + 1) * T]

    def x2A_ap(c):
        return garena[:, 20480 + c * T: 20480 + (c + 1) * T]

    xb3A = garena.reshape([128, 64, T])[:, 32:40, :]

    h48 = harena.reshape([128, 48, T])

    def xn0_ap(c):
        return harena[:, 20480 + c * T: 20480 + (c + 1) * T]

    def g0_ap(oc):
        if oc < 8:
            return garena[:, 8192 + oc * T: 8192 + (oc + 1) * T]
        if oc < 16:
            return garena[:, 20480 + (oc - 8) * T: 20480 + (oc - 7) * T]
        return harena[:, 16384 + (oc - 16) * T: 16384 + (oc - 15) * T]

    def xb0_ap(c):
        return harena[:, c * T:(c + 1) * T]

    xb3_0 = h48[:, 0:8, :]

    def xbL1_ap(c):
        return harena[:, 16384 + c * T: 16384 + (c + 1) * T]

    def x2L1_ap(c):
        return harena[:, 20480 + c * T: 20480 + (c + 1) * T]

    xb3L1 = h48[:, 32:40, :]

    garBv = garena[:, 0:8192].rearrange("p (k two t) -> p k two t", two=2, t=T)

    st_mu = garena_f[0:1, 12288:12800]
    st_msq = garena_f[0:1, 12800:13312]
    st_musq = garena_f[0:1, 13312:13824]
    st_std = garena_f[0:1, 13824:14336]
    st_rstd = garena_f[0:1, 14336:14848]
    tmpc = garena_f[:, 14848:15360]
    st_rstd2 = garena_f[0:1, 15360:15872]
    tmps = [garena[:, 30720:31232], garena[:, 31232:31744]]

    def xln_ap(c):
        return harena_f[:, c * T:(c + 1) * T]

    def xn2p_ap(par, c):
        base = 8192 if par == 0 else 16384
        return harena[:, base + c * T: base + (c + 1) * T]

    def xn2_ap(c):
        return harena[:, 8192 + c * T: 8192 + (c + 1) * T]

    def xp_ap(c):
        return harena[:, 12288 + c * T: 12288 + (c + 1) * T]

    def xbS_ap(c):
        return harena[:, 16384 + c * T: 16384 + (c + 1) * T]

    def x2S_ap(c):
        return harena[:, 20480 + c * T: 20480 + (c + 1) * T]

    xb3_ssm = harena.reshape([128, 48, T])[:, 32:40, :]
    x23_ssm = harena.reshape([128, 48, T])[:, 40:48, :]

    ST_ROW = [32, 64, 0, 32]

    def st_row(t, k):
        r = ST_ROW[t]
        base = 12288 + k * 512
        return garena_f[r:r + 1, base: base + 512]

    def z_ap(tile, c):
        base = (tile * 8 + c) * T
        return garena_f[:, base: base + T]

    def u_ap(t0, n):
        return wpan_f[0:64, 2048 + t0: 2048 + t0 + n]

    def hs_ap(t0, n):
        return harena_f[0:64, t0: t0 + n]

    def yd_ap(tile, kc):
        base = 4096 + kc * TOK + tile * T
        return harena[:, base: base + T]

    yd2r = harena.reshape([128, 12, 2, 1024])

    def yd2_rhs(kc, tile):
        if tile < 2:
            return yd2r[:, 2 + kc, :, tile * 256:(tile + 1) * 256]
        return garBv[:, kc, :, (tile - 2) * 256:(tile - 1) * 256]

    resid_scr = resid.rearrange("p (c j two) -> p c two j", c=DC, two=2)

    def resid_int(c, tile):
        return resid_scr[:, c, :, tile * 256:(tile + 1) * 256]

    SLOT_RANGE = {}
    for _s in range(8):
        SLOT_RANGE[("k2", _s)] = (_s * 2048, (_s + 1) * 2048)
    for _s in range(4):
        SLOT_RANGE[("k4", _s)] = (_s * 4096, (_s + 1) * 4096)

    def byte_slot(key, nbytes, dt):
        lo, hi = SLOT_RANGE[key]
        assert nbytes <= hi - lo
        if dt == FP8:
            return wpan_f8[:, lo: lo + nbytes]
        return wpan[:, lo // 2: (lo + nbytes) // 2]

    def wdt_of(w):
        return FP8 if w in TERNARY else BF16

    B = Build(nc)
    stack = contextlib.ExitStack()
    for snm in ("sS", "sG", "sT", "sA", "sV", "sC", "sD", "sP", "sF"):
        B.sem(snm, stack.enter_context(nc.semaphore(snm)))
    sT, sA, sV = B.sems["sT"], B.sems["sA"], B.sems["sV"]

    def gdma(out, in_, **kw):
        return B.inc("gpsimd", lambda e, o=out, i=in_, kw=kw: e.dma_start(out=o, in_=i, **kw), "sG", 16)

    def gfence(tot=None):
        if tot is None:
            tot = B.sems["sG"].n
        return B.inc("gpsimd", lambda e, tot=tot: e.wait_ge(B.sems["sG"].h, tot), "sF")

    xin3 = xin.rearrange("(c p) t -> p c t", p=128)
    m_x1 = B.inc("scalar", lambda e: e.dma_start(
        out=resid3[:, :, 0:T], in_=xin3[:, :, 0:T]), "sD", 16)
    m_x = B.inc("scalar", lambda e: e.dma_start(
        out=resid3[:, :, T:TOK], in_=xin3[:, :, T:TOK]), "sD", 16)
    for cn, dn in [("s1", "d_s1"), ("s2", "d_s2"), ("sh1", "d_sh1"), ("sh2", "d_sh2"),
                   ("bout", "d_bout"), ("gam", "d_gam"), ("bet", "d_bet"),
                   ("binx", "d_binx"), ("binz", "d_binz"), ("alog", "d_alog"),
                   ("bxp", "d_bxp"), ("kvec", "d_kvec"), ("wxpf", "d_wxp"),
                   ("dmat", "d_dmat"), ("eye8", "d_eye8")]:
        B.inc("scalar", lambda e, o=C[cn][:, :], i=dins[dn]: e.dma_start(out=o, in_=i), "sD", 16)
    m_consts = B.inc("scalar", lambda e: e.dma_start(out=C["s3h"][:, :], in_=dins["d_s3"]), "sD", 16)

    conv_gate = {"w1": None, "w2": None,
                 "h1": lambda g: 16 * (4 + 2 * g),
                 "h2": lambda g: 16 * (24 + 3 * g),
                 "w3": lambda g: 16 * 60,
                 "win": lambda g: 16 * 80,
                 "wout": lambda g: 16 * 90}
    conv_step = {w: 2 if w in ("w1", "w2") else 4 for w in WSPECS}
    conv_groups = []
    for oc0 in range(0, 32, 2):
        for w in ("w1", "w2"):
            conv_groups.append((w, oc0, 2, None))
    for w, (K, O) in WSPECS.items():
        if w in ("w1", "w2"):
            continue
        KC, OC = K // 128, O // 128
        step = min(4, OC)
        for gi, oc0 in enumerate(range(0, OC, step)):
            conv_groups.append((w, oc0, step, conv_gate[w](gi)))
    conv_m = {w: [] for w in WSPECS}
    pending = []
    for gidx, (w, oc0, step, gate) in enumerate(conv_groups):
        if gate is not None:
            B.wait("gpsimd", "sS", gate)
        gdma(wbf[w][oc0:oc0 + step], wf32[w][oc0:oc0 + step], max_dma_last_dim=8192)
        pending.append(w)
        if len(pending) >= (2 if gidx < 32 else 8) or gidx == len(conv_groups) - 1:
            m = gfence()
            for w2 in pending:
                conv_m[w2].append(m)
            pending = []

    mw1 = B.inc("gpsimd", lambda e: e.collective_compute(
        "AllGather", ALU.bypass,
        replica_groups=[[0, 1, 2, 3, 4, 5, 6, 7]],
        ins=[ag_w8i.opt()], outs=[ag_w8o.opt()]), "sC", 1)
    B.wait("gpsimd", "sC", mw1)

    B.wait("vector", "sD", m_consts)
    B.em("vector", lambda e: e.memset(C["eps"][:, :], EPS))
    B.em("vector", lambda e: e.memset(C["eps2"][:, :], EPS * EPS))
    B.em("vector", lambda e: e.memset(C["onesb"][:, :], 1.0))
    B.em("vector", lambda e: e.memset(C["onesr"][:, :], 1.0))
    B.em("vector", lambda e: e.tensor_scalar_mul(C["s3h"][:, :], C["s3h"][:, :], 0.5))
    m_setup = B.inc("vector", lambda e: e.tensor_copy(C["wxpb"][:, :], C["wxpf"][:, :]), "sV")
    B.wait("scalar", "sV", m_setup)
    m_texp = B.inc("scalar", lambda e: e.activation(C["texp"][:, :], C["alog"][:, :], AF.Exp), "sA")
    B.wait("vector", "sA", m_texp)
    m_negt = B.inc("vector", lambda e: e.tensor_scalar_mul(C["acol"][:, :], C["texp"][:, :], -1.0), "sV")
    B.wait("scalar", "sV", m_negt)
    m_acol = B.inc("scalar", lambda e: e.activation(C["acol"][:, :], C["acol"][:, :], AF.Exp), "sA")
    B.wait("vector", "sD", m_consts)
    m_wq0 = B.inc("vector", lambda e: e.tensor_scalar_mul(
        C["wqe"][:, :], C["kvec"][:, :], C["texp"][:, :]), "sV")
    B.wait("scalar", "sV", m_wq0)
    m_wqe = B.inc("scalar", lambda e: e.activation(
        C["wqe"][:, :], C["wqe"][:, :], AF.Exp, scale=-1.0), "sA")

    state = {
        "harena_free": ("sT", 0),
        "garena_free": ("sT", 0),
        "psb_free": ("sV", 0),
        "psr_free": ("sV", 0),
        "bank_free": {i: ("sV", 0) for i in range(4)},
        "slot_free": {},
        "tmps_free": {0: ("sV", 0), 1: ("sV", 0)},
        "resid_ready": ("sV", 0),
    }
    B.wait("vector", "sD", m_x1)

    def emit_ln(tile, kind, alt=True, defer_apply=False, scratch=None, xn_fn=None,
                copy_wait=False):
        t0 = tile * T
        xnf = xn_fn if xn_fn is not None else xn_ap
        if kind == "ffn":
            src_c = lambda c: resid_ap(c, t0)
            if scratch is not None:
                xbf, x2f, xb3 = scratch
            else:
                xbf, x2f, xb3 = (xbA_ap, x2A_ap, xb3A) if alt else (xb_ap, x2_ap, xb3_ffn)
        elif kind in ("outer", "ssm"):
            src_c = lambda c: resid_ap(c, t0)
            xbf, x2f, xb3 = xbS_ap, x2S_ap, xb3_ssm
        else:
            src_c = xln_ap
            xbf, x2f, xb3 = xbS_ap, x2S_ap, xb3_ssm

        if kind == "inner":
            for c in range(DC - 1):
                B.em("vector", lambda e, c=c: e.tensor_copy(xbf(c), src_c(c)))
            m_xb = B.inc("vector", lambda e: e.tensor_copy(xbf(DC - 1), src_c(DC - 1)), "sV")
        else:
            if copy_wait:
                B.wait("vector", *state["garena_free"])
            src3 = resid3[:, :, t0:t0 + T]
            m_xb = B.inc("vector", lambda e, s3=src3, d3=xb3: e.tensor_copy(d3, s3), "sV")
        B.wait("scalar", "sV", m_xb)
        for c in range(DC - 1):
            B.em("scalar", lambda e, c=c: e.activation(x2f(c), xbf(c), AF.Square))
        m_x2 = B.inc("scalar", lambda e: e.activation(x2f(DC - 1), xbf(DC - 1), AF.Square), "sA")
        B.wait("tensor", "sV", max(m_xb, state["psr_free"][1]))
        for c in range(DC):
            B.em("tensor", lambda e, c=c: e.matmul(
                psr0[0:1, :], C["onesb"][:, :], xbf(c), start=(c == 0), stop=(c == DC - 1)))
        B.wait("tensor", "sA", m_x2)
        for c in range(DC - 1):
            B.em("tensor", lambda e, c=c: e.matmul(
                psr1[0:1, :], C["onesb"][:, :], x2f(c), start=(c == 0), stop=False))
        m_red = B.inc("tensor", lambda e: e.matmul(
            psr1[0:1, :], C["onesb"][:, :], x2f(DC - 1), start=False, stop=True), "sT")
        B.wait("vector", "sT", m_red)
        B.em("vector", lambda e: e.tensor_scalar_mul(st_mu, psr0[0:1, :], 1.0 / DM))
        B.em("vector", lambda e: e.tensor_mul(st_musq, st_mu, st_mu))
        m_var = B.inc("vector", lambda e: e.scalar_tensor_tensor(
            st_msq, psr1[0:1, :], 1.0 / DM, st_musq, ALU.mult, ALU.subtract), "sV")
        state["psr_free"] = ("sV", m_var)
        B.wait("scalar", "sV", m_var)
        if kind in ("ssm", "inner"):
            B.em("scalar", lambda e: e.activation(st_std, st_msq, AF.Sqrt, bias=C["eps"][0:1, :]))
            m_std = B.inc("scalar", lambda e: e.activation(
                st_rstd2, st_msq, AF.Sqrt, bias=C["eps2"][0:1, :], scale=1.0 + EPS), "sA")
        else:
            m_std = B.inc("scalar", lambda e: e.activation(st_std, st_msq, AF.Sqrt, bias=C["eps"][0:1, :]), "sA")
        B.wait("tensor", "sA", m_std)
        B.wait("tensor", "sV", state["psb_free"][1])
        B.em("tensor", lambda e: e.matmul(psb0[:, :], C["onesr"][0:1, :], st_mu, start=True, stop=True))
        if kind == "ssm":
            B.em("tensor", lambda e: e.matmul(psb1[:, :], C["onesr"][0:1, :], st_std, start=True, stop=True))
            m_bc = B.inc("tensor", lambda e: e.matmul(psr0[:, :], C["onesr"][0:1, :], st_rstd2, start=True, stop=True), "sT")
        else:
            m_bc = B.inc("tensor", lambda e: e.matmul(psb1[:, :], C["onesr"][0:1, :], st_std, start=True, stop=True), "sT")
        if kind == "ssm":
            B.wait("vector", "sT", m_bc)
            B.em("vector", lambda e: e.reciprocal(tmpc, psr0[:, :]))
            B.em("vector", lambda e: e.tensor_mul(tmpc, tmpc, psb1[:, :]))
            m_bc = B.inc("vector", lambda e: e.reciprocal(psb1[:, :], psb1[:, :]), "sV")
            state["psr_free"] = ("sV", m_bc)
            bc_sem = "sV"
        else:
            bc_sem = "sT"
        if kind == "ffn":
            def apply_fn():
                B.wait("vector", *state["garena_free"])
                B.wait("vector", "sT", m_bc)
                B.em("vector", lambda e: e.reciprocal(psb1[:, :], psb1[:, :]))
                for c in range(DC):
                    B.em("vector", lambda e, c=c: e.tensor_sub(tmpc, src_c(c), psb0[:, :]))
                    if c == DC - 1:
                        m_apply = B.inc("vector", lambda e, c=c: e.tensor_mul(
                            xnf(c), tmpc, psb1[:, :]), "sV")
                    else:
                        B.em("vector", lambda e, c=c: e.tensor_mul(xnf(c), tmpc, psb1[:, :]))
                state["psb_free"] = ("sV", m_apply)
                return ("sV", m_apply)
            if defer_apply:
                return apply_fn
            return apply_fn() + (None,)
        if bc_sem == "sT":
            B.wait("vector", "sT", m_bc)
        deferred = None
        if False:
            pass
        elif kind == "ssm":
            m_apply = 0
            for q in range(4):
                cs = [q * 2, q * 2 + 1]
                for c in cs:
                    B.em("vector", lambda e, c=c: e.tensor_sub(xln_ap(c), src_c(c), psb0[:, :]))
                    if c == cs[-1]:
                        m_vmul = B.inc("vector", lambda e, c=c: e.tensor_mul(
                            xln_ap(c), xln_ap(c), psb1[:, :]), "sV")
                    else:
                        B.em("vector", lambda e, c=c: e.tensor_mul(xln_ap(c), xln_ap(c), psb1[:, :]))
                B.wait("gpsimd", "sV", m_vmul)
                for c in cs:
                    B.em("gpsimd", lambda e, c=c: e.tensor_mul(x2S_ap(c), xln_ap(c), tmpc))
                    if c == cs[-1]:
                        m_apply = B.inc("gpsimd", lambda e, c=c: e.tensor_scalar(
                            xn2_ap(c), x2S_ap(c), C["gam"][:, c:c + 1], C["bet"][:, c:c + 1],
                            ALU.mult, ALU.add), "sP")
                    else:
                        B.em("gpsimd", lambda e, c=c: e.tensor_scalar(
                            xn2_ap(c), x2S_ap(c), C["gam"][:, c:c + 1], C["bet"][:, c:c + 1],
                            ALU.mult, ALU.add))
            deferred = [
                (lambda e, c=c, t0=t0: e.tensor_add(resid_ap(c, t0), resid_ap(c, t0), xln_ap(c)))
                for c in range(DC)]
            state["psb_free"] = ("sV", m_vmul)
            state["psr_free"] = ("sV", m_vmul)
            return ("sP", m_apply, deferred)
        elif kind == "outer":
            for c in range(DC):
                B.em("vector", lambda e, c=c: e.tensor_sub(tmpc, src_c(c), psb0[:, :]))
                if c == DC - 1:
                    m_apply = B.inc("vector", lambda e, c=c: e.tensor_mul(
                        xln_ap(c), tmpc, psb1[:, :]), "sV")
                else:
                    B.em("vector", lambda e, c=c: e.tensor_mul(xln_ap(c), tmpc, psb1[:, :]))
            deferred = [
                (lambda e, c=c, t0=t0: e.tensor_add(resid_ap(c, t0), resid_ap(c, t0), xln_ap(c)))
                for c in range(DC)]
            ret_sem = "sV"
        else:
            for c in range(DC):
                B.em("vector", lambda e, c=c: e.tensor_sub(tmpc, src_c(c), psb0[:, :]))
                if c == DC - 1:
                    m_vmul = B.inc("vector", lambda e, c=c: e.tensor_mul(
                        x2S_ap(c), tmpc, psb1[:, :]), "sV")
                else:
                    B.em("vector", lambda e, c=c: e.tensor_mul(x2S_ap(c), tmpc, psb1[:, :]))
            B.wait("gpsimd", "sV", m_vmul)
            for c in range(DC - 1):
                B.em("gpsimd", lambda e, c=c: e.tensor_scalar(
                    xn2_ap(c), x2S_ap(c), C["gam"][:, c:c + 1], C["bet"][:, c:c + 1],
                    ALU.mult, ALU.add))
            m_apply = B.inc("gpsimd", lambda e: e.tensor_scalar(
                xn2_ap(DC - 1), x2S_ap(DC - 1), C["gam"][:, DC - 1:DC], C["bet"][:, DC - 1:DC],
                ALU.mult, ALU.add), "sP")
            ret_sem = "sP"
        state["psb_free"] = (ret_sem, m_apply) if ret_sem == "sV" else ("sV", m_vmul)
        return (ret_sem, m_apply, deferred)

    def _slot_overlaps(slot_key):
        lo, hi = SLOT_RANGE[slot_key]
        return [k for k, (l2, h2) in SLOT_RANGE.items() if l2 < hi and lo < h2]

    def load_group(wname, oc0, noc, KC, slot_key):
        dt = wdt_of(wname)
        n_el = noc * KC * 128
        slot_ap = byte_slot(slot_key, n_el * (1 if dt == FP8 else 2), dt)
        waits = {}
        for k in _slot_overlaps(slot_key):
            sem, m = state["slot_free"].get(k, ("sT", 0))
            waits[sem] = max(waits.get(sem, 0), m)
        for sem, m in waits.items():
            B.wait("sync", sem, m)
        B.wait("sync", "sF", conv_m[wname][(oc0 + noc - 1) // conv_step[wname]])
        src = wbf[wname][oc0:oc0 + noc].rearrange("n p k o -> p n k o")
        return B.inc("sync", lambda e, o=slot_ap, i=src: e.dma_start(
            out=o, in_=i), "sS", 16), slot_ap

    def stage(wname, KC, OC, group, nslots, rhs_fn, evac_fn, rhs_ready, inject=None):
        for gi, oc0 in enumerate(range(0, OC, group)):
            if inject is not None:
                inject(gi)
            slot_key = ("k4", gi % nslots)
            m_load, pan = load_group(wname, oc0, group, KC, slot_key)
            B.wait("tensor", "sS", m_load)
            B.wait("tensor", rhs_ready[0], rhs_ready[1])
            for j in range(group):
                oc = oc0 + j
                bank = oc % 4
                bsem, bm = state["bank_free"][bank]
                B.wait("tensor", bsem, bm)
                for kc in range(KC - 1):
                    off = (j * KC + kc) * 128
                    B.em("tensor", lambda e, p=pan, o=off, kc=kc, b=bank: e.matmul(
                        ps[b][:, :], p[:, o:o + 128], rhs_fn(kc),
                        start=(kc == 0), stop=False))
                off = (j * KC + KC - 1) * 128
                m_pe = B.inc("tensor", lambda e, p=pan, o=off, b=bank: e.matmul(
                    ps[b][:, :], p[:, o:o + 128], rhs_fn(KC - 1),
                    start=(KC == 1), stop=True), "sT")
                if j == group - 1:
                    state["slot_free"][slot_key] = ("sT", m_pe)
                state["bank_free"][bank] = evac_fn(oc, ps[bank][:, :], m_pe)

    def emit_w12(tile, m_ln, xn_fn=None):
        xnf = xn_fn if xn_fn is not None else xn_ap
        for gi, oc0 in enumerate(range(0, HC, 2)):
            ka, kb = ("k2", (2 * gi) % 8), ("k2", (2 * gi + 1) % 8)
            m_l1, pan_a = load_group("w1", oc0, 2, DC, ka)
            m_l2, pan_b = load_group("w2", oc0, 2, DC, kb)
            B.wait("tensor", "sS", m_l2)
            B.wait("tensor", m_ln[0], m_ln[1])
            for j in range(2):
                oc = oc0 + j
                bA = (oc % 2) * 2
                bB = bA + 1
                for b in (bA, bB):
                    bsem, bm = state["bank_free"][b]
                    B.wait("tensor", bsem, bm)
                for kc in range(DC - 1):
                    off = (j * DC + kc) * 128
                    B.em("tensor", lambda e, p=pan_a, o=off, kc=kc, b=bA: e.matmul(
                        ps[b][:, :], p[:, o:o + 128], xnf(kc),
                        start=(kc == 0), stop=False))
                off = (j * DC + DC - 1) * 128
                m_pe1 = B.inc("tensor", lambda e, p=pan_a, o=off, b=bA: e.matmul(
                    ps[b][:, :], p[:, o:o + 128], xnf(DC - 1),
                    start=False, stop=True), "sT")
                for kc in range(DC - 1):
                    off = (j * DC + kc) * 128
                    B.em("tensor", lambda e, p=pan_b, o=off, kc=kc, b=bB: e.matmul(
                        ps[b][:, :], p[:, o:o + 128], xnf(kc),
                        start=(kc == 0), stop=False))
                off = (j * DC + DC - 1) * 128
                m_pe2 = B.inc("tensor", lambda e, p=pan_b, o=off, b=bB: e.matmul(
                    ps[b][:, :], p[:, o:o + 128], xnf(DC - 1),
                    start=False, stop=True), "sT")
                if j == 1:
                    state["slot_free"][ka] = ("sT", m_pe1)
                    state["slot_free"][kb] = ("sT", m_pe2)
                B.wait("scalar", "sT", m_pe1)
                B.wait("scalar", *state["tmps_free"][oc % 2])
                m_silu = B.inc("scalar", lambda e, b=bA, oc=oc: e.activation(
                    tmps[oc % 2], ps[b][:, :], AF.Silu, scale=C["s1"][:, oc:oc + 1]), "sA")
                state["bank_free"][bA] = ("sA", m_silu)
                B.wait("vector", "sT", m_pe2)
                B.wait("vector", "sA", m_silu)
                m_h = B.inc("vector", lambda e, b=bB, oc=oc: e.scalar_tensor_tensor(
                    h_ap(oc), ps[b][:, :], C["s2"][:, oc:oc + 1], tmps[oc % 2],
                    ALU.mult, ALU.mult), "sV")
                state["bank_free"][bB] = ("sV", m_h)
                state["tmps_free"][oc % 2] = ("sV", m_h)
        return sV.n

    def ffn_tile(tile, m_ln, xn_fn=None, g_fn=None, pend_mk=None, pend_at="h1",
                 apply_now=True):
        g_fn = g_fn if g_fn is not None else g_ap
        t0 = tile * T
        B.wait("tensor", *state["harena_free"])
        m_h = emit_w12(tile, m_ln, xn_fn=xn_fn)

        def ev_gelu_g(oc, p, m_pe):
            B.wait("scalar", "sT", m_pe)
            m = B.inc("scalar", lambda e, p=p, oc=oc: e.activation(
                g_fn(oc), p, AF.Gelu, scale=C["sh1"][:, oc:oc + 1]), "sA")
            return ("sA", m)

        stage("h1", HC, HC, 1, 4, h_ap, ev_gelu_g, ("sV", m_h))
        m_h1 = sA.n
        state["harena_free"] = ("sT", sT.n)
        apply_fn = pend_mk() if (pend_mk and pend_at == "h1") else None

        def ev_gelu_h(oc, p, m_pe):
            B.wait("scalar", "sT", m_pe)
            m = B.inc("scalar", lambda e, p=p, oc=oc: e.activation(
                h_ap(oc), p, AF.Gelu, scale=C["sh2"][:, oc:oc + 1]), "sA")
            return ("sA", m)

        stage("h2", HC, HC, 1, 4, g_fn, ev_gelu_h, ("sA", m_h1))
        m_h2 = sA.n
        state["garena_free"] = ("sT", sT.n)
        if pend_mk and pend_at == "h2":
            apply_fn = pend_mk()
        res = apply_fn() if (apply_fn is not None and apply_now) else apply_fn

        def ev_w3(oc, p, m_pe):
            B.wait("vector", "sT", m_pe)
            m = B.inc("vector", lambda e, p=p, oc=oc, t0=t0: e.scalar_tensor_tensor(
                resid_ap(oc, t0), p, C["s3h"][:, oc:oc + 1], resid_ap(oc, t0),
                ALU.mult, ALU.add), "sV")
            return ("sV", m)

        stage("w3", HC, DC, 1, 4, h_ap, ev_w3, ("sA", m_h2))
        state["harena_free"] = ("sT", sT.n)
        return res

    def emit_ffn(ln0_m=None, after_tile=None):
        m_lns = {0: ln0_m if ln0_m is not None else emit_ln(0, "ffn")[:2]}
        B.wait("vector", "sD", m_x)
        for tile in range(NT):
            pm = (lambda t=tile: emit_ln(t + 1, "ffn", defer_apply=True)) \
                if tile + 1 < NT else None
            r = ffn_tile(tile, m_lns[tile], pend_mk=pm)
            if r is not None:
                m_lns[tile + 1] = r
            if after_tile is not None:
                after_tile(tile)
        state["resid_ready"] = ("sV", sV.n)

    ssmA = {"pend": None, "red": {}, "var": {}, "sq": {}}

    def ssm_partA_copy(tile):
        t0 = tile * T
        src3 = resid3[:, :, t0:t0 + T]
        B.wait("vector", "sT", ssmA["red"].get(tile - 1, 0))
        m_xb = B.inc("vector", lambda e, s3=src3: e.tensor_copy(xb3_ssm, s3), "sV")
        m_x2 = B.inc("vector", lambda e: e.tensor_mul(x23_ssm, xb3_ssm, xb3_ssm), "sV")
        ssmA["pend"] = (tile, m_xb, m_x2)

    def ssm_partA_reduce():
        if ssmA["pend"] is None:
            return
        tile, m_xb, m_x2 = ssmA["pend"]
        ssmA["pend"] = None
        R = ST_ROW[tile]
        B.wait("tensor", "sV", max(m_xb, state["psr_free"][1]))
        for c in range(DC):
            B.em("tensor", lambda e, c=c, R=R: e.matmul(
                psr0[R:R + 1, :], C["onesb"][:, :], xbS_ap(c), start=(c == 0), stop=(c == DC - 1)))
        B.wait("tensor", "sV", m_x2)
        for c in range(DC - 1):
            B.em("tensor", lambda e, c=c, R=R: e.matmul(
                psr1[R:R + 1, :], C["onesb"][:, :], x2S_ap(c), start=(c == 0), stop=False))
        m_red = B.inc("tensor", lambda e, R=R: e.matmul(
            psr1[R:R + 1, :], C["onesb"][:, :], x2S_ap(DC - 1), start=False, stop=True), "sT")
        ssmA["red"][tile] = m_red
        B.wait("vector", "sT", m_red)
        B.em("vector", lambda e, t=tile, R=R: e.tensor_scalar_mul(
            st_row(t, 0), psr0[R:R + 1, :], 1.0 / DM))
        B.em("vector", lambda e, t=tile: e.tensor_mul(st_row(t, 1), st_row(t, 0), st_row(t, 0)))
        m_var = B.inc("vector", lambda e, t=tile, R=R: e.scalar_tensor_tensor(
            st_row(t, 2), psr1[R:R + 1, :], 1.0 / DM, st_row(t, 1), ALU.mult, ALU.subtract), "sV")
        ssmA["var"][tile] = m_var
        state["psr_free"] = ("sV", m_var)

    def ssm_partA(tile):
        ssm_partA_reduce()
        ssm_partA_copy(tile)

    def ssm_sqrts(ts):
        B.wait("scalar", "sV", ssmA["var"][ts[-1]])
        for t in ts:
            R = ST_ROW[t]
            B.em("scalar", lambda e, t=t, R=R: e.activation(
                st_row(t, 3), st_row(t, 2), AF.Sqrt, bias=C["eps"][R:R + 1, :]))
            m = B.inc("scalar", lambda e, t=t, R=R: e.activation(
                st_row(t, 4), st_row(t, 2), AF.Sqrt, bias=C["eps2"][R:R + 1, :], scale=1.0 + EPS), "sA")
            ssmA["sq"][t] = m

    def emit_ffn2(m_ln0, finish_stage4):
        pm1 = lambda: emit_ln(1, "ffn", defer_apply=True,
                              scratch=(xbL1_ap, x2L1_ap, xb3L1), xn_fn=xbL1_ap,
                              copy_wait=True)
        apply1 = ffn_tile(0, m_ln0, xn_fn=xn0_ap, g_fn=g0_ap,
                          pend_mk=pm1, pend_at="h2", apply_now=False)
        m_lns = {1: apply1()}
        finish_stage4()
        for tile in range(1, NT):
            pm = (lambda t=tile: emit_ln(t + 1, "ffn", defer_apply=True)) \
                if tile + 1 < NT else None
            r = ffn_tile(tile, m_lns[tile], pend_mk=pm,
                         xn_fn=xbL1_ap if tile == 1 else None)
            if r is not None:
                m_lns[tile + 1] = r
        state["resid_ready"] = ("sV", sV.n)

    def emit_ssm(m_acol):
        m_inner_l = {}
        B.wait("gpsimd", "sD", m_consts)

        def ssm_bc(tile, m_prev_apply):
            R = ST_ROW[tile]
            B.wait("tensor", "sA", ssmA["sq"][tile])
            B.wait("tensor", "sV", state["psb_free"][1])
            B.wait("tensor", "sV", state["psr_free"][1])
            B.em("tensor", lambda e, t=tile, R=R: e.matmul(
                psb0[:, :], C["onesr"][R:R + 1, :], st_row(t, 0), start=True, stop=True))
            B.em("tensor", lambda e, t=tile, R=R: e.matmul(
                psb1[:, :], C["onesr"][R:R + 1, :], st_row(t, 3), start=True, stop=True))
            m_bcT = B.inc("tensor", lambda e, t=tile, R=R: e.matmul(
                psr0[:, :], C["onesr"][R:R + 1, :], st_row(t, 4), start=True, stop=True), "sT")
            B.wait("vector", "sP", m_prev_apply)
            B.wait("vector", "sT", m_bcT)
            B.em("vector", lambda e: e.reciprocal(tmpc, psr0[:, :]))
            B.em("vector", lambda e: e.tensor_mul(tmpc, tmpc, psb1[:, :]))
            m_fix = B.inc("vector", lambda e: e.reciprocal(psb1[:, :], psb1[:, :]), "sV")
            state["psr_free"] = ("sV", m_fix)

        def ssm_apply(tile):
            t0 = tile * T
            src_c = lambda c: resid_ap(c, t0)
            par = tile % 2
            m_apply = 0
            for q in range(4):
                cs = [q * 2, q * 2 + 1]
                for c in cs:
                    B.em("vector", lambda e, c=c: e.tensor_sub(xln_ap(c), src_c(c), psb0[:, :]))
                    if c == cs[-1]:
                        m_vmul = B.inc("vector", lambda e, c=c: e.tensor_mul(
                            xln_ap(c), xln_ap(c), psb1[:, :]), "sV")
                    else:
                        B.em("vector", lambda e, c=c: e.tensor_mul(xln_ap(c), xln_ap(c), psb1[:, :]))
                B.wait("gpsimd", "sV", m_vmul)
                for c in cs:
                    B.em("gpsimd", lambda e, c=c: e.tensor_mul(x2S_ap(c), xln_ap(c), tmpc))
                    if c == cs[-1]:
                        m_apply = B.inc("gpsimd", lambda e, c=c, par=par: e.tensor_scalar(
                            xn2p_ap(par, c), x2S_ap(c), C["gam"][:, c:c + 1], C["bet"][:, c:c + 1],
                            ALU.mult, ALU.add), "sP")
                    else:
                        B.em("gpsimd", lambda e, c=c, par=par: e.tensor_scalar(
                            xn2p_ap(par, c), x2S_ap(c), C["gam"][:, c:c + 1], C["bet"][:, c:c + 1],
                            ALU.mult, ALU.add))
                for c in cs:
                    B.em("vector", lambda e, c=c, t0=t0: e.tensor_add(
                        resid_ap(c, t0), resid_ap(c, t0), xln_ap(c)))
            state["psb_free"] = ("sV", m_vmul)
            return ("sP", m_apply, None)

        if not ssmA["var"]:
            for t in range(NT - 1):
                ssm_partA(t)
            ssm_partA_reduce()
            ssm_partA_copy(NT - 1)
        ssm_sqrts([0, 1, 2])
        if dbg_small is not None:
            B.wait("gpsimd", "sA", ssmA["sq"][2])
            gdma(dbg_st, garena_f[0:65, 12288:14848])
        B.wait("vector", *state["harena_free"])
        B.wait("vector", *state["garena_free"])
        ssm_bc(0, 0)
        ssm_partA_reduce()
        ssm_sqrts([3])
        m_inner_l[0] = ssm_apply(0)
        for tile in range(NT):
            t0 = tile * T
            m_inner = m_inner_l[tile]

            def inject(gi, t=tile + 1, prev=m_inner):
                if t < NT:
                    if gi == 0:
                        ssm_bc(t, prev[1])
                    elif gi == 2:
                        m_inner_l[t] = ssm_apply(t)

            def ev_xz(oc, p, m_pe, tile=tile):
                B.wait("scalar", "sT", m_pe)
                if oc < 8:
                    m = B.inc("scalar", lambda e, p=p, oc=oc: e.activation(
                        xp_ap(oc), p, AF.Silu, bias=C["binx"][:, oc:oc + 1]), "sA")
                else:
                    m = B.inc("scalar", lambda e, p=p, oc=oc, tile=tile: e.activation(
                        z_ap(tile, oc - 8), p, AF.Sigmoid, bias=C["binz"][:, oc - 8:oc - 7]), "sA")
                return ("sA", m)

            stage("win", DC, 16, 2, 2,
                  lambda c, par=tile % 2: xn2p_ap(par, c), ev_xz,
                  (m_inner[0], m_inner[1]), inject=inject)
            m_xp = sA.n
            bank = 0
            bsem, bm = state["bank_free"][bank]
            B.wait("tensor", bsem, bm)
            B.wait("tensor", "sA", m_xp)
            for kc in range(DC - 1):
                B.em("tensor", lambda e, kc=kc, b=bank: e.matmul(
                    ps[b][0:64, :], C["wxpb"][:, kc * 64:(kc + 1) * 64], xp_ap(kc),
                    start=(kc == 0), stop=False))
            m_upe = B.inc("tensor", lambda e, b=bank: e.matmul(
                ps[b][0:64, :], C["wxpb"][:, (DC - 1) * 64:DC * 64], xp_ap(DC - 1),
                start=False, stop=True), "sT")
            B.wait("vector", "sT", m_upe)
            m_u = B.inc("vector", lambda e, t0=t0, b=bank: e.tensor_scalar_add(
                u_ap(t0, T), ps[b][0:64, :], C["bxp"][:, :]), "sV")
            state["bank_free"][bank] = ("sV", m_u)

        B.wait("vector", "sA", m_acol)
        m_scan1 = B.inc("vector", lambda e: e.tensor_tensor_scan(
            hs_ap(0, TOK), C["acol"].broadcast_to([64, TOK])[:, :], u_ap(0, TOK),
            0.0, ALU.mult, ALU.add), "sV")
        B.wait("gpsimd", "sV", m_scan1)
        m_f = gdma(ag_in, hs_ap(TOK - 1, 1))
        B.wait("gpsimd", "sG", m_f)
        m_cc = B.inc("gpsimd", lambda e: e.collective_compute(
            "AllGather", ALU.bypass,
            replica_groups=[[0, 1, 2, 3, 4, 5, 6, 7]],
            ins=[ag_in.opt()], outs=[ag_out.opt()]), "sC", 1)
        B.wait("gpsimd", "sC", m_cc)
        m_fsb = gdma(C["fsb8"][:, :], ag_out.rearrange("(r p) one -> r (p one)", p=64))
        m_fsbF = gfence()
        B.wait("tensor", "sF", m_fsbF)
        B.wait("tensor", state["psr_free"][0], state["psr_free"][1])
        m_ftr = B.inc("tensor", lambda e: e.matmul(
            psr1[0:64, 0:8], C["fsb8"][:, :], C["eye8"][:, :],
            start=True, stop=True), "sT")
        B.wait("vector", "sT", m_ftr)
        B.wait("vector", "sA", m_wqe)
        m_wf = B.inc("vector", lambda e: e.tensor_mul(
            C["wq"][:, :], C["wqe"][:, :], psr1[0:64, 0:8]), "sV")
        B.wait("vector", "sV", m_wf)
        m_c = B.inc("vector", lambda e: e.reduce_sum(
            C["ccol"][:, :], C["wq"][:, :], axis=mybir.AxisListType.X), "sV")
        B.wait("vector", "sV", m_c)
        m_scan2 = B.inc("vector", lambda e: e.tensor_tensor_scan(
            hs_ap(0, TOK), C["acol"].broadcast_to([64, TOK])[:, :], u_ap(0, TOK),
            C["ccol"][:, :], ALU.mult, ALU.add), "sV")
        if dbg_small is not None:
            B.wait("gpsimd", "sV", m_scan2)
            gdma(dbg_u, u_ap(0, TOK))
            gdma(dbg_h, hs_ap(0, TOK))
            gdma(dbg_small[:, 0:1], C["ccol"][:, :], allow_slow_non_contiguous=True)
            gdma(dbg_small[:, 1:5], C["fsb"][:, 0:4], allow_slow_non_contiguous=True)
            gdma(dbg_small[:, 5:9], C["wq"][:, 0:4], allow_slow_non_contiguous=True)
            gdma(dbg_small[:, 9:10], C["acol"][:, :], allow_slow_non_contiguous=True)
            gdma(dbg_small[:, 10:11], C["texp"][:, :], allow_slow_non_contiguous=True)

        B.wait("vector", "sA", B.sems["sA"].n)
        m_cc_ag = []
        for ti, tile in enumerate((0, 2, 1, 3)):
            t0 = tile * T
            m_yd = 0
            for oc in range(DC):
                bank = (ti * DC + oc) % 4
                bsem, bm = state["bank_free"][bank]
                B.wait("tensor", bsem, bm)
                B.wait("tensor", "sV", m_scan2)
                m_pe = B.inc("tensor", lambda e, oc=oc, t0=t0, b=bank: e.matmul(
                    ps[b][:, :], C["dmat"][:, oc * 128:(oc + 1) * 128], hs_ap(t0, T),
                    start=True, stop=True), "sT")
                B.wait("vector", "sT", m_pe)
                m_yd = B.inc("vector", lambda e, tile=tile, oc=oc, b=bank: e.tensor_mul(
                    yd_ap(tile, oc), ps[b][:, :], z_ap(tile, oc)), "sV")
                state["bank_free"][bank] = ("sV", m_yd)
            B.wait("gpsimd", "sV", m_yd)
            m_agi = gdma(
                ag_ydi[tile % 2].rearrange("p (o w t) -> p o w t", o=8, w=2)[:, :, tile // 2, :],
                harena.reshape([128, 12, 2048])[:, 2:10, tile * T:(tile + 1) * T])
            if ti % 2 == 1:
                B.wait("gpsimd", "sG", m_agi)
                if m_cc_ag:
                    B.wait("gpsimd", "sC", m_cc_ag[-1])
                m_cc_ag.append(B.inc("gpsimd", lambda e, ab=tile % 2: e.collective_compute(
                    "AllGather", ALU.bypass,
                    replica_groups=[[0, 1, 2, 3, 4, 5, 6, 7]],
                    ins=[ag_ydi[ab].opt()], outs=[ag_ydo[ab].opt()]), "sC", 1))
        m_yd_all = sV.n

        sGs = B.sems["sG"]
        m_fs = []
        for ab in range(2):
            B.wait("gpsimd", "sC", m_cc_ag[ab])
            agv = ag_ydo[ab].rearrange("(r p) (o w t) -> r p o w t", p=128, o=8, w=2)

            def em_fetch(e, agv=agv, ab=ab):
                for arm in nc.switch_core_id(engines=[e]):
                    cid = arm.logical % 8
                    q0, q1, tb = cid // 2, 4 + cid // 2, cid % 2
                    for par, q in ((0, q0), (1, q1)):
                        if ab == 0:
                            dst = yd2r[:, 2:10, par, 0:512]
                        else:
                            dst = garBv[:, :, par, :]
                        e.dma_start(out=dst, in_=agv[q, :, :, tb, :]).then_inc(sGs.h, 16)

            B.em("gpsimd", em_fetch)
            sGs.n += 32
            m_fs.append(gfence())
        m_f1 = m_fs[-1]

        m_wo = []
        for gidx in range(2):
            B.wait("sync", "sV", m_yd_all)
            B.wait("sync", "sF", conv_m["wout"][gidx])
            src = wbf["wout"][gidx * 4:(gidx + 1) * 4].rearrange("n p k o -> p n k o")
            dst = garena[:, 12288 + gidx * 4096: 12288 + (gidx + 1) * 4096]
            m_wo.append(B.inc("sync", lambda e, o=dst, i=src: e.dma_start(
                out=o, in_=i), "sS", 16))

        def wout_sl(oc, kc):
            g, j = oc // 4, oc % 4
            base = 12288 + g * 4096 + (j * DC + kc) * 128
            return garena[:, base: base + 128]

        def stage4_tile(tile):
            for oc in range(DC):
                bank = (tile * DC + oc) % 4
                bsem, bm = state["bank_free"][bank]
                B.wait("tensor", bsem, bm)
                B.wait("tensor", "sS", m_wo[oc // 4])
                B.wait("tensor", "sF", m_fs[tile // 2])
                for kc in range(DC - 1):
                    B.em("tensor", lambda e, oc=oc, kc=kc, tile=tile, b=bank: e.matmul(
                        ps[b][:, :], wout_sl(oc, kc), yd2_rhs(kc, tile),
                        start=(kc == 0), stop=False))
                m_pe = B.inc("tensor", lambda e, oc=oc, tile=tile, b=bank: e.matmul(
                    ps[b][:, :], wout_sl(oc, DC - 1), yd2_rhs(DC - 1, tile),
                    start=False, stop=True), "sT")
                B.wait("vector", "sT", m_pe)
                m_y = B.inc("vector", lambda e, oc=oc, tile=tile, b=bank: e.scalar_tensor_tensor(
                    resid_int(oc, tile), ps[b][:, :], C["bout"][:, oc:oc + 1], resid_int(oc, tile),
                    ALU.add, ALU.add), "sV")
                state["bank_free"][bank] = ("sV", m_y)

        special = "ffn2" in phases and tap_after is None
        m_ln0_next = None
        B.wait("vector", "sP", state.get("m_adds", 0))
        stage4_tile(0)
        if special:
            m_ln0_next = emit_ln(0, "ffn", scratch=(xb0_ap, x2_ap, xb3_0),
                                 xn_fn=xn0_ap)[:2]
        stage4_tile(1)
        state["harena_free"] = ("sT", sT.n)
        for k in [("k4", 2), ("k4", 3), ("k2", 4), ("k2", 5), ("k2", 6), ("k2", 7)]:
            state["slot_free"][k] = ("sV", m_scan2)

        def finish_stage4():
            stage4_tile(2)
            stage4_tile(3)
            state["garena_free"] = ("sT", sT.n)
            state["resid_ready"] = ("sV", sV.n)

        if not special:
            state["garena_free"] = ("sT", sT.n)
            finish_stage4()
            return None, None
        return m_ln0_next, finish_stage4

    def maybe_tap(ph):
        if tap_after == ph:
            B.wait("gpsimd", *state["resid_ready"])
            gdma(dbg_tap.rearrange("(c p) t -> p c t", p=128), resid3[:, :, :])

    m_ln0_ffn2, fs4 = None, None
    if "ffn1" in phases:
        emit_ffn(after_tile=ssm_partA if "ssm" in phases else None)
        maybe_tap("ffn1")
    if "ssm" in phases:
        m_ln0_ffn2, fs4 = emit_ssm(m_acol)
        maybe_tap("ssm")
    if "ffn2" in phases:
        if fs4 is not None:
            emit_ffn2(m_ln0_ffn2, fs4)
        else:
            emit_ffn()

    B.wait("gpsimd", *state["resid_ready"])
    m_out = gdma(yout.rearrange("(c p) t -> p c t", p=128), resid3[:, :, :])
    B.wait("gpsimd", "sG", m_out)

    with nc.Block() as block:
        @block.sync
        def _(e):
            B.run(e, "sync")

        @block.scalar
        def _(e):
            B.run(e, "scalar")

        @block.vector
        def _(e):
            B.run(e, "vector")

        @block.tensor
        def _(e):
            B.run(e, "tensor")

        @block.gpsimd
        def _(e):
            B.run(e, "gpsimd")

    stack.close()
    return nc


def _mk_in_maps(inputs):
    x = np.asarray(inputs["L0_attn_x"], dtype=np.float32)
    b_in = np.asarray(inputs["L0_attn_b_in"], dtype=np.float32)
    base = dict(
        d_s1=_coltile(inputs["L0_ffn_w1_s"], 32),
        d_s2=_coltile(inputs["L0_ffn_w2_s"], 32),
        d_sh1=_coltile(inputs["L0_ffn_h1_s"], 32),
        d_sh2=_coltile(inputs["L0_ffn_h2_s"], 32),
        d_s3=_coltile(inputs["L0_ffn_w3_s"], 8),
        d_bout=_coltile(inputs["L0_attn_b_out"], 8),
        d_gam=_coltile(inputs["L0_attn_gamma"], 8),
        d_bet=_coltile(inputs["L0_attn_beta"], 8),
        d_binx=_coltile(b_in[:1024], 8),
        d_binz=_coltile(b_in[1024:], 8),
        d_alog=np.ascontiguousarray(np.asarray(inputs["L0_attn_A_log"], dtype=np.float32).reshape(64, 1)),
        d_bxp=np.ascontiguousarray(np.asarray(inputs["L0_attn_b_xp"], dtype=np.float32).reshape(64, 1)),
        d_wxp=np.ascontiguousarray(
            np.asarray(inputs["L0_attn_W_xp"], dtype=np.float32).T.reshape(8, 128, 64)
            .transpose(1, 0, 2).reshape(128, 512)),
        d_dmat=np.ascontiguousarray(np.asarray(inputs["L0_attn_Dmat"], dtype=np.float32)),
        d_eye8=np.eye(8, dtype=np.float32),
        wf_w1=_panels(inputs["L0_ffn_w1_w"]),
        wf_w2=_panels(inputs["L0_ffn_w2_w"]),
        wf_h1=_panels(inputs["L0_ffn_h1_w"]),
        wf_h2=_panels(inputs["L0_ffn_h2_w"]),
        wf_w3=_panels(inputs["L0_ffn_w3_w"]),
        wf_win=_panels(inputs["L0_attn_W_in"]),
        wf_wout=_panels(inputs["L0_attn_W_out"]),
    )
    in_maps = []
    for c in range(N_CORES):
        b, r = c // 4, c % 4
        m = dict(base)
        m["xin"] = np.ascontiguousarray(x[b, r * TOK:(r + 1) * TOK, :].T)
        kv = np.zeros((64, 8), dtype=np.float32)
        for q in range(8):
            qq = q - 4 * b
            kv[:, q] = 2048.0 * (r - 1 - qq) if (0 <= qq < r) else 1e9
        m["d_kvec"] = kv
        in_maps.append(m)
    return in_maps


def kernel(**inputs):
    in_maps = _mk_in_maps(inputs)
    nc = build_nc()
    res = run_bass_kernel_spmd(nc, in_maps, core_ids=list(range(N_CORES)))
    out = np.empty((2, 8192, 1024), dtype=np.float32)
    for c in range(N_CORES):
        b, r = c // 4, c % 4
        out[b, r * TOK:(r + 1) * TOK, :] = np.asarray(res.results[c]["yout"]).T
    return out


# version 34
# speedup vs baseline: 1.0477x; 1.0206x over previous
import contextlib
import numpy as np
import concourse.bass as bass
import concourse.mybir as mybir
from concourse.bass_utils import run_bass_kernel_spmd

F32 = mybir.dt.float32
BF16 = mybir.dt.bfloat16
AF = mybir.ActivationFunctionType
ALU = mybir.AluOpType

N_CORES = 8
TOK = 2048
T = 512
NT = TOK // T
DM = 1024
DC = DM // 128
DH = 4096
HC = DH // 128
EPS = 1e-5

WSPECS = {
    "w1": (1024, 4096), "w2": (1024, 4096),
    "h1": (4096, 4096), "h2": (4096, 4096),
    "w3": (4096, 1024),
    "win": (1024, 2048), "wout": (1024, 1024),
}
FP8 = mybir.dt.float8e4
TERNARY = {"w1", "w2", "h1", "h2", "w3"}


def _panels(w):
    O, K = w.shape
    KC, OC = K // 128, O // 128
    wt = np.ascontiguousarray(np.asarray(w, dtype=np.float32).T)
    return np.ascontiguousarray(wt.reshape(KC, 128, OC, 128).transpose(2, 1, 0, 3))


def _coltile(v, noc):
    v = np.asarray(v, dtype=np.float32)
    return np.ascontiguousarray(v.reshape(noc, 128).T)


class _Sem:
    def __init__(self, h):
        self.h = h
        self.n = 0


class Build:
    ENGINES = ("sync", "scalar", "vector", "tensor", "gpsimd")

    def __init__(self, nc):
        self.nc = nc
        self.prog = {e: [] for e in self.ENGINES}
        self.sems = {}

    def sem(self, name, handle):
        self.sems[name] = _Sem(handle)
        return self.sems[name]

    def em(self, eng, fn):
        self.prog[eng].append(fn)

    def wait(self, eng, sem, m):
        if not m:
            return
        s = self.sems[sem]
        self.em(eng, lambda e, s=s, m=m: e.wait_ge(s.h, m))

    def inc(self, eng, build_fn, sem, k=1):
        s = self.sems[sem]
        s.n += k
        m = s.n
        self.em(eng, lambda e, f=build_fn, s=s, k=k: f(e).then_inc(s.h, k))
        return m

    def run(self, eng_obj, eng_name):
        for f in self.prog[eng_name]:
            f(eng_obj)


def build_nc(phases=("ffn1", "ssm", "ffn2"), tap_after=None):
    nc = bass.Bass("TRN2", target_bir_lowering=False, debug=False)
    dbg_tap = None
    dbg_small = None
    if tap_after is not None:
        dbg_tap = nc.dram_tensor("dbg_tap", [DM, TOK], F32, kind="ExternalOutput").ap()
        dbg_small = nc.dram_tensor("dbg_small", [64, 11], F32, kind="ExternalOutput").ap()
        dbg_u = nc.dram_tensor("dbg_u", [64, TOK], F32, kind="ExternalOutput").ap()
        dbg_h = nc.dram_tensor("dbg_h", [64, TOK], F32, kind="ExternalOutput").ap()
        dbg_st = nc.dram_tensor("dbg_st", [65, 2560], F32, kind="ExternalOutput").ap()

    xin = nc.dram_tensor("xin", [DM, TOK], F32, kind="ExternalInput").ap()
    yout = nc.dram_tensor("yout", [DM, TOK], F32, kind="ExternalOutput").ap()
    wf32, wbf = {}, {}
    for w, (K, O) in WSPECS.items():
        KC, OC = K // 128, O // 128
        wf32[w] = nc.dram_tensor("wf_" + w, [OC, 128, KC, 128], F32, kind="ExternalInput").ap()
        wdt = FP8 if w in TERNARY else BF16
        wbf[w] = nc.dram_tensor("wb_" + w, [OC, 128, KC, 128], wdt, kind="Internal").ap()
    dins = {}
    for nm, shp in [("d_s1", [128, 32]), ("d_s2", [128, 32]), ("d_sh1", [128, 32]),
                    ("d_sh2", [128, 32]), ("d_s3", [128, 8]), ("d_bout", [128, 8]),
                    ("d_gam", [128, 8]), ("d_bet", [128, 8]), ("d_binx", [128, 8]),
                    ("d_binz", [128, 8]), ("d_alog", [64, 1]), ("d_bxp", [64, 1]),
                    ("d_kvec", [64, 8]), ("d_wxp", [128, 512]), ("d_dmat", [64, 1024]),
                    ("d_eye8", [8, 8])]:
        dins[nm] = nc.dram_tensor(nm, shp, F32, kind="ExternalInput").ap()
    ag_in = nc.dram_tensor("ag_in", [64, 1], F32, kind="Internal").ap()
    ag_out = nc.dram_tensor("ag_out", [512, 1], F32, kind="Internal", addr_space="Shared").ap()
    ag_ydi = [nc.dram_tensor(f"ag_ydi{i}", [128, 8192], BF16, kind="Internal").ap() for i in range(2)]
    ag_ydo = [nc.dram_tensor(f"ag_ydo{i}", [1024, 8192], BF16, kind="Internal", addr_space="Shared").ap() for i in range(2)]
    ag_w8i = nc.dram_tensor("ag_w8i", [64, 1], F32, kind="Internal").ap()
    ag_w8o = nc.dram_tensor("ag_w8o", [512, 1], F32, kind="Internal", addr_space="Shared").ap()

    resid = nc.alloc_sbuf_tensor("resid", [128, DC * TOK], F32)
    harena = nc.alloc_sbuf_tensor("harena", [128, 24576], BF16)
    garena = nc.alloc_sbuf_tensor("garena", [128, 32768], BF16)
    wpan = nc.alloc_sbuf_tensor("wpan", [128, 8192], BF16)
    harena_f = harena.bitcast(F32)
    garena_f = garena.bitcast(F32)
    wpan_f = wpan.bitcast(F32)
    wpan_f8 = wpan.bitcast(FP8)

    C = {}
    for nm, shp, dt in [("s1", [128, 32], F32), ("s2", [128, 32], F32),
                        ("sh1", [128, 32], F32), ("sh2", [128, 32], F32),
                        ("s3h", [128, 8], F32), ("bout", [128, 8], F32),
                        ("gam", [128, 8], F32), ("bet", [128, 8], F32),
                        ("binx", [128, 8], F32), ("binz", [128, 8], F32),
                        ("alog", [64, 1], F32), ("bxp", [64, 1], F32),
                        ("kvec", [64, 8], F32), ("texp", [64, 1], F32),
                        ("acol", [64, 1], F32), ("wq", [64, 8], F32),
                        ("fsb", [64, 8], F32), ("ccol", [64, 1], F32),
                        ("fsb8", [8, 64], F32), ("wqe", [64, 8], F32),
                        ("eye8", [8, 8], F32),
                        ("wxpf", [128, 512], F32), ("wxpb", [128, 512], BF16),
                        ("dmat", [64, 1024], F32), ("eps", [65, 1], F32), ("eps2", [65, 1], F32),
                        ("onesb", [128, 1], BF16), ("onesr", [65, 128], F32)]:
        C[nm] = nc.alloc_sbuf_tensor("c_" + nm, shp, dt)

    ps = [nc.alloc_psum_tensor(f"ps{i}", [128, 512], F32) for i in range(4)]
    psb0 = nc.alloc_psum_tensor("psb0", [128, 512], F32)
    psb1 = nc.alloc_psum_tensor("psb1", [128, 512], F32)
    psr0 = nc.alloc_psum_tensor("psr0", [128, 512], F32)
    psr1 = nc.alloc_psum_tensor("psr1", [128, 512], F32)

    resid3 = resid.reshape([128, DC, TOK])

    def resid_ap(c, t0, n=T):
        return resid[:, c * TOK + t0: c * TOK + t0 + n]

    def h_ap(oc):
        return harena[:, oc * T:(oc + 1) * T]

    def g_ap(oc):
        return garena[:, oc * T:(oc + 1) * T]

    xn_ap = g_ap

    def xb_ap(c):
        return garena[:, 4096 + c * T: 4096 + (c + 1) * T]

    def x2_ap(c):
        return garena[:, 8192 + c * T: 8192 + (c + 1) * T]

    xb3_ffn = garena.reshape([128, 64, T])[:, 8:16, :]

    def xbA_ap(c):
        return garena[:, 16384 + c * T: 16384 + (c + 1) * T]

    def x2A_ap(c):
        return garena[:, 20480 + c * T: 20480 + (c + 1) * T]

    xb3A = garena.reshape([128, 64, T])[:, 32:40, :]

    h48 = harena.reshape([128, 48, T])

    def xn0_ap(c):
        return harena[:, 20480 + c * T: 20480 + (c + 1) * T]

    def g0_ap(oc):
        if oc < 8:
            return garena[:, 8192 + oc * T: 8192 + (oc + 1) * T]
        if oc < 16:
            return garena[:, 20480 + (oc - 8) * T: 20480 + (oc - 7) * T]
        return harena[:, 16384 + (oc - 16) * T: 16384 + (oc - 15) * T]

    def xb0_ap(c):
        return harena[:, c * T:(c + 1) * T]

    xb3_0 = h48[:, 0:8, :]

    def xbL1_ap(c):
        return harena[:, 16384 + c * T: 16384 + (c + 1) * T]

    def x2L1_ap(c):
        return harena[:, 20480 + c * T: 20480 + (c + 1) * T]

    xb3L1 = h48[:, 32:40, :]

    garBv = garena[:, 0:8192].rearrange("p (k two t) -> p k two t", two=2, t=T)

    st_mu = garena_f[0:1, 12288:12800]
    st_msq = garena_f[0:1, 12800:13312]
    st_musq = garena_f[0:1, 13312:13824]
    st_std = garena_f[0:1, 13824:14336]
    st_rstd = garena_f[0:1, 14336:14848]
    tmpc = garena_f[:, 14848:15360]
    st_rstd2 = garena_f[0:1, 15360:15872]
    tmps = [garena[:, 30720:31232], garena[:, 31232:31744]]

    def xln_ap(c):
        return harena_f[:, c * T:(c + 1) * T]

    def xn2p_ap(par, c):
        base = 8192 if par == 0 else 16384
        return harena[:, base + c * T: base + (c + 1) * T]

    def xn2_ap(c):
        return harena[:, 8192 + c * T: 8192 + (c + 1) * T]

    def xp_ap(c):
        return harena[:, 12288 + c * T: 12288 + (c + 1) * T]

    def xbS_ap(c):
        return harena[:, 16384 + c * T: 16384 + (c + 1) * T]

    def x2S_ap(c):
        return harena[:, 20480 + c * T: 20480 + (c + 1) * T]

    xb3_ssm = harena.reshape([128, 48, T])[:, 32:40, :]
    x23_ssm = harena.reshape([128, 48, T])[:, 40:48, :]

    ST_ROW = [32, 64, 0, 32]

    def st_row(t, k):
        r = ST_ROW[t]
        base = 12288 + k * 512
        return garena_f[r:r + 1, base: base + 512]

    def z_ap(tile, c):
        base = (tile * 8 + c) * T
        return garena_f[:, base: base + T]

    def u_ap(t0, n):
        return wpan_f[0:64, 2048 + t0: 2048 + t0 + n]

    def hs_ap(t0, n):
        return harena_f[0:64, t0: t0 + n]

    def yd_ap(tile, kc):
        base = 4096 + kc * TOK + tile * T
        return harena[:, base: base + T]

    yd2r = harena.reshape([128, 12, 2, 1024])

    def yd2_rhs(kc, tile):
        if tile < 2:
            return yd2r[:, 2 + kc, :, tile * 256:(tile + 1) * 256]
        return garBv[:, kc, :, (tile - 2) * 256:(tile - 1) * 256]

    resid_scr = resid.rearrange("p (c j two) -> p c two j", c=DC, two=2)

    def resid_int(c, tile):
        return resid_scr[:, c, :, tile * 256:(tile + 1) * 256]

    SLOT_RANGE = {}
    for _s in range(8):
        SLOT_RANGE[("k2", _s)] = (_s * 2048, (_s + 1) * 2048)
    for _s in range(4):
        SLOT_RANGE[("k4", _s)] = (_s * 4096, (_s + 1) * 4096)

    def byte_slot(key, nbytes, dt):
        lo, hi = SLOT_RANGE[key]
        assert nbytes <= hi - lo
        if dt == FP8:
            return wpan_f8[:, lo: lo + nbytes]
        return wpan[:, lo // 2: (lo + nbytes) // 2]

    def wdt_of(w):
        return FP8 if w in TERNARY else BF16

    B = Build(nc)
    stack = contextlib.ExitStack()
    for snm in ("sS", "sG", "sT", "sA", "sV", "sC", "sD", "sP", "sF"):
        B.sem(snm, stack.enter_context(nc.semaphore(snm)))
    sT, sA, sV = B.sems["sT"], B.sems["sA"], B.sems["sV"]

    def gdma(out, in_, **kw):
        return B.inc("gpsimd", lambda e, o=out, i=in_, kw=kw: e.dma_start(out=o, in_=i, **kw), "sG", 16)

    def gfence(tot=None):
        if tot is None:
            tot = B.sems["sG"].n
        return B.inc("gpsimd", lambda e, tot=tot: e.wait_ge(B.sems["sG"].h, tot), "sF")

    xin3 = xin.rearrange("(c p) t -> p c t", p=128)
    m_x1 = B.inc("scalar", lambda e: e.dma_start(
        out=resid3[:, :, 0:T], in_=xin3[:, :, 0:T]), "sD", 16)
    m_x = B.inc("scalar", lambda e: e.dma_start(
        out=resid3[:, :, T:TOK], in_=xin3[:, :, T:TOK]), "sD", 16)
    for cn, dn in [("s1", "d_s1"), ("s2", "d_s2"), ("sh1", "d_sh1"), ("sh2", "d_sh2"),
                   ("bout", "d_bout"), ("gam", "d_gam"), ("bet", "d_bet"),
                   ("binx", "d_binx"), ("binz", "d_binz"), ("alog", "d_alog"),
                   ("bxp", "d_bxp"), ("kvec", "d_kvec"), ("wxpf", "d_wxp"),
                   ("dmat", "d_dmat"), ("eye8", "d_eye8")]:
        B.inc("scalar", lambda e, o=C[cn][:, :], i=dins[dn]: e.dma_start(out=o, in_=i), "sD", 16)
    m_consts = B.inc("scalar", lambda e: e.dma_start(out=C["s3h"][:, :], in_=dins["d_s3"]), "sD", 16)

    conv_gate = {"w1": None, "w2": None,
                 "h1": lambda g: 16 * (4 + 2 * g),
                 "h2": lambda g: 16 * (24 + 3 * g),
                 "w3": lambda g: 16 * 60,
                 "win": lambda g: 16 * 80,
                 "wout": lambda g: 16 * 90}
    conv_step = {w: 2 if w in ("w1", "w2") else 4 for w in WSPECS}
    conv_groups = []
    for oc0 in range(0, 32, 2):
        for w in ("w1", "w2"):
            conv_groups.append((w, oc0, 2, None))
    for w, (K, O) in WSPECS.items():
        if w in ("w1", "w2"):
            continue
        KC, OC = K // 128, O // 128
        step = min(4, OC)
        for gi, oc0 in enumerate(range(0, OC, step)):
            conv_groups.append((w, oc0, step, conv_gate[w](gi)))
    conv_m = {w: [] for w in WSPECS}
    pending = []
    for gidx, (w, oc0, step, gate) in enumerate(conv_groups):
        if gate is not None:
            B.wait("gpsimd", "sS", gate)
        gdma(wbf[w][oc0:oc0 + step], wf32[w][oc0:oc0 + step], max_dma_last_dim=8192)
        pending.append(w)
        if len(pending) >= (4 if gidx < 32 else 8) or gidx == len(conv_groups) - 1:
            m = gfence()
            for w2 in pending:
                conv_m[w2].append(m)
            pending = []

    mw1 = B.inc("gpsimd", lambda e: e.collective_compute(
        "AllGather", ALU.bypass,
        replica_groups=[[0, 1, 2, 3, 4, 5, 6, 7]],
        ins=[ag_w8i.opt()], outs=[ag_w8o.opt()]), "sC", 1)
    B.wait("gpsimd", "sC", mw1)

    B.wait("vector", "sD", m_consts)
    B.em("vector", lambda e: e.memset(C["eps"][:, :], EPS))
    B.em("vector", lambda e: e.memset(C["eps2"][:, :], EPS * EPS))
    B.em("vector", lambda e: e.memset(C["onesb"][:, :], 1.0))
    B.em("vector", lambda e: e.memset(C["onesr"][:, :], 1.0))
    B.em("vector", lambda e: e.tensor_scalar_mul(C["s3h"][:, :], C["s3h"][:, :], 0.5))
    m_setup = B.inc("vector", lambda e: e.tensor_copy(C["wxpb"][:, :], C["wxpf"][:, :]), "sV")
    B.wait("scalar", "sV", m_setup)
    m_texp = B.inc("scalar", lambda e: e.activation(C["texp"][:, :], C["alog"][:, :], AF.Exp), "sA")
    B.wait("vector", "sA", m_texp)
    m_negt = B.inc("vector", lambda e: e.tensor_scalar_mul(C["acol"][:, :], C["texp"][:, :], -1.0), "sV")
    B.wait("scalar", "sV", m_negt)
    m_acol = B.inc("scalar", lambda e: e.activation(C["acol"][:, :], C["acol"][:, :], AF.Exp), "sA")
    B.wait("vector", "sD", m_consts)
    m_wq0 = B.inc("vector", lambda e: e.tensor_scalar_mul(
        C["wqe"][:, :], C["kvec"][:, :], C["texp"][:, :]), "sV")
    B.wait("scalar", "sV", m_wq0)
    m_wqe = B.inc("scalar", lambda e: e.activation(
        C["wqe"][:, :], C["wqe"][:, :], AF.Exp, scale=-1.0), "sA")

    state = {
        "harena_free": ("sT", 0),
        "garena_free": ("sT", 0),
        "psb_free": ("sV", 0),
        "psr_free": ("sV", 0),
        "bank_free": {i: ("sV", 0) for i in range(4)},
        "slot_free": {},
        "tmps_free": {0: ("sV", 0), 1: ("sV", 0)},
        "resid_ready": ("sV", 0),
    }
    B.wait("vector", "sD", m_x1)

    def emit_ln(tile, kind, alt=True, defer_apply=False, scratch=None, xn_fn=None,
                copy_wait=False):
        t0 = tile * T
        xnf = xn_fn if xn_fn is not None else xn_ap
        if kind == "ffn":
            src_c = lambda c: resid_ap(c, t0)
            if scratch is not None:
                xbf, x2f, xb3 = scratch
            else:
                xbf, x2f, xb3 = (xbA_ap, x2A_ap, xb3A) if alt else (xb_ap, x2_ap, xb3_ffn)
        elif kind in ("outer", "ssm"):
            src_c = lambda c: resid_ap(c, t0)
            xbf, x2f, xb3 = xbS_ap, x2S_ap, xb3_ssm
        else:
            src_c = xln_ap
            xbf, x2f, xb3 = xbS_ap, x2S_ap, xb3_ssm

        if kind == "inner":
            for c in range(DC - 1):
                B.em("vector", lambda e, c=c: e.tensor_copy(xbf(c), src_c(c)))
            m_xb = B.inc("vector", lambda e: e.tensor_copy(xbf(DC - 1), src_c(DC - 1)), "sV")
        else:
            if copy_wait:
                B.wait("vector", *state["garena_free"])
            src3 = resid3[:, :, t0:t0 + T]
            m_xb = B.inc("vector", lambda e, s3=src3, d3=xb3: e.tensor_copy(d3, s3), "sV")
        B.wait("scalar", "sV", m_xb)
        for c in range(DC - 1):
            B.em("scalar", lambda e, c=c: e.activation(x2f(c), xbf(c), AF.Square))
        m_x2 = B.inc("scalar", lambda e: e.activation(x2f(DC - 1), xbf(DC - 1), AF.Square), "sA")
        B.wait("tensor", "sV", max(m_xb, state["psr_free"][1]))
        for c in range(DC):
            B.em("tensor", lambda e, c=c: e.matmul(
                psr0[0:1, :], C["onesb"][:, :], xbf(c), start=(c == 0), stop=(c == DC - 1)))
        B.wait("tensor", "sA", m_x2)
        for c in range(DC - 1):
            B.em("tensor", lambda e, c=c: e.matmul(
                psr1[0:1, :], C["onesb"][:, :], x2f(c), start=(c == 0), stop=False))
        m_red = B.inc("tensor", lambda e: e.matmul(
            psr1[0:1, :], C["onesb"][:, :], x2f(DC - 1), start=False, stop=True), "sT")
        B.wait("vector", "sT", m_red)
        B.em("vector", lambda e: e.tensor_scalar_mul(st_mu, psr0[0:1, :], 1.0 / DM))
        B.em("vector", lambda e: e.tensor_mul(st_musq, st_mu, st_mu))
        m_var = B.inc("vector", lambda e: e.scalar_tensor_tensor(
            st_msq, psr1[0:1, :], 1.0 / DM, st_musq, ALU.mult, ALU.subtract), "sV")
        state["psr_free"] = ("sV", m_var)
        B.wait("scalar", "sV", m_var)
        if kind in ("ssm", "inner"):
            B.em("scalar", lambda e: e.activation(st_std, st_msq, AF.Sqrt, bias=C["eps"][0:1, :]))
            m_std = B.inc("scalar", lambda e: e.activation(
                st_rstd2, st_msq, AF.Sqrt, bias=C["eps2"][0:1, :], scale=1.0 + EPS), "sA")
        else:
            m_std = B.inc("scalar", lambda e: e.activation(st_std, st_msq, AF.Sqrt, bias=C["eps"][0:1, :]), "sA")
        B.wait("tensor", "sA", m_std)
        B.wait("tensor", "sV", state["psb_free"][1])
        B.em("tensor", lambda e: e.matmul(psb0[:, :], C["onesr"][0:1, :], st_mu, start=True, stop=True))
        if kind == "ssm":
            B.em("tensor", lambda e: e.matmul(psb1[:, :], C["onesr"][0:1, :], st_std, start=True, stop=True))
            m_bc = B.inc("tensor", lambda e: e.matmul(psr0[:, :], C["onesr"][0:1, :], st_rstd2, start=True, stop=True), "sT")
        else:
            m_bc = B.inc("tensor", lambda e: e.matmul(psb1[:, :], C["onesr"][0:1, :], st_std, start=True, stop=True), "sT")
        if kind == "ssm":
            B.wait("vector", "sT", m_bc)
            B.em("vector", lambda e: e.reciprocal(tmpc, psr0[:, :]))
            B.em("vector", lambda e: e.tensor_mul(tmpc, tmpc, psb1[:, :]))
            m_bc = B.inc("vector", lambda e: e.reciprocal(psb1[:, :], psb1[:, :]), "sV")
            state["psr_free"] = ("sV", m_bc)
            bc_sem = "sV"
        else:
            bc_sem = "sT"
        if kind == "ffn":
            def apply_fn():
                B.wait("vector", *state["garena_free"])
                B.wait("vector", "sT", m_bc)
                B.em("vector", lambda e: e.reciprocal(psb1[:, :], psb1[:, :]))
                for c in range(DC):
                    B.em("vector", lambda e, c=c: e.tensor_sub(tmpc, src_c(c), psb0[:, :]))
                    if c == DC - 1:
                        m_apply = B.inc("vector", lambda e, c=c: e.tensor_mul(
                            xnf(c), tmpc, psb1[:, :]), "sV")
                    else:
                        B.em("vector", lambda e, c=c: e.tensor_mul(xnf(c), tmpc, psb1[:, :]))
                state["psb_free"] = ("sV", m_apply)
                return ("sV", m_apply)
            if defer_apply:
                return apply_fn
            return apply_fn() + (None,)
        if bc_sem == "sT":
            B.wait("vector", "sT", m_bc)
        deferred = None
        if False:
            pass
        elif kind == "ssm":
            m_apply = 0
            for q in range(4):
                cs = [q * 2, q * 2 + 1]
                for c in cs:
                    B.em("vector", lambda e, c=c: e.tensor_sub(xln_ap(c), src_c(c), psb0[:, :]))
                    if c == cs[-1]:
                        m_vmul = B.inc("vector", lambda e, c=c: e.tensor_mul(
                            xln_ap(c), xln_ap(c), psb1[:, :]), "sV")
                    else:
                        B.em("vector", lambda e, c=c: e.tensor_mul(xln_ap(c), xln_ap(c), psb1[:, :]))
                B.wait("gpsimd", "sV", m_vmul)
                for c in cs:
                    B.em("gpsimd", lambda e, c=c: e.tensor_mul(x2S_ap(c), xln_ap(c), tmpc))
                    if c == cs[-1]:
                        m_apply = B.inc("gpsimd", lambda e, c=c: e.tensor_scalar(
                            xn2_ap(c), x2S_ap(c), C["gam"][:, c:c + 1], C["bet"][:, c:c + 1],
                            ALU.mult, ALU.add), "sP")
                    else:
                        B.em("gpsimd", lambda e, c=c: e.tensor_scalar(
                            xn2_ap(c), x2S_ap(c), C["gam"][:, c:c + 1], C["bet"][:, c:c + 1],
                            ALU.mult, ALU.add))
            deferred = [
                (lambda e, c=c, t0=t0: e.tensor_add(resid_ap(c, t0), resid_ap(c, t0), xln_ap(c)))
                for c in range(DC)]
            state["psb_free"] = ("sV", m_vmul)
            state["psr_free"] = ("sV", m_vmul)
            return ("sP", m_apply, deferred)
        elif kind == "outer":
            for c in range(DC):
                B.em("vector", lambda e, c=c: e.tensor_sub(tmpc, src_c(c), psb0[:, :]))
                if c == DC - 1:
                    m_apply = B.inc("vector", lambda e, c=c: e.tensor_mul(
                        xln_ap(c), tmpc, psb1[:, :]), "sV")
                else:
                    B.em("vector", lambda e, c=c: e.tensor_mul(xln_ap(c), tmpc, psb1[:, :]))
            deferred = [
                (lambda e, c=c, t0=t0: e.tensor_add(resid_ap(c, t0), resid_ap(c, t0), xln_ap(c)))
                for c in range(DC)]
            ret_sem = "sV"
        else:
            for c in range(DC):
                B.em("vector", lambda e, c=c: e.tensor_sub(tmpc, src_c(c), psb0[:, :]))
                if c == DC - 1:
                    m_vmul = B.inc("vector", lambda e, c=c: e.tensor_mul(
                        x2S_ap(c), tmpc, psb1[:, :]), "sV")
                else:
                    B.em("vector", lambda e, c=c: e.tensor_mul(x2S_ap(c), tmpc, psb1[:, :]))
            B.wait("gpsimd", "sV", m_vmul)
            for c in range(DC - 1):
                B.em("gpsimd", lambda e, c=c: e.tensor_scalar(
                    xn2_ap(c), x2S_ap(c), C["gam"][:, c:c + 1], C["bet"][:, c:c + 1],
                    ALU.mult, ALU.add))
            m_apply = B.inc("gpsimd", lambda e: e.tensor_scalar(
                xn2_ap(DC - 1), x2S_ap(DC - 1), C["gam"][:, DC - 1:DC], C["bet"][:, DC - 1:DC],
                ALU.mult, ALU.add), "sP")
            ret_sem = "sP"
        state["psb_free"] = (ret_sem, m_apply) if ret_sem == "sV" else ("sV", m_vmul)
        return (ret_sem, m_apply, deferred)

    def _slot_overlaps(slot_key):
        lo, hi = SLOT_RANGE[slot_key]
        return [k for k, (l2, h2) in SLOT_RANGE.items() if l2 < hi and lo < h2]

    def load_group(wname, oc0, noc, KC, slot_key):
        dt = wdt_of(wname)
        n_el = noc * KC * 128
        slot_ap = byte_slot(slot_key, n_el * (1 if dt == FP8 else 2), dt)
        waits = {}
        for k in _slot_overlaps(slot_key):
            sem, m = state["slot_free"].get(k, ("sT", 0))
            waits[sem] = max(waits.get(sem, 0), m)
        for sem, m in waits.items():
            B.wait("sync", sem, m)
        B.wait("sync", "sF", conv_m[wname][(oc0 + noc - 1) // conv_step[wname]])
        src = wbf[wname][oc0:oc0 + noc].rearrange("n p k o -> p n k o")
        return B.inc("sync", lambda e, o=slot_ap, i=src: e.dma_start(
            out=o, in_=i), "sS", 16), slot_ap

    def stage(wname, KC, OC, group, nslots, rhs_fn, evac_fn, rhs_ready, inject=None):
        for gi, oc0 in enumerate(range(0, OC, group)):
            if inject is not None:
                inject(gi)
            slot_key = ("k4", gi % nslots)
            m_load, pan = load_group(wname, oc0, group, KC, slot_key)
            B.wait("tensor", "sS", m_load)
            B.wait("tensor", rhs_ready[0], rhs_ready[1])
            for j in range(group):
                oc = oc0 + j
                bank = oc % 4
                bsem, bm = state["bank_free"][bank]
                B.wait("tensor", bsem, bm)
                for kc in range(KC - 1):
                    off = (j * KC + kc) * 128
                    B.em("tensor", lambda e, p=pan, o=off, kc=kc, b=bank: e.matmul(
                        ps[b][:, :], p[:, o:o + 128], rhs_fn(kc),
                        start=(kc == 0), stop=False))
                off = (j * KC + KC - 1) * 128
                m_pe = B.inc("tensor", lambda e, p=pan, o=off, b=bank: e.matmul(
                    ps[b][:, :], p[:, o:o + 128], rhs_fn(KC - 1),
                    start=(KC == 1), stop=True), "sT")
                if j == group - 1:
                    state["slot_free"][slot_key] = ("sT", m_pe)
                state["bank_free"][bank] = evac_fn(oc, ps[bank][:, :], m_pe)

    def emit_w12(tile, m_ln, xn_fn=None):
        xnf = xn_fn if xn_fn is not None else xn_ap
        for gi, oc0 in enumerate(range(0, HC, 2)):
            ka, kb = ("k2", (2 * gi) % 8), ("k2", (2 * gi + 1) % 8)
            m_l1, pan_a = load_group("w1", oc0, 2, DC, ka)
            m_l2, pan_b = load_group("w2", oc0, 2, DC, kb)
            B.wait("tensor", "sS", m_l2)
            B.wait("tensor", m_ln[0], m_ln[1])
            for j in range(2):
                oc = oc0 + j
                bA = (oc % 2) * 2
                bB = bA + 1
                for b in (bA, bB):
                    bsem, bm = state["bank_free"][b]
                    B.wait("tensor", bsem, bm)
                for kc in range(DC - 1):
                    off = (j * DC + kc) * 128
                    B.em("tensor", lambda e, p=pan_a, o=off, kc=kc, b=bA: e.matmul(
                        ps[b][:, :], p[:, o:o + 128], xnf(kc),
                        start=(kc == 0), stop=False))
                off = (j * DC + DC - 1) * 128
                m_pe1 = B.inc("tensor", lambda e, p=pan_a, o=off, b=bA: e.matmul(
                    ps[b][:, :], p[:, o:o + 128], xnf(DC - 1),
                    start=False, stop=True), "sT")
                for kc in range(DC - 1):
                    off = (j * DC + kc) * 128
                    B.em("tensor", lambda e, p=pan_b, o=off, kc=kc, b=bB: e.matmul(
                        ps[b][:, :], p[:, o:o + 128], xnf(kc),
                        start=(kc == 0), stop=False))
                off = (j * DC + DC - 1) * 128
                m_pe2 = B.inc("tensor", lambda e, p=pan_b, o=off, b=bB: e.matmul(
                    ps[b][:, :], p[:, o:o + 128], xnf(DC - 1),
                    start=False, stop=True), "sT")
                if j == 1:
                    state["slot_free"][ka] = ("sT", m_pe1)
                    state["slot_free"][kb] = ("sT", m_pe2)
                B.wait("scalar", "sT", m_pe1)
                B.wait("scalar", *state["tmps_free"][oc % 2])
                m_silu = B.inc("scalar", lambda e, b=bA, oc=oc: e.activation(
                    tmps[oc % 2], ps[b][:, :], AF.Silu, scale=C["s1"][:, oc:oc + 1]), "sA")
                state["bank_free"][bA] = ("sA", m_silu)
                B.wait("vector", "sT", m_pe2)
                B.wait("vector", "sA", m_silu)
                m_h = B.inc("vector", lambda e, b=bB, oc=oc: e.scalar_tensor_tensor(
                    h_ap(oc), ps[b][:, :], C["s2"][:, oc:oc + 1], tmps[oc % 2],
                    ALU.mult, ALU.mult), "sV")
                state["bank_free"][bB] = ("sV", m_h)
                state["tmps_free"][oc % 2] = ("sV", m_h)
        return sV.n

    def ffn_tile(tile, m_ln, xn_fn=None, g_fn=None, pend_mk=None, pend_at="h1",
                 apply_now=True):
        g_fn = g_fn if g_fn is not None else g_ap
        t0 = tile * T
        B.wait("tensor", *state["harena_free"])
        m_h = emit_w12(tile, m_ln, xn_fn=xn_fn)

        def ev_gelu_g(oc, p, m_pe):
            B.wait("scalar", "sT", m_pe)
            m = B.inc("scalar", lambda e, p=p, oc=oc: e.activation(
                g_fn(oc), p, AF.Gelu, scale=C["sh1"][:, oc:oc + 1]), "sA")
            return ("sA", m)

        stage("h1", HC, HC, 1, 4, h_ap, ev_gelu_g, ("sV", m_h))
        m_h1 = sA.n
        state["harena_free"] = ("sT", sT.n)
        apply_fn = pend_mk() if (pend_mk and pend_at == "h1") else None

        def ev_gelu_h(oc, p, m_pe):
            B.wait("scalar", "sT", m_pe)
            m = B.inc("scalar", lambda e, p=p, oc=oc: e.activation(
                h_ap(oc), p, AF.Gelu, scale=C["sh2"][:, oc:oc + 1]), "sA")
            return ("sA", m)

        stage("h2", HC, HC, 1, 4, g_fn, ev_gelu_h, ("sA", m_h1))
        m_h2 = sA.n
        state["garena_free"] = ("sT", sT.n)
        if pend_mk and pend_at == "h2":
            apply_fn = pend_mk()
        res = apply_fn() if (apply_fn is not None and apply_now) else apply_fn

        def ev_w3(oc, p, m_pe):
            B.wait("vector", "sT", m_pe)
            m = B.inc("vector", lambda e, p=p, oc=oc, t0=t0: e.scalar_tensor_tensor(
                resid_ap(oc, t0), p, C["s3h"][:, oc:oc + 1], resid_ap(oc, t0),
                ALU.mult, ALU.add), "sV")
            return ("sV", m)

        stage("w3", HC, DC, 1, 4, h_ap, ev_w3, ("sA", m_h2))
        state["harena_free"] = ("sT", sT.n)
        return res

    def emit_ffn(ln0_m=None, after_tile=None):
        m_lns = {0: ln0_m if ln0_m is not None else emit_ln(0, "ffn")[:2]}
        B.wait("vector", "sD", m_x)
        for tile in range(NT):
            pm = (lambda t=tile: emit_ln(t + 1, "ffn", defer_apply=True)) \
                if tile + 1 < NT else None
            r = ffn_tile(tile, m_lns[tile], pend_mk=pm)
            if r is not None:
                m_lns[tile + 1] = r
            if after_tile is not None:
                after_tile(tile)
        state["resid_ready"] = ("sV", sV.n)

    ssmA = {"pend": None, "red": {}, "var": {}, "sq": {}}

    def ssm_partA_copy(tile):
        t0 = tile * T
        src3 = resid3[:, :, t0:t0 + T]
        B.wait("vector", "sT", ssmA["red"].get(tile - 1, 0))
        m_xb = B.inc("vector", lambda e, s3=src3: e.tensor_copy(xb3_ssm, s3), "sV")
        m_x2 = B.inc("vector", lambda e: e.tensor_mul(x23_ssm, xb3_ssm, xb3_ssm), "sV")
        ssmA["pend"] = (tile, m_xb, m_x2)

    def ssm_partA_reduce():
        if ssmA["pend"] is None:
            return
        tile, m_xb, m_x2 = ssmA["pend"]
        ssmA["pend"] = None
        R = ST_ROW[tile]
        B.wait("tensor", "sV", max(m_xb, state["psr_free"][1]))
        for c in range(DC):
            B.em("tensor", lambda e, c=c, R=R: e.matmul(
                psr0[R:R + 1, :], C["onesb"][:, :], xbS_ap(c), start=(c == 0), stop=(c == DC - 1)))
        B.wait("tensor", "sV", m_x2)
        for c in range(DC - 1):
            B.em("tensor", lambda e, c=c, R=R: e.matmul(
                psr1[R:R + 1, :], C["onesb"][:, :], x2S_ap(c), start=(c == 0), stop=False))
        m_red = B.inc("tensor", lambda e, R=R: e.matmul(
            psr1[R:R + 1, :], C["onesb"][:, :], x2S_ap(DC - 1), start=False, stop=True), "sT")
        ssmA["red"][tile] = m_red
        B.wait("vector", "sT", m_red)
        B.em("vector", lambda e, t=tile, R=R: e.tensor_scalar_mul(
            st_row(t, 0), psr0[R:R + 1, :], 1.0 / DM))
        B.em("vector", lambda e, t=tile: e.tensor_mul(st_row(t, 1), st_row(t, 0), st_row(t, 0)))
        m_var = B.inc("vector", lambda e, t=tile, R=R: e.scalar_tensor_tensor(
            st_row(t, 2), psr1[R:R + 1, :], 1.0 / DM, st_row(t, 1), ALU.mult, ALU.subtract), "sV")
        ssmA["var"][tile] = m_var
        state["psr_free"] = ("sV", m_var)

    def ssm_partA(tile):
        ssm_partA_reduce()
        ssm_partA_copy(tile)

    def ssm_sqrts(ts):
        B.wait("scalar", "sV", ssmA["var"][ts[-1]])
        for t in ts:
            R = ST_ROW[t]
            B.em("scalar", lambda e, t=t, R=R: e.activation(
                st_row(t, 3), st_row(t, 2), AF.Sqrt, bias=C["eps"][R:R + 1, :]))
            m = B.inc("scalar", lambda e, t=t, R=R: e.activation(
                st_row(t, 4), st_row(t, 2), AF.Sqrt, bias=C["eps2"][R:R + 1, :], scale=1.0 + EPS), "sA")
            ssmA["sq"][t] = m

    def emit_ffn2(m_ln0, finish_stage4):
        pm1 = lambda: emit_ln(1, "ffn", defer_apply=True,
                              scratch=(xbL1_ap, x2L1_ap, xb3L1), xn_fn=xbL1_ap,
                              copy_wait=True)
        apply1 = ffn_tile(0, m_ln0, xn_fn=xn0_ap, g_fn=g0_ap,
                          pend_mk=pm1, pend_at="h2", apply_now=False)
        m_lns = {1: apply1()}
        finish_stage4()
        for tile in range(1, NT):
            pm = (lambda t=tile: emit_ln(t + 1, "ffn", defer_apply=True)) \
                if tile + 1 < NT else None
            r = ffn_tile(tile, m_lns[tile], pend_mk=pm,
                         xn_fn=xbL1_ap if tile == 1 else None)
            if r is not None:
                m_lns[tile + 1] = r
        state["resid_ready"] = ("sV", sV.n)

    def emit_ssm(m_acol):
        m_inner_l = {}
        B.wait("gpsimd", "sD", m_consts)

        def ssm_bc(tile, m_prev_apply):
            R = ST_ROW[tile]
            B.wait("tensor", "sA", ssmA["sq"][tile])
            B.wait("tensor", "sV", state["psb_free"][1])
            B.wait("tensor", "sV", state["psr_free"][1])
            B.em("tensor", lambda e, t=tile, R=R: e.matmul(
                psb0[:, :], C["onesr"][R:R + 1, :], st_row(t, 0), start=True, stop=True))
            B.em("tensor", lambda e, t=tile, R=R: e.matmul(
                psb1[:, :], C["onesr"][R:R + 1, :], st_row(t, 3), start=True, stop=True))
            m_bcT = B.inc("tensor", lambda e, t=tile, R=R: e.matmul(
                psr0[:, :], C["onesr"][R:R + 1, :], st_row(t, 4), start=True, stop=True), "sT")
            B.wait("vector", "sP", m_prev_apply)
            B.wait("vector", "sT", m_bcT)
            B.em("vector", lambda e: e.reciprocal(tmpc, psr0[:, :]))
            B.em("vector", lambda e: e.tensor_mul(tmpc, tmpc, psb1[:, :]))
            m_fix = B.inc("vector", lambda e: e.reciprocal(psb1[:, :], psb1[:, :]), "sV")
            state["psr_free"] = ("sV", m_fix)

        def ssm_apply(tile):
            t0 = tile * T
            src_c = lambda c: resid_ap(c, t0)
            par = tile % 2
            m_apply = 0
            for q in range(4):
                cs = [q * 2, q * 2 + 1]
                for c in cs:
                    B.em("vector", lambda e, c=c: e.tensor_sub(xln_ap(c), src_c(c), psb0[:, :]))
                    if c == cs[-1]:
                        m_vmul = B.inc("vector", lambda e, c=c: e.tensor_mul(
                            xln_ap(c), xln_ap(c), psb1[:, :]), "sV")
                    else:
                        B.em("vector", lambda e, c=c: e.tensor_mul(xln_ap(c), xln_ap(c), psb1[:, :]))
                B.wait("gpsimd", "sV", m_vmul)
                for c in cs:
                    B.em("gpsimd", lambda e, c=c: e.tensor_mul(x2S_ap(c), xln_ap(c), tmpc))
                    if c == cs[-1]:
                        m_apply = B.inc("gpsimd", lambda e, c=c, par=par: e.tensor_scalar(
                            xn2p_ap(par, c), x2S_ap(c), C["gam"][:, c:c + 1], C["bet"][:, c:c + 1],
                            ALU.mult, ALU.add), "sP")
                    else:
                        B.em("gpsimd", lambda e, c=c, par=par: e.tensor_scalar(
                            xn2p_ap(par, c), x2S_ap(c), C["gam"][:, c:c + 1], C["bet"][:, c:c + 1],
                            ALU.mult, ALU.add))
                for c in cs:
                    B.em("vector", lambda e, c=c, t0=t0: e.tensor_add(
                        resid_ap(c, t0), resid_ap(c, t0), xln_ap(c)))
            state["psb_free"] = ("sV", m_vmul)
            return ("sP", m_apply, None)

        if not ssmA["var"]:
            for t in range(NT - 1):
                ssm_partA(t)
            ssm_partA_reduce()
            ssm_partA_copy(NT - 1)
        ssm_sqrts([0, 1, 2])
        if dbg_small is not None:
            B.wait("gpsimd", "sA", ssmA["sq"][2])
            gdma(dbg_st, garena_f[0:65, 12288:14848])
        B.wait("vector", *state["harena_free"])
        B.wait("vector", *state["garena_free"])
        ssm_bc(0, 0)
        ssm_partA_reduce()
        ssm_sqrts([3])
        m_inner_l[0] = ssm_apply(0)
        for tile in range(NT):
            t0 = tile * T
            m_inner = m_inner_l[tile]

            def inject(gi, t=tile + 1, prev=m_inner):
                if t < NT:
                    if gi == 0:
                        ssm_bc(t, prev[1])
                    elif gi == 2:
                        m_inner_l[t] = ssm_apply(t)

            def ev_xz(oc, p, m_pe, tile=tile):
                B.wait("scalar", "sT", m_pe)
                if oc < 8:
                    m = B.inc("scalar", lambda e, p=p, oc=oc: e.activation(
                        xp_ap(oc), p, AF.Silu, bias=C["binx"][:, oc:oc + 1]), "sA")
                else:
                    m = B.inc("scalar", lambda e, p=p, oc=oc, tile=tile: e.activation(
                        z_ap(tile, oc - 8), p, AF.Sigmoid, bias=C["binz"][:, oc - 8:oc - 7]), "sA")
                return ("sA", m)

            stage("win", DC, 16, 2, 2,
                  lambda c, par=tile % 2: xn2p_ap(par, c), ev_xz,
                  (m_inner[0], m_inner[1]), inject=inject)
            m_xp = sA.n
            bank = 0
            bsem, bm = state["bank_free"][bank]
            B.wait("tensor", bsem, bm)
            B.wait("tensor", "sA", m_xp)
            for kc in range(DC - 1):
                B.em("tensor", lambda e, kc=kc, b=bank: e.matmul(
                    ps[b][0:64, :], C["wxpb"][:, kc * 64:(kc + 1) * 64], xp_ap(kc),
                    start=(kc == 0), stop=False))
            m_upe = B.inc("tensor", lambda e, b=bank: e.matmul(
                ps[b][0:64, :], C["wxpb"][:, (DC - 1) * 64:DC * 64], xp_ap(DC - 1),
                start=False, stop=True), "sT")
            B.wait("vector", "sT", m_upe)
            m_u = B.inc("vector", lambda e, t0=t0, b=bank: e.tensor_scalar_add(
                u_ap(t0, T), ps[b][0:64, :], C["bxp"][:, :]), "sV")
            state["bank_free"][bank] = ("sV", m_u)

        B.wait("vector", "sA", m_acol)
        m_scan1 = B.inc("vector", lambda e: e.tensor_tensor_scan(
            hs_ap(0, TOK), C["acol"].broadcast_to([64, TOK])[:, :], u_ap(0, TOK),
            0.0, ALU.mult, ALU.add), "sV")
        B.wait("gpsimd", "sV", m_scan1)
        m_f = gdma(ag_in, hs_ap(TOK - 1, 1))
        B.wait("gpsimd", "sG", m_f)
        m_cc = B.inc("gpsimd", lambda e: e.collective_compute(
            "AllGather", ALU.bypass,
            replica_groups=[[0, 1, 2, 3, 4, 5, 6, 7]],
            ins=[ag_in.opt()], outs=[ag_out.opt()]), "sC", 1)
        B.wait("gpsimd", "sC", m_cc)
        m_fsb = gdma(C["fsb8"][:, :], ag_out.rearrange("(r p) one -> r (p one)", p=64))
        m_fsbF = gfence()
        B.wait("tensor", "sF", m_fsbF)
        B.wait("tensor", state["psr_free"][0], state["psr_free"][1])
        m_ftr = B.inc("tensor", lambda e: e.matmul(
            psr1[0:64, 0:8], C["fsb8"][:, :], C["eye8"][:, :],
            start=True, stop=True), "sT")
        B.wait("vector", "sT", m_ftr)
        B.wait("vector", "sA", m_wqe)
        m_wf = B.inc("vector", lambda e: e.tensor_mul(
            C["wq"][:, :], C["wqe"][:, :], psr1[0:64, 0:8]), "sV")
        B.wait("vector", "sV", m_wf)
        m_c = B.inc("vector", lambda e: e.reduce_sum(
            C["ccol"][:, :], C["wq"][:, :], axis=mybir.AxisListType.X), "sV")
        B.wait("vector", "sV", m_c)
        m_scan2 = B.inc("vector", lambda e: e.tensor_tensor_scan(
            hs_ap(0, TOK), C["acol"].broadcast_to([64, TOK])[:, :], u_ap(0, TOK),
            C["ccol"][:, :], ALU.mult, ALU.add), "sV")
        if dbg_small is not None:
            B.wait("gpsimd", "sV", m_scan2)
            gdma(dbg_u, u_ap(0, TOK))
            gdma(dbg_h, hs_ap(0, TOK))
            gdma(dbg_small[:, 0:1], C["ccol"][:, :], allow_slow_non_contiguous=True)
            gdma(dbg_small[:, 1:5], C["fsb"][:, 0:4], allow_slow_non_contiguous=True)
            gdma(dbg_small[:, 5:9], C["wq"][:, 0:4], allow_slow_non_contiguous=True)
            gdma(dbg_small[:, 9:10], C["acol"][:, :], allow_slow_non_contiguous=True)
            gdma(dbg_small[:, 10:11], C["texp"][:, :], allow_slow_non_contiguous=True)

        B.wait("vector", "sA", B.sems["sA"].n)
        m_cc_ag = []
        for ti, tile in enumerate((0, 2, 1, 3)):
            t0 = tile * T
            m_yd = 0
            for oc in range(DC):
                bank = (ti * DC + oc) % 4
                bsem, bm = state["bank_free"][bank]
                B.wait("tensor", bsem, bm)
                B.wait("tensor", "sV", m_scan2)
                m_pe = B.inc("tensor", lambda e, oc=oc, t0=t0, b=bank: e.matmul(
                    ps[b][:, :], C["dmat"][:, oc * 128:(oc + 1) * 128], hs_ap(t0, T),
                    start=True, stop=True), "sT")
                B.wait("vector", "sT", m_pe)
                m_yd = B.inc("vector", lambda e, tile=tile, oc=oc, b=bank: e.tensor_mul(
                    yd_ap(tile, oc), ps[b][:, :], z_ap(tile, oc)), "sV")
                state["bank_free"][bank] = ("sV", m_yd)
            B.wait("gpsimd", "sV", m_yd)
            m_agi = gdma(
                ag_ydi[tile % 2].rearrange("p (o w t) -> p o w t", o=8, w=2)[:, :, tile // 2, :],
                harena.reshape([128, 12, 2048])[:, 2:10, tile * T:(tile + 1) * T])
            if ti % 2 == 1:
                B.wait("gpsimd", "sG", m_agi)
                if m_cc_ag:
                    B.wait("gpsimd", "sC", m_cc_ag[-1])
                m_cc_ag.append(B.inc("gpsimd", lambda e, ab=tile % 2: e.collective_compute(
                    "AllGather", ALU.bypass,
                    replica_groups=[[0, 1, 2, 3, 4, 5, 6, 7]],
                    ins=[ag_ydi[ab].opt()], outs=[ag_ydo[ab].opt()]), "sC", 1))
        m_yd_all = sV.n

        sGs = B.sems["sG"]
        m_fs = []
        for ab in range(2):
            B.wait("gpsimd", "sC", m_cc_ag[ab])
            agv = ag_ydo[ab].rearrange("(r p) (o w t) -> r p o w t", p=128, o=8, w=2)

            def em_fetch(e, agv=agv, ab=ab):
                for arm in nc.switch_core_id(engines=[e]):
                    cid = arm.logical % 8
                    q0, q1, tb = cid // 2, 4 + cid // 2, cid % 2
                    for par, q in ((0, q0), (1, q1)):
                        if ab == 0:
                            dst = yd2r[:, 2:10, par, 0:512]
                        else:
                            dst = garBv[:, :, par, :]
                        e.dma_start(out=dst, in_=agv[q, :, :, tb, :]).then_inc(sGs.h, 16)

            B.em("gpsimd", em_fetch)
            sGs.n += 32
            m_fs.append(gfence())
        m_f1 = m_fs[-1]

        m_wo = []
        for gidx in range(2):
            B.wait("sync", "sV", m_yd_all)
            B.wait("sync", "sF", conv_m["wout"][gidx])
            src = wbf["wout"][gidx * 4:(gidx + 1) * 4].rearrange("n p k o -> p n k o")
            dst = garena[:, 12288 + gidx * 4096: 12288 + (gidx + 1) * 4096]
            m_wo.append(B.inc("sync", lambda e, o=dst, i=src: e.dma_start(
                out=o, in_=i), "sS", 16))

        def wout_sl(oc, kc):
            g, j = oc // 4, oc % 4
            base = 12288 + g * 4096 + (j * DC + kc) * 128
            return garena[:, base: base + 128]

        def stage4_tile(tile):
            for oc in range(DC):
                bank = (tile * DC + oc) % 4
                bsem, bm = state["bank_free"][bank]
                B.wait("tensor", bsem, bm)
                B.wait("tensor", "sS", m_wo[oc // 4])
                B.wait("tensor", "sF", m_fs[tile // 2])
                for kc in range(DC - 1):
                    B.em("tensor", lambda e, oc=oc, kc=kc, tile=tile, b=bank: e.matmul(
                        ps[b][:, :], wout_sl(oc, kc), yd2_rhs(kc, tile),
                        start=(kc == 0), stop=False))
                m_pe = B.inc("tensor", lambda e, oc=oc, tile=tile, b=bank: e.matmul(
                    ps[b][:, :], wout_sl(oc, DC - 1), yd2_rhs(DC - 1, tile),
                    start=False, stop=True), "sT")
                B.wait("vector", "sT", m_pe)
                m_y = B.inc("vector", lambda e, oc=oc, tile=tile, b=bank: e.scalar_tensor_tensor(
                    resid_int(oc, tile), ps[b][:, :], C["bout"][:, oc:oc + 1], resid_int(oc, tile),
                    ALU.add, ALU.add), "sV")
                state["bank_free"][bank] = ("sV", m_y)

        special = "ffn2" in phases and tap_after is None
        m_ln0_next = None
        B.wait("vector", "sP", state.get("m_adds", 0))
        stage4_tile(0)
        if special:
            m_ln0_next = emit_ln(0, "ffn", scratch=(xb0_ap, x2_ap, xb3_0),
                                 xn_fn=xn0_ap)[:2]
        stage4_tile(1)
        state["harena_free"] = ("sT", sT.n)
        for k in [("k4", 2), ("k4", 3), ("k2", 4), ("k2", 5), ("k2", 6), ("k2", 7)]:
            state["slot_free"][k] = ("sV", m_scan2)

        def finish_stage4():
            stage4_tile(2)
            stage4_tile(3)
            state["garena_free"] = ("sT", sT.n)
            state["resid_ready"] = ("sV", sV.n)

        if not special:
            state["garena_free"] = ("sT", sT.n)
            finish_stage4()
            return None, None
        return m_ln0_next, finish_stage4

    def maybe_tap(ph):
        if tap_after == ph:
            B.wait("gpsimd", *state["resid_ready"])
            gdma(dbg_tap.rearrange("(c p) t -> p c t", p=128), resid3[:, :, :])

    m_ln0_ffn2, fs4 = None, None
    if "ffn1" in phases:
        emit_ffn(after_tile=ssm_partA if "ssm" in phases else None)
        maybe_tap("ffn1")
    if "ssm" in phases:
        m_ln0_ffn2, fs4 = emit_ssm(m_acol)
        maybe_tap("ssm")
    if "ffn2" in phases:
        if fs4 is not None:
            emit_ffn2(m_ln0_ffn2, fs4)
        else:
            emit_ffn()

    B.wait("gpsimd", *state["resid_ready"])
    m_out = gdma(yout.rearrange("(c p) t -> p c t", p=128), resid3[:, :, :])
    B.wait("gpsimd", "sG", m_out)

    with nc.Block() as block:
        @block.sync
        def _(e):
            B.run(e, "sync")

        @block.scalar
        def _(e):
            B.run(e, "scalar")

        @block.vector
        def _(e):
            B.run(e, "vector")

        @block.tensor
        def _(e):
            B.run(e, "tensor")

        @block.gpsimd
        def _(e):
            B.run(e, "gpsimd")

    stack.close()
    return nc


def _mk_in_maps(inputs):
    x = np.asarray(inputs["L0_attn_x"], dtype=np.float32)
    b_in = np.asarray(inputs["L0_attn_b_in"], dtype=np.float32)
    base = dict(
        d_s1=_coltile(inputs["L0_ffn_w1_s"], 32),
        d_s2=_coltile(inputs["L0_ffn_w2_s"], 32),
        d_sh1=_coltile(inputs["L0_ffn_h1_s"], 32),
        d_sh2=_coltile(inputs["L0_ffn_h2_s"], 32),
        d_s3=_coltile(inputs["L0_ffn_w3_s"], 8),
        d_bout=_coltile(inputs["L0_attn_b_out"], 8),
        d_gam=_coltile(inputs["L0_attn_gamma"], 8),
        d_bet=_coltile(inputs["L0_attn_beta"], 8),
        d_binx=_coltile(b_in[:1024], 8),
        d_binz=_coltile(b_in[1024:], 8),
        d_alog=np.ascontiguousarray(np.asarray(inputs["L0_attn_A_log"], dtype=np.float32).reshape(64, 1)),
        d_bxp=np.ascontiguousarray(np.asarray(inputs["L0_attn_b_xp"], dtype=np.float32).reshape(64, 1)),
        d_wxp=np.ascontiguousarray(
            np.asarray(inputs["L0_attn_W_xp"], dtype=np.float32).T.reshape(8, 128, 64)
            .transpose(1, 0, 2).reshape(128, 512)),
        d_dmat=np.ascontiguousarray(np.asarray(inputs["L0_attn_Dmat"], dtype=np.float32)),
        d_eye8=np.eye(8, dtype=np.float32),
        wf_w1=_panels(inputs["L0_ffn_w1_w"]),
        wf_w2=_panels(inputs["L0_ffn_w2_w"]),
        wf_h1=_panels(inputs["L0_ffn_h1_w"]),
        wf_h2=_panels(inputs["L0_ffn_h2_w"]),
        wf_w3=_panels(inputs["L0_ffn_w3_w"]),
        wf_win=_panels(inputs["L0_attn_W_in"]),
        wf_wout=_panels(inputs["L0_attn_W_out"]),
    )
    in_maps = []
    for c in range(N_CORES):
        b, r = c // 4, c % 4
        m = dict(base)
        m["xin"] = np.ascontiguousarray(x[b, r * TOK:(r + 1) * TOK, :].T)
        kv = np.zeros((64, 8), dtype=np.float32)
        for q in range(8):
            qq = q - 4 * b
            kv[:, q] = 2048.0 * (r - 1 - qq) if (0 <= qq < r) else 1e9
        m["d_kvec"] = kv
        in_maps.append(m)
    return in_maps


def kernel(**inputs):
    in_maps = _mk_in_maps(inputs)
    nc = build_nc()
    res = run_bass_kernel_spmd(nc, in_maps, core_ids=list(range(N_CORES)))
    out = np.empty((2, 8192, 1024), dtype=np.float32)
    for c in range(N_CORES):
        b, r = c // 4, c % 4
        out[b, r * TOK:(r + 1) * TOK, :] = np.asarray(res.results[c]["yout"]).T
    return out


# version 35 (final)
# speedup vs baseline: 1.0777x; 1.0286x over previous
import contextlib
import numpy as np
import concourse.bass as bass
import concourse.mybir as mybir
from concourse.bass_utils import run_bass_kernel_spmd

F32 = mybir.dt.float32
BF16 = mybir.dt.bfloat16
AF = mybir.ActivationFunctionType
ALU = mybir.AluOpType

N_CORES = 8
TOK = 2048
T = 512
NT = TOK // T
DM = 1024
DC = DM // 128
DH = 4096
HC = DH // 128
EPS = 1e-5

WSPECS = {
    "w1": (1024, 4096), "w2": (1024, 4096),
    "h1": (4096, 4096), "h2": (4096, 4096),
    "w3": (4096, 1024),
    "win": (1024, 2048), "wout": (1024, 1024),
}
FP8 = mybir.dt.float8e4
TERNARY = {"w1", "w2", "h1", "h2", "w3"}


def _panels(w):
    O, K = w.shape
    KC, OC = K // 128, O // 128
    wt = np.ascontiguousarray(np.asarray(w, dtype=np.float32).T)
    return np.ascontiguousarray(wt.reshape(KC, 128, OC, 128).transpose(2, 1, 0, 3))


def _coltile(v, noc):
    v = np.asarray(v, dtype=np.float32)
    return np.ascontiguousarray(v.reshape(noc, 128).T)


class _Sem:
    def __init__(self, h):
        self.h = h
        self.n = 0


class Build:
    ENGINES = ("sync", "scalar", "vector", "tensor", "gpsimd")

    def __init__(self, nc):
        self.nc = nc
        self.prog = {e: [] for e in self.ENGINES}
        self.sems = {}

    def sem(self, name, handle):
        self.sems[name] = _Sem(handle)
        return self.sems[name]

    def em(self, eng, fn):
        self.prog[eng].append(fn)

    def wait(self, eng, sem, m):
        if not m:
            return
        s = self.sems[sem]
        self.em(eng, lambda e, s=s, m=m: e.wait_ge(s.h, m))

    def inc(self, eng, build_fn, sem, k=1):
        s = self.sems[sem]
        s.n += k
        m = s.n
        self.em(eng, lambda e, f=build_fn, s=s, k=k: f(e).then_inc(s.h, k))
        return m

    def run(self, eng_obj, eng_name):
        for f in self.prog[eng_name]:
            f(eng_obj)


def build_nc(phases=("ffn1", "ssm", "ffn2"), tap_after=None):
    nc = bass.Bass("TRN2", target_bir_lowering=False, debug=False)
    dbg_tap = None
    dbg_small = None
    if tap_after is not None:
        dbg_tap = nc.dram_tensor("dbg_tap", [DM, TOK], F32, kind="ExternalOutput").ap()
        dbg_small = nc.dram_tensor("dbg_small", [64, 11], F32, kind="ExternalOutput").ap()
        dbg_u = nc.dram_tensor("dbg_u", [64, TOK], F32, kind="ExternalOutput").ap()
        dbg_h = nc.dram_tensor("dbg_h", [64, TOK], F32, kind="ExternalOutput").ap()
        dbg_st = nc.dram_tensor("dbg_st", [65, 2560], F32, kind="ExternalOutput").ap()

    xin = nc.dram_tensor("xin", [DM, TOK], F32, kind="ExternalInput").ap()
    yout = nc.dram_tensor("yout", [DM, TOK], F32, kind="ExternalOutput").ap()
    wf32, wbf = {}, {}
    for w, (K, O) in WSPECS.items():
        KC, OC = K // 128, O // 128
        wf32[w] = nc.dram_tensor("wf_" + w, [OC, 128, KC, 128], F32, kind="ExternalInput").ap()
        wdt = FP8 if w in TERNARY else BF16
        wbf[w] = nc.dram_tensor("wb_" + w, [OC, 128, KC, 128], wdt, kind="Internal").ap()
    dins = {}
    for nm, shp in [("d_s1", [128, 32]), ("d_s2", [128, 32]), ("d_sh1", [128, 32]),
                    ("d_sh2", [128, 32]), ("d_s3", [128, 8]), ("d_bout", [128, 8]),
                    ("d_gam", [128, 8]), ("d_bet", [128, 8]), ("d_binx", [128, 8]),
                    ("d_binz", [128, 8]), ("d_alog", [64, 1]), ("d_bxp", [64, 1]),
                    ("d_kvec", [64, 8]), ("d_wxp", [128, 512]), ("d_dmat", [64, 1024]),
                    ("d_eye8", [8, 8])]:
        dins[nm] = nc.dram_tensor(nm, shp, F32, kind="ExternalInput").ap()
    ag_in = nc.dram_tensor("ag_in", [64, 1], F32, kind="Internal").ap()
    ag_out = nc.dram_tensor("ag_out", [512, 1], F32, kind="Internal", addr_space="Shared").ap()
    ag_ydi = [nc.dram_tensor(f"ag_ydi{i}", [128, 8192], BF16, kind="Internal").ap() for i in range(2)]
    ag_ydo = [nc.dram_tensor(f"ag_ydo{i}", [1024, 8192], BF16, kind="Internal", addr_space="Shared").ap() for i in range(2)]
    ag_w8i = nc.dram_tensor("ag_w8i", [64, 1], F32, kind="Internal").ap()
    ag_w8o = nc.dram_tensor("ag_w8o", [512, 1], F32, kind="Internal", addr_space="Shared").ap()

    resid = nc.alloc_sbuf_tensor("resid", [128, DC * TOK], F32)
    harena = nc.alloc_sbuf_tensor("harena", [128, 24576], BF16)
    garena = nc.alloc_sbuf_tensor("garena", [128, 32768], BF16)
    wpan = nc.alloc_sbuf_tensor("wpan", [128, 8192], BF16)
    harena_f = harena.bitcast(F32)
    garena_f = garena.bitcast(F32)
    wpan_f = wpan.bitcast(F32)
    wpan_f8 = wpan.bitcast(FP8)

    C = {}
    for nm, shp, dt in [("s1", [128, 32], F32), ("s2", [128, 32], F32),
                        ("sh1", [128, 32], F32), ("sh2", [128, 32], F32),
                        ("s3h", [128, 8], F32), ("bout", [128, 8], F32),
                        ("gam", [128, 8], F32), ("bet", [128, 8], F32),
                        ("binx", [128, 8], F32), ("binz", [128, 8], F32),
                        ("alog", [64, 1], F32), ("bxp", [64, 1], F32),
                        ("kvec", [64, 8], F32), ("texp", [64, 1], F32),
                        ("acol", [64, 1], F32), ("wq", [64, 8], F32),
                        ("fsb", [64, 8], F32), ("ccol", [64, 1], F32),
                        ("fsb8", [8, 64], F32), ("wqe", [64, 8], F32),
                        ("eye8", [8, 8], F32),
                        ("wxpf", [128, 512], F32), ("wxpb", [128, 512], BF16),
                        ("dmat", [64, 1024], F32), ("eps", [65, 1], F32), ("eps2", [65, 1], F32),
                        ("onesb", [128, 1], BF16), ("onesr", [65, 128], F32)]:
        C[nm] = nc.alloc_sbuf_tensor("c_" + nm, shp, dt)

    ps = [nc.alloc_psum_tensor(f"ps{i}", [128, 512], F32) for i in range(4)]
    psb0 = nc.alloc_psum_tensor("psb0", [128, 512], F32)
    psb1 = nc.alloc_psum_tensor("psb1", [128, 512], F32)
    psr0 = nc.alloc_psum_tensor("psr0", [128, 512], F32)
    psr1 = nc.alloc_psum_tensor("psr1", [128, 512], F32)

    resid3 = resid.reshape([128, DC, TOK])

    def resid_ap(c, t0, n=T):
        return resid[:, c * TOK + t0: c * TOK + t0 + n]

    def h_ap(oc):
        return harena[:, oc * T:(oc + 1) * T]

    def g_ap(oc):
        return garena[:, oc * T:(oc + 1) * T]

    xn_ap = g_ap

    def xb_ap(c):
        return garena[:, 4096 + c * T: 4096 + (c + 1) * T]

    def x2_ap(c):
        return garena[:, 8192 + c * T: 8192 + (c + 1) * T]

    xb3_ffn = garena.reshape([128, 64, T])[:, 8:16, :]

    def xbA_ap(c):
        return garena[:, 16384 + c * T: 16384 + (c + 1) * T]

    def x2A_ap(c):
        return garena[:, 20480 + c * T: 20480 + (c + 1) * T]

    xb3A = garena.reshape([128, 64, T])[:, 32:40, :]

    h48 = harena.reshape([128, 48, T])

    def xn0_ap(c):
        return harena[:, 20480 + c * T: 20480 + (c + 1) * T]

    def g0_ap(oc):
        if oc < 8:
            return garena[:, 8192 + oc * T: 8192 + (oc + 1) * T]
        if oc < 16:
            return garena[:, 20480 + (oc - 8) * T: 20480 + (oc - 7) * T]
        return harena[:, 16384 + (oc - 16) * T: 16384 + (oc - 15) * T]

    def xb0_ap(c):
        return harena[:, c * T:(c + 1) * T]

    xb3_0 = h48[:, 0:8, :]

    def xbL1_ap(c):
        return harena[:, 16384 + c * T: 16384 + (c + 1) * T]

    def x2L1_ap(c):
        return harena[:, 20480 + c * T: 20480 + (c + 1) * T]

    xb3L1 = h48[:, 32:40, :]

    garBv = garena[:, 0:8192].rearrange("p (k two t) -> p k two t", two=2, t=T)

    st_mu = garena_f[0:1, 12288:12800]
    st_msq = garena_f[0:1, 12800:13312]
    st_musq = garena_f[0:1, 13312:13824]
    st_std = garena_f[0:1, 13824:14336]
    st_rstd = garena_f[0:1, 14336:14848]
    tmpc = garena_f[:, 14848:15360]
    st_rstd2 = garena_f[0:1, 15360:15872]
    tmps = [garena[:, 30720:31232], garena[:, 31232:31744]]

    def xln_ap(c):
        return harena_f[:, c * T:(c + 1) * T]

    def xn2p_ap(par, c):
        base = 8192 if par == 0 else 16384
        return harena[:, base + c * T: base + (c + 1) * T]

    def xn2_ap(c):
        return harena[:, 8192 + c * T: 8192 + (c + 1) * T]

    def xp_ap(c):
        return harena[:, 12288 + c * T: 12288 + (c + 1) * T]

    def xbS_ap(c):
        return harena[:, 16384 + c * T: 16384 + (c + 1) * T]

    def x2S_ap(c):
        return harena[:, 20480 + c * T: 20480 + (c + 1) * T]

    xb3_ssm = harena.reshape([128, 48, T])[:, 32:40, :]
    x23_ssm = harena.reshape([128, 48, T])[:, 40:48, :]

    ST_ROW = [32, 64, 0, 32]

    def st_row(t, k):
        r = ST_ROW[t]
        base = 12288 + k * 512
        return garena_f[r:r + 1, base: base + 512]

    def z_ap(tile, c):
        base = (tile * 8 + c) * T
        return garena_f[:, base: base + T]

    def u_ap(t0, n):
        return wpan_f[0:64, 2048 + t0: 2048 + t0 + n]

    def hs_ap(t0, n):
        return harena_f[0:64, t0: t0 + n]

    def yd_ap(tile, kc):
        base = 4096 + kc * TOK + tile * T
        return harena[:, base: base + T]

    yd2r = harena.reshape([128, 12, 2, 1024])

    def yd2_rhs(kc, tile):
        if tile < 2:
            return yd2r[:, 2 + kc, :, tile * 256:(tile + 1) * 256]
        return garBv[:, kc, :, (tile - 2) * 256:(tile - 1) * 256]

    resid_scr = resid.rearrange("p (c j two) -> p c two j", c=DC, two=2)

    def resid_int(c, tile):
        return resid_scr[:, c, :, tile * 256:(tile + 1) * 256]

    SLOT_RANGE = {}
    for _s in range(8):
        SLOT_RANGE[("k2", _s)] = (_s * 2048, (_s + 1) * 2048)
    for _s in range(4):
        SLOT_RANGE[("k4", _s)] = (_s * 4096, (_s + 1) * 4096)

    def byte_slot(key, nbytes, dt):
        lo, hi = SLOT_RANGE[key]
        assert nbytes <= hi - lo
        if dt == FP8:
            return wpan_f8[:, lo: lo + nbytes]
        return wpan[:, lo // 2: (lo + nbytes) // 2]

    def wdt_of(w):
        return FP8 if w in TERNARY else BF16

    B = Build(nc)
    stack = contextlib.ExitStack()
    for snm in ("sS", "sG", "sT", "sA", "sV", "sC", "sD", "sP"):
        B.sem(snm, stack.enter_context(nc.semaphore(snm)))
    sT, sA, sV = B.sems["sT"], B.sems["sA"], B.sems["sV"]

    def gdma(out, in_, **kw):
        return B.inc("gpsimd", lambda e, o=out, i=in_, kw=kw: e.dma_start(out=o, in_=i, **kw), "sG", 16)

    xin3 = xin.rearrange("(c p) t -> p c t", p=128)
    m_x1 = B.inc("scalar", lambda e: e.dma_start(
        out=resid3[:, :, 0:T], in_=xin3[:, :, 0:T]), "sD", 16)
    m_x = B.inc("scalar", lambda e: e.dma_start(
        out=resid3[:, :, T:TOK], in_=xin3[:, :, T:TOK]), "sD", 16)
    for cn, dn in [("s1", "d_s1"), ("s2", "d_s2"), ("sh1", "d_sh1"), ("sh2", "d_sh2"),
                   ("bout", "d_bout"), ("gam", "d_gam"), ("bet", "d_bet"),
                   ("binx", "d_binx"), ("binz", "d_binz"), ("alog", "d_alog"),
                   ("bxp", "d_bxp"), ("kvec", "d_kvec"), ("wxpf", "d_wxp"),
                   ("dmat", "d_dmat"), ("eye8", "d_eye8")]:
        B.inc("scalar", lambda e, o=C[cn][:, :], i=dins[dn]: e.dma_start(out=o, in_=i), "sD", 16)
    m_consts = B.inc("scalar", lambda e: e.dma_start(out=C["s3h"][:, :], in_=dins["d_s3"]), "sD", 16)

    conv_gate = {"w1": None, "w2": None,
                 "h1": lambda g: 16 * (4 + 2 * g),
                 "h2": lambda g: 16 * (24 + 3 * g),
                 "w3": lambda g: 16 * 60,
                 "win": lambda g: 16 * 80,
                 "wout": lambda g: 16 * 90}
    conv_step = {w: 2 if w in ("w1", "w2") else 4 for w in WSPECS}
    conv_m = {w: [] for w in WSPECS}
    for oc0 in range(0, 32, 2):
        for w in ("w1", "w2"):
            conv_m[w].append(gdma(wbf[w][oc0:oc0 + 2], wf32[w][oc0:oc0 + 2],
                                  max_dma_last_dim=8192))
    for w, (K, O) in WSPECS.items():
        if w in ("w1", "w2"):
            continue
        KC, OC = K // 128, O // 128
        step = min(4, OC)
        for gi, oc0 in enumerate(range(0, OC, step)):
            if conv_gate[w] is not None:
                B.wait("gpsimd", "sS", conv_gate[w](gi))
            conv_m[w].append(gdma(wbf[w][oc0:oc0 + step], wf32[w][oc0:oc0 + step],
                                  max_dma_last_dim=8192))

    mw1 = B.inc("gpsimd", lambda e: e.collective_compute(
        "AllGather", ALU.bypass,
        replica_groups=[[0, 1, 2, 3, 4, 5, 6, 7]],
        ins=[ag_w8i.opt()], outs=[ag_w8o.opt()]), "sC", 1)
    B.wait("gpsimd", "sC", mw1)

    B.wait("vector", "sD", m_consts)
    B.em("vector", lambda e: e.memset(C["eps"][:, :], EPS))
    B.em("vector", lambda e: e.memset(C["eps2"][:, :], EPS * EPS))
    B.em("vector", lambda e: e.memset(C["onesb"][:, :], 1.0))
    B.em("vector", lambda e: e.memset(C["onesr"][:, :], 1.0))
    B.em("vector", lambda e: e.tensor_scalar_mul(C["s3h"][:, :], C["s3h"][:, :], 0.5))
    m_setup = B.inc("vector", lambda e: e.tensor_copy(C["wxpb"][:, :], C["wxpf"][:, :]), "sV")
    B.wait("scalar", "sV", m_setup)
    m_texp = B.inc("scalar", lambda e: e.activation(C["texp"][:, :], C["alog"][:, :], AF.Exp), "sA")
    B.wait("vector", "sA", m_texp)
    m_negt = B.inc("vector", lambda e: e.tensor_scalar_mul(C["acol"][:, :], C["texp"][:, :], -1.0), "sV")
    B.wait("scalar", "sV", m_negt)
    m_acol = B.inc("scalar", lambda e: e.activation(C["acol"][:, :], C["acol"][:, :], AF.Exp), "sA")
    B.wait("vector", "sD", m_consts)
    m_wq0 = B.inc("vector", lambda e: e.tensor_scalar_mul(
        C["wqe"][:, :], C["kvec"][:, :], C["texp"][:, :]), "sV")
    B.wait("scalar", "sV", m_wq0)
    m_wqe = B.inc("scalar", lambda e: e.activation(
        C["wqe"][:, :], C["wqe"][:, :], AF.Exp, scale=-1.0), "sA")

    state = {
        "harena_free": ("sT", 0),
        "garena_free": ("sT", 0),
        "psb_free": ("sV", 0),
        "psr_free": ("sV", 0),
        "bank_free": {i: ("sV", 0) for i in range(4)},
        "slot_free": {},
        "tmps_free": {0: ("sV", 0), 1: ("sV", 0)},
        "resid_ready": ("sV", 0),
    }
    B.wait("vector", "sD", m_x1)

    def emit_ln(tile, kind, alt=True, defer_apply=False, scratch=None, xn_fn=None,
                copy_wait=False):
        t0 = tile * T
        xnf = xn_fn if xn_fn is not None else xn_ap
        if kind == "ffn":
            src_c = lambda c: resid_ap(c, t0)
            if scratch is not None:
                xbf, x2f, xb3 = scratch
            else:
                xbf, x2f, xb3 = (xbA_ap, x2A_ap, xb3A) if alt else (xb_ap, x2_ap, xb3_ffn)
        elif kind in ("outer", "ssm"):
            src_c = lambda c: resid_ap(c, t0)
            xbf, x2f, xb3 = xbS_ap, x2S_ap, xb3_ssm
        else:
            src_c = xln_ap
            xbf, x2f, xb3 = xbS_ap, x2S_ap, xb3_ssm

        if kind == "inner":
            for c in range(DC - 1):
                B.em("vector", lambda e, c=c: e.tensor_copy(xbf(c), src_c(c)))
            m_xb = B.inc("vector", lambda e: e.tensor_copy(xbf(DC - 1), src_c(DC - 1)), "sV")
        else:
            if copy_wait:
                B.wait("vector", *state["garena_free"])
            src3 = resid3[:, :, t0:t0 + T]
            m_xb = B.inc("vector", lambda e, s3=src3, d3=xb3: e.tensor_copy(d3, s3), "sV")
        B.wait("scalar", "sV", m_xb)
        for c in range(DC - 1):
            B.em("scalar", lambda e, c=c: e.activation(x2f(c), xbf(c), AF.Square))
        m_x2 = B.inc("scalar", lambda e: e.activation(x2f(DC - 1), xbf(DC - 1), AF.Square), "sA")
        B.wait("tensor", "sV", max(m_xb, state["psr_free"][1]))
        for c in range(DC):
            B.em("tensor", lambda e, c=c: e.matmul(
                psr0[0:1, :], C["onesb"][:, :], xbf(c), start=(c == 0), stop=(c == DC - 1)))
        B.wait("tensor", "sA", m_x2)
        for c in range(DC - 1):
            B.em("tensor", lambda e, c=c: e.matmul(
                psr1[0:1, :], C["onesb"][:, :], x2f(c), start=(c == 0), stop=False))
        m_red = B.inc("tensor", lambda e: e.matmul(
            psr1[0:1, :], C["onesb"][:, :], x2f(DC - 1), start=False, stop=True), "sT")
        B.wait("vector", "sT", m_red)
        B.em("vector", lambda e: e.tensor_scalar_mul(st_mu, psr0[0:1, :], 1.0 / DM))
        B.em("vector", lambda e: e.tensor_mul(st_musq, st_mu, st_mu))
        m_var = B.inc("vector", lambda e: e.scalar_tensor_tensor(
            st_msq, psr1[0:1, :], 1.0 / DM, st_musq, ALU.mult, ALU.subtract), "sV")
        state["psr_free"] = ("sV", m_var)
        B.wait("scalar", "sV", m_var)
        if kind in ("ssm", "inner"):
            B.em("scalar", lambda e: e.activation(st_std, st_msq, AF.Sqrt, bias=C["eps"][0:1, :]))
            m_std = B.inc("scalar", lambda e: e.activation(
                st_rstd2, st_msq, AF.Sqrt, bias=C["eps2"][0:1, :], scale=1.0 + EPS), "sA")
        else:
            m_std = B.inc("scalar", lambda e: e.activation(st_std, st_msq, AF.Sqrt, bias=C["eps"][0:1, :]), "sA")
        B.wait("tensor", "sA", m_std)
        B.wait("tensor", "sV", state["psb_free"][1])
        B.em("tensor", lambda e: e.matmul(psb0[:, :], C["onesr"][0:1, :], st_mu, start=True, stop=True))
        if kind == "ssm":
            B.em("tensor", lambda e: e.matmul(psb1[:, :], C["onesr"][0:1, :], st_std, start=True, stop=True))
            m_bc = B.inc("tensor", lambda e: e.matmul(psr0[:, :], C["onesr"][0:1, :], st_rstd2, start=True, stop=True), "sT")
        else:
            m_bc = B.inc("tensor", lambda e: e.matmul(psb1[:, :], C["onesr"][0:1, :], st_std, start=True, stop=True), "sT")
        if kind == "ssm":
            B.wait("vector", "sT", m_bc)
            B.em("vector", lambda e: e.reciprocal(tmpc, psr0[:, :]))
            B.em("vector", lambda e: e.tensor_mul(tmpc, tmpc, psb1[:, :]))
            m_bc = B.inc("vector", lambda e: e.reciprocal(psb1[:, :], psb1[:, :]), "sV")
            state["psr_free"] = ("sV", m_bc)
            bc_sem = "sV"
        else:
            bc_sem = "sT"
        if kind == "ffn":
            def apply_fn():
                B.wait("vector", *state["garena_free"])
                B.wait("vector", "sT", m_bc)
                B.em("vector", lambda e: e.reciprocal(psb1[:, :], psb1[:, :]))
                for c in range(DC):
                    B.em("vector", lambda e, c=c: e.tensor_sub(tmpc, src_c(c), psb0[:, :]))
                    if c == DC - 1:
                        m_apply = B.inc("vector", lambda e, c=c: e.tensor_mul(
                            xnf(c), tmpc, psb1[:, :]), "sV")
                    else:
                        B.em("vector", lambda e, c=c: e.tensor_mul(xnf(c), tmpc, psb1[:, :]))
                state["psb_free"] = ("sV", m_apply)
                return ("sV", m_apply)
            if defer_apply:
                return apply_fn
            return apply_fn() + (None,)
        if bc_sem == "sT":
            B.wait("vector", "sT", m_bc)
        deferred = None
        if False:
            pass
        elif kind == "ssm":
            m_apply = 0
            for q in range(4):
                cs = [q * 2, q * 2 + 1]
                for c in cs:
                    B.em("vector", lambda e, c=c: e.tensor_sub(xln_ap(c), src_c(c), psb0[:, :]))
                    if c == cs[-1]:
                        m_vmul = B.inc("vector", lambda e, c=c: e.tensor_mul(
                            xln_ap(c), xln_ap(c), psb1[:, :]), "sV")
                    else:
                        B.em("vector", lambda e, c=c: e.tensor_mul(xln_ap(c), xln_ap(c), psb1[:, :]))
                B.wait("gpsimd", "sV", m_vmul)
                for c in cs:
                    B.em("gpsimd", lambda e, c=c: e.tensor_mul(x2S_ap(c), xln_ap(c), tmpc))
                    if c == cs[-1]:
                        m_apply = B.inc("gpsimd", lambda e, c=c: e.tensor_scalar(
                            xn2_ap(c), x2S_ap(c), C["gam"][:, c:c + 1], C["bet"][:, c:c + 1],
                            ALU.mult, ALU.add), "sP")
                    else:
                        B.em("gpsimd", lambda e, c=c: e.tensor_scalar(
                            xn2_ap(c), x2S_ap(c), C["gam"][:, c:c + 1], C["bet"][:, c:c + 1],
                            ALU.mult, ALU.add))
            deferred = [
                (lambda e, c=c, t0=t0: e.tensor_add(resid_ap(c, t0), resid_ap(c, t0), xln_ap(c)))
                for c in range(DC)]
            state["psb_free"] = ("sV", m_vmul)
            state["psr_free"] = ("sV", m_vmul)
            return ("sP", m_apply, deferred)
        elif kind == "outer":
            for c in range(DC):
                B.em("vector", lambda e, c=c: e.tensor_sub(tmpc, src_c(c), psb0[:, :]))
                if c == DC - 1:
                    m_apply = B.inc("vector", lambda e, c=c: e.tensor_mul(
                        xln_ap(c), tmpc, psb1[:, :]), "sV")
                else:
                    B.em("vector", lambda e, c=c: e.tensor_mul(xln_ap(c), tmpc, psb1[:, :]))
            deferred = [
                (lambda e, c=c, t0=t0: e.tensor_add(resid_ap(c, t0), resid_ap(c, t0), xln_ap(c)))
                for c in range(DC)]
            ret_sem = "sV"
        else:
            for c in range(DC):
                B.em("vector", lambda e, c=c: e.tensor_sub(tmpc, src_c(c), psb0[:, :]))
                if c == DC - 1:
                    m_vmul = B.inc("vector", lambda e, c=c: e.tensor_mul(
                        x2S_ap(c), tmpc, psb1[:, :]), "sV")
                else:
                    B.em("vector", lambda e, c=c: e.tensor_mul(x2S_ap(c), tmpc, psb1[:, :]))
            B.wait("gpsimd", "sV", m_vmul)
            for c in range(DC - 1):
                B.em("gpsimd", lambda e, c=c: e.tensor_scalar(
                    xn2_ap(c), x2S_ap(c), C["gam"][:, c:c + 1], C["bet"][:, c:c + 1],
                    ALU.mult, ALU.add))
            m_apply = B.inc("gpsimd", lambda e: e.tensor_scalar(
                xn2_ap(DC - 1), x2S_ap(DC - 1), C["gam"][:, DC - 1:DC], C["bet"][:, DC - 1:DC],
                ALU.mult, ALU.add), "sP")
            ret_sem = "sP"
        state["psb_free"] = (ret_sem, m_apply) if ret_sem == "sV" else ("sV", m_vmul)
        return (ret_sem, m_apply, deferred)

    def _slot_overlaps(slot_key):
        lo, hi = SLOT_RANGE[slot_key]
        return [k for k, (l2, h2) in SLOT_RANGE.items() if l2 < hi and lo < h2]

    def load_group(wname, oc0, noc, KC, slot_key):
        dt = wdt_of(wname)
        n_el = noc * KC * 128
        slot_ap = byte_slot(slot_key, n_el * (1 if dt == FP8 else 2), dt)
        waits = {}
        for k in _slot_overlaps(slot_key):
            sem, m = state["slot_free"].get(k, ("sT", 0))
            waits[sem] = max(waits.get(sem, 0), m)
        for sem, m in waits.items():
            B.wait("sync", sem, m)
        B.wait("sync", "sG", conv_m[wname][(oc0 + noc - 1) // conv_step[wname]])
        src = wbf[wname][oc0:oc0 + noc].rearrange("n p k o -> p n k o")
        return B.inc("sync", lambda e, o=slot_ap, i=src: e.dma_start(
            out=o, in_=i), "sS", 16), slot_ap

    def stage(wname, KC, OC, group, nslots, rhs_fn, evac_fn, rhs_ready, inject=None):
        for gi, oc0 in enumerate(range(0, OC, group)):
            if inject is not None:
                inject(gi)
            slot_key = ("k4", gi % nslots)
            m_load, pan = load_group(wname, oc0, group, KC, slot_key)
            B.wait("tensor", "sS", m_load)
            B.wait("tensor", rhs_ready[0], rhs_ready[1])
            for j in range(group):
                oc = oc0 + j
                bank = oc % 4
                bsem, bm = state["bank_free"][bank]
                B.wait("tensor", bsem, bm)
                for kc in range(KC - 1):
                    off = (j * KC + kc) * 128
                    B.em("tensor", lambda e, p=pan, o=off, kc=kc, b=bank: e.matmul(
                        ps[b][:, :], p[:, o:o + 128], rhs_fn(kc),
                        start=(kc == 0), stop=False))
                off = (j * KC + KC - 1) * 128
                m_pe = B.inc("tensor", lambda e, p=pan, o=off, b=bank: e.matmul(
                    ps[b][:, :], p[:, o:o + 128], rhs_fn(KC - 1),
                    start=(KC == 1), stop=True), "sT")
                if j == group - 1:
                    state["slot_free"][slot_key] = ("sT", m_pe)
                state["bank_free"][bank] = evac_fn(oc, ps[bank][:, :], m_pe)

    def emit_w12(tile, m_ln, xn_fn=None):
        xnf = xn_fn if xn_fn is not None else xn_ap
        for gi, oc0 in enumerate(range(0, HC, 2)):
            ka, kb = ("k2", (2 * gi) % 8), ("k2", (2 * gi + 1) % 8)
            m_l1, pan_a = load_group("w1", oc0, 2, DC, ka)
            m_l2, pan_b = load_group("w2", oc0, 2, DC, kb)
            B.wait("tensor", "sS", m_l2)
            B.wait("tensor", m_ln[0], m_ln[1])
            for j in range(2):
                oc = oc0 + j
                bA = (oc % 2) * 2
                bB = bA + 1
                for b in (bA, bB):
                    bsem, bm = state["bank_free"][b]
                    B.wait("tensor", bsem, bm)
                for kc in range(DC - 1):
                    off = (j * DC + kc) * 128
                    B.em("tensor", lambda e, p=pan_a, o=off, kc=kc, b=bA: e.matmul(
                        ps[b][:, :], p[:, o:o + 128], xnf(kc),
                        start=(kc == 0), stop=False))
                off = (j * DC + DC - 1) * 128
                m_pe1 = B.inc("tensor", lambda e, p=pan_a, o=off, b=bA: e.matmul(
                    ps[b][:, :], p[:, o:o + 128], xnf(DC - 1),
                    start=False, stop=True), "sT")
                for kc in range(DC - 1):
                    off = (j * DC + kc) * 128
                    B.em("tensor", lambda e, p=pan_b, o=off, kc=kc, b=bB: e.matmul(
                        ps[b][:, :], p[:, o:o + 128], xnf(kc),
                        start=(kc == 0), stop=False))
                off = (j * DC + DC - 1) * 128
                m_pe2 = B.inc("tensor", lambda e, p=pan_b, o=off, b=bB: e.matmul(
                    ps[b][:, :], p[:, o:o + 128], xnf(DC - 1),
                    start=False, stop=True), "sT")
                if j == 1:
                    state["slot_free"][ka] = ("sT", m_pe1)
                    state["slot_free"][kb] = ("sT", m_pe2)
                B.wait("scalar", "sT", m_pe1)
                B.wait("scalar", *state["tmps_free"][oc % 2])
                m_silu = B.inc("scalar", lambda e, b=bA, oc=oc: e.activation(
                    tmps[oc % 2], ps[b][:, :], AF.Silu, scale=C["s1"][:, oc:oc + 1]), "sA")
                state["bank_free"][bA] = ("sA", m_silu)
                B.wait("vector", "sT", m_pe2)
                B.wait("vector", "sA", m_silu)
                m_h = B.inc("vector", lambda e, b=bB, oc=oc: e.scalar_tensor_tensor(
                    h_ap(oc), ps[b][:, :], C["s2"][:, oc:oc + 1], tmps[oc % 2],
                    ALU.mult, ALU.mult), "sV")
                state["bank_free"][bB] = ("sV", m_h)
                state["tmps_free"][oc % 2] = ("sV", m_h)
        return sV.n

    def ffn_tile(tile, m_ln, xn_fn=None, g_fn=None, pend_mk=None, pend_at="h1",
                 apply_now=True):
        g_fn = g_fn if g_fn is not None else g_ap
        t0 = tile * T
        B.wait("tensor", *state["harena_free"])
        m_h = emit_w12(tile, m_ln, xn_fn=xn_fn)

        def ev_gelu_g(oc, p, m_pe):
            B.wait("scalar", "sT", m_pe)
            m = B.inc("scalar", lambda e, p=p, oc=oc: e.activation(
                g_fn(oc), p, AF.Gelu, scale=C["sh1"][:, oc:oc + 1]), "sA")
            return ("sA", m)

        stage("h1", HC, HC, 1, 4, h_ap, ev_gelu_g, ("sV", m_h))
        m_h1 = sA.n
        state["harena_free"] = ("sT", sT.n)
        apply_fn = pend_mk() if (pend_mk and pend_at == "h1") else None

        def ev_gelu_h(oc, p, m_pe):
            B.wait("scalar", "sT", m_pe)
            m = B.inc("scalar", lambda e, p=p, oc=oc: e.activation(
                h_ap(oc), p, AF.Gelu, scale=C["sh2"][:, oc:oc + 1]), "sA")
            return ("sA", m)

        stage("h2", HC, HC, 1, 4, g_fn, ev_gelu_h, ("sA", m_h1))
        m_h2 = sA.n
        state["garena_free"] = ("sT", sT.n)
        if pend_mk and pend_at == "h2":
            apply_fn = pend_mk()
        res = apply_fn() if (apply_fn is not None and apply_now) else apply_fn

        def ev_w3(oc, p, m_pe):
            B.wait("vector", "sT", m_pe)
            m = B.inc("vector", lambda e, p=p, oc=oc, t0=t0: e.scalar_tensor_tensor(
                resid_ap(oc, t0), p, C["s3h"][:, oc:oc + 1], resid_ap(oc, t0),
                ALU.mult, ALU.add), "sV")
            return ("sV", m)

        stage("w3", HC, DC, 1, 4, h_ap, ev_w3, ("sA", m_h2))
        state["harena_free"] = ("sT", sT.n)
        return res

    def emit_ffn(ln0_m=None, after_tile=None):
        m_lns = {0: ln0_m if ln0_m is not None else emit_ln(0, "ffn")[:2]}
        B.wait("vector", "sD", m_x)
        for tile in range(NT):
            pm = (lambda t=tile: emit_ln(t + 1, "ffn", defer_apply=True)) \
                if tile + 1 < NT else None
            r = ffn_tile(tile, m_lns[tile], pend_mk=pm)
            if r is not None:
                m_lns[tile + 1] = r
            if after_tile is not None:
                after_tile(tile)
        state["resid_ready"] = ("sV", sV.n)

    ssmA = {"pend": None, "red": {}, "var": {}, "sq": {}}

    def ssm_partA_copy(tile):
        t0 = tile * T
        src3 = resid3[:, :, t0:t0 + T]
        B.wait("vector", "sT", ssmA["red"].get(tile - 1, 0))
        m_xb = B.inc("vector", lambda e, s3=src3: e.tensor_copy(xb3_ssm, s3), "sV")
        m_x2 = B.inc("vector", lambda e: e.tensor_mul(x23_ssm, xb3_ssm, xb3_ssm), "sV")
        ssmA["pend"] = (tile, m_xb, m_x2)

    def ssm_partA_reduceT():
        if ssmA["pend"] is None:
            return None
        tile, m_xb, m_x2 = ssmA["pend"]
        ssmA["pend"] = None
        R = ST_ROW[tile]
        B.wait("tensor", "sV", max(m_xb, state["psr_free"][1]))
        for c in range(DC):
            B.em("tensor", lambda e, c=c, R=R: e.matmul(
                psr0[R:R + 1, :], C["onesb"][:, :], xbS_ap(c), start=(c == 0), stop=(c == DC - 1)))
        B.wait("tensor", "sV", m_x2)
        for c in range(DC - 1):
            B.em("tensor", lambda e, c=c, R=R: e.matmul(
                psr1[R:R + 1, :], C["onesb"][:, :], x2S_ap(c), start=(c == 0), stop=False))
        m_red = B.inc("tensor", lambda e, R=R: e.matmul(
            psr1[R:R + 1, :], C["onesb"][:, :], x2S_ap(DC - 1), start=False, stop=True), "sT")
        ssmA["red"][tile] = m_red
        return tile

    def ssm_partA_statsV(tile):
        R = ST_ROW[tile]
        m_red = ssmA["red"][tile]
        B.wait("vector", "sT", m_red)
        B.em("vector", lambda e, t=tile, R=R: e.tensor_scalar_mul(
            st_row(t, 0), psr0[R:R + 1, :], 1.0 / DM))
        B.em("vector", lambda e, t=tile: e.tensor_mul(st_row(t, 1), st_row(t, 0), st_row(t, 0)))
        m_var = B.inc("vector", lambda e, t=tile, R=R: e.scalar_tensor_tensor(
            st_row(t, 2), psr1[R:R + 1, :], 1.0 / DM, st_row(t, 1), ALU.mult, ALU.subtract), "sV")
        ssmA["var"][tile] = m_var
        state["psr_free"] = ("sV", m_var)

    def ssm_partA_reduce():
        t = ssm_partA_reduceT()
        if t is not None:
            ssm_partA_statsV(t)

    def ssm_partA(tile):
        ssm_partA_reduce()
        ssm_partA_copy(tile)

    def ssm_sqrts(ts):
        B.wait("scalar", "sV", ssmA["var"][ts[-1]])
        for t in ts:
            R = ST_ROW[t]
            B.em("scalar", lambda e, t=t, R=R: e.activation(
                st_row(t, 3), st_row(t, 2), AF.Sqrt, bias=C["eps"][R:R + 1, :]))
            m = B.inc("scalar", lambda e, t=t, R=R: e.activation(
                st_row(t, 4), st_row(t, 2), AF.Sqrt, bias=C["eps2"][R:R + 1, :], scale=1.0 + EPS), "sA")
            ssmA["sq"][t] = m

    def emit_ffn2(m_ln0, finish_stage4):
        pm1 = lambda: emit_ln(1, "ffn", defer_apply=True,
                              scratch=(xbL1_ap, x2L1_ap, xb3L1), xn_fn=xbL1_ap,
                              copy_wait=True)
        apply1 = ffn_tile(0, m_ln0, xn_fn=xn0_ap, g_fn=g0_ap,
                          pend_mk=pm1, pend_at="h2", apply_now=False)
        m_lns = {1: apply1()}
        finish_stage4()
        for tile in range(1, NT):
            pm = (lambda t=tile: emit_ln(t + 1, "ffn", defer_apply=True)) \
                if tile + 1 < NT else None
            r = ffn_tile(tile, m_lns[tile], pend_mk=pm,
                         xn_fn=xbL1_ap if tile == 1 else None)
            if r is not None:
                m_lns[tile + 1] = r
        state["resid_ready"] = ("sV", sV.n)

    def emit_ssm(m_acol):
        m_inner_l = {}
        B.wait("gpsimd", "sD", m_consts)

        def ssm_bc(tile, m_prev_apply):
            R = ST_ROW[tile]
            B.wait("tensor", "sA", ssmA["sq"][tile])
            B.wait("tensor", "sV", state["psb_free"][1])
            B.wait("tensor", "sV", state["psr_free"][1])
            B.em("tensor", lambda e, t=tile, R=R: e.matmul(
                psb0[:, :], C["onesr"][R:R + 1, :], st_row(t, 0), start=True, stop=True))
            B.em("tensor", lambda e, t=tile, R=R: e.matmul(
                psb1[:, :], C["onesr"][R:R + 1, :], st_row(t, 3), start=True, stop=True))
            m_bcT = B.inc("tensor", lambda e, t=tile, R=R: e.matmul(
                psr0[:, :], C["onesr"][R:R + 1, :], st_row(t, 4), start=True, stop=True), "sT")
            B.wait("vector", "sP", m_prev_apply)
            B.wait("vector", "sT", m_bcT)
            B.em("vector", lambda e: e.reciprocal(tmpc, psr0[:, :]))
            B.em("vector", lambda e: e.tensor_mul(tmpc, tmpc, psb1[:, :]))
            m_fix = B.inc("vector", lambda e: e.reciprocal(psb1[:, :], psb1[:, :]), "sV")
            state["psr_free"] = ("sV", m_fix)

        def ssm_apply(tile):
            t0 = tile * T
            src_c = lambda c: resid_ap(c, t0)
            par = tile % 2
            m_apply = 0
            for q in range(4):
                cs = [q * 2, q * 2 + 1]
                for c in cs:
                    B.em("vector", lambda e, c=c: e.tensor_sub(xln_ap(c), src_c(c), psb0[:, :]))
                    if c == cs[-1]:
                        m_vmul = B.inc("vector", lambda e, c=c: e.tensor_mul(
                            xln_ap(c), xln_ap(c), psb1[:, :]), "sV")
                    else:
                        B.em("vector", lambda e, c=c: e.tensor_mul(xln_ap(c), xln_ap(c), psb1[:, :]))
                B.wait("gpsimd", "sV", m_vmul)
                for c in cs:
                    B.em("gpsimd", lambda e, c=c: e.tensor_mul(x2S_ap(c), xln_ap(c), tmpc))
                    if c == cs[-1]:
                        m_apply = B.inc("gpsimd", lambda e, c=c, par=par: e.tensor_scalar(
                            xn2p_ap(par, c), x2S_ap(c), C["gam"][:, c:c + 1], C["bet"][:, c:c + 1],
                            ALU.mult, ALU.add), "sP")
                    else:
                        B.em("gpsimd", lambda e, c=c, par=par: e.tensor_scalar(
                            xn2p_ap(par, c), x2S_ap(c), C["gam"][:, c:c + 1], C["bet"][:, c:c + 1],
                            ALU.mult, ALU.add))
                for c in cs:
                    B.em("vector", lambda e, c=c, t0=t0: e.tensor_add(
                        resid_ap(c, t0), resid_ap(c, t0), xln_ap(c)))
            state["psb_free"] = ("sV", m_vmul)
            return ("sP", m_apply, None)

        if not ssmA["var"]:
            for t in range(NT - 1):
                ssm_partA(t)
            ssm_partA_reduce()
            ssm_partA_copy(NT - 1)
        ssm_sqrts([0, 1, 2])
        if dbg_small is not None:
            B.wait("gpsimd", "sA", ssmA["sq"][2])
            gdma(dbg_st, garena_f[0:65, 12288:14848])
        B.wait("vector", *state["harena_free"])
        B.wait("vector", *state["garena_free"])
        ssm_bc(0, 0)
        t3 = ssm_partA_reduceT()
        B.wait("gpsimd", "sT", ssmA["red"][3])
        m_inner_l[0] = ssm_apply(0)
        ssm_partA_statsV(t3)
        ssm_sqrts([3])
        for tile in range(NT):
            t0 = tile * T
            m_inner = m_inner_l[tile]

            def inject(gi, t=tile + 1, prev=m_inner):
                if t < NT:
                    if gi == 0:
                        ssm_bc(t, prev[1])
                    elif gi == 2:
                        m_inner_l[t] = ssm_apply(t)

            def ev_xz(oc, p, m_pe, tile=tile):
                B.wait("scalar", "sT", m_pe)
                if oc < 8:
                    m = B.inc("scalar", lambda e, p=p, oc=oc: e.activation(
                        xp_ap(oc), p, AF.Silu, bias=C["binx"][:, oc:oc + 1]), "sA")
                else:
                    m = B.inc("scalar", lambda e, p=p, oc=oc, tile=tile: e.activation(
                        z_ap(tile, oc - 8), p, AF.Sigmoid, bias=C["binz"][:, oc - 8:oc - 7]), "sA")
                return ("sA", m)

            stage("win", DC, 16, 2, 2,
                  lambda c, par=tile % 2: xn2p_ap(par, c), ev_xz,
                  (m_inner[0], m_inner[1]), inject=inject)
            m_xp = sA.n
            bank = 0
            bsem, bm = state["bank_free"][bank]
            B.wait("tensor", bsem, bm)
            B.wait("tensor", "sA", m_xp)
            for kc in range(DC - 1):
                B.em("tensor", lambda e, kc=kc, b=bank: e.matmul(
                    ps[b][0:64, :], C["wxpb"][:, kc * 64:(kc + 1) * 64], xp_ap(kc),
                    start=(kc == 0), stop=False))
            m_upe = B.inc("tensor", lambda e, b=bank: e.matmul(
                ps[b][0:64, :], C["wxpb"][:, (DC - 1) * 64:DC * 64], xp_ap(DC - 1),
                start=False, stop=True), "sT")
            B.wait("vector", "sT", m_upe)
            m_u = B.inc("vector", lambda e, t0=t0, b=bank: e.tensor_scalar_add(
                u_ap(t0, T), ps[b][0:64, :], C["bxp"][:, :]), "sV")
            state["bank_free"][bank] = ("sV", m_u)

        B.wait("vector", "sA", m_acol)
        m_scan1 = B.inc("vector", lambda e: e.tensor_tensor_scan(
            hs_ap(0, TOK), C["acol"].broadcast_to([64, TOK])[:, :], u_ap(0, TOK),
            0.0, ALU.mult, ALU.add), "sV")
        B.wait("gpsimd", "sV", m_scan1)
        m_f = gdma(ag_in, hs_ap(TOK - 1, 1))
        B.wait("gpsimd", "sG", m_f)
        m_cc = B.inc("gpsimd", lambda e: e.collective_compute(
            "AllGather", ALU.bypass,
            replica_groups=[[0, 1, 2, 3, 4, 5, 6, 7]],
            ins=[ag_in.opt()], outs=[ag_out.opt()]), "sC", 1)
        B.wait("gpsimd", "sC", m_cc)
        m_fsb = gdma(C["fsb8"][:, :], ag_out.rearrange("(r p) one -> r (p one)", p=64))
        B.wait("tensor", "sG", m_fsb)
        B.wait("tensor", state["psr_free"][0], state["psr_free"][1])
        m_ftr = B.inc("tensor", lambda e: e.matmul(
            psr1[0:64, 0:8], C["fsb8"][:, :], C["eye8"][:, :],
            start=True, stop=True), "sT")
        B.wait("vector", "sT", m_ftr)
        B.wait("vector", "sA", m_wqe)
        m_wf = B.inc("vector", lambda e: e.tensor_mul(
            C["wq"][:, :], C["wqe"][:, :], psr1[0:64, 0:8]), "sV")
        B.wait("vector", "sV", m_wf)
        m_c = B.inc("vector", lambda e: e.reduce_sum(
            C["ccol"][:, :], C["wq"][:, :], axis=mybir.AxisListType.X), "sV")
        B.wait("vector", "sV", m_c)
        m_scan2 = B.inc("vector", lambda e: e.tensor_tensor_scan(
            hs_ap(0, TOK), C["acol"].broadcast_to([64, TOK])[:, :], u_ap(0, TOK),
            C["ccol"][:, :], ALU.mult, ALU.add), "sV")
        if dbg_small is not None:
            B.wait("gpsimd", "sV", m_scan2)
            gdma(dbg_u, u_ap(0, TOK))
            gdma(dbg_h, hs_ap(0, TOK))
            gdma(dbg_small[:, 0:1], C["ccol"][:, :], allow_slow_non_contiguous=True)
            gdma(dbg_small[:, 1:5], C["fsb"][:, 0:4], allow_slow_non_contiguous=True)
            gdma(dbg_small[:, 5:9], C["wq"][:, 0:4], allow_slow_non_contiguous=True)
            gdma(dbg_small[:, 9:10], C["acol"][:, :], allow_slow_non_contiguous=True)
            gdma(dbg_small[:, 10:11], C["texp"][:, :], allow_slow_non_contiguous=True)

        B.wait("vector", "sA", B.sems["sA"].n)
        m_cc_ag = []
        for ti, tile in enumerate((0, 2, 1, 3)):
            t0 = tile * T
            m_yd = 0
            for oc in range(DC):
                bank = (ti * DC + oc) % 4
                bsem, bm = state["bank_free"][bank]
                B.wait("tensor", bsem, bm)
                B.wait("tensor", "sV", m_scan2)
                m_pe = B.inc("tensor", lambda e, oc=oc, t0=t0, b=bank: e.matmul(
                    ps[b][:, :], C["dmat"][:, oc * 128:(oc + 1) * 128], hs_ap(t0, T),
                    start=True, stop=True), "sT")
                B.wait("vector", "sT", m_pe)
                m_yd = B.inc("vector", lambda e, tile=tile, oc=oc, b=bank: e.tensor_mul(
                    yd_ap(tile, oc), ps[b][:, :], z_ap(tile, oc)), "sV")
                state["bank_free"][bank] = ("sV", m_yd)
            B.wait("gpsimd", "sV", m_yd)
            m_agi = gdma(
                ag_ydi[tile % 2].rearrange("p (o w t) -> p o w t", o=8, w=2)[:, :, tile // 2, :],
                harena.reshape([128, 12, 2048])[:, 2:10, tile * T:(tile + 1) * T])
            if ti % 2 == 1:
                B.wait("gpsimd", "sG", m_agi)
                if m_cc_ag:
                    B.wait("gpsimd", "sC", m_cc_ag[-1])
                m_cc_ag.append(B.inc("gpsimd", lambda e, ab=tile % 2: e.collective_compute(
                    "AllGather", ALU.bypass,
                    replica_groups=[[0, 1, 2, 3, 4, 5, 6, 7]],
                    ins=[ag_ydi[ab].opt()], outs=[ag_ydo[ab].opt()]), "sC", 1))
        m_yd_all = sV.n

        sGs = B.sems["sG"]
        m_fs = []
        for ab in range(2):
            B.wait("gpsimd", "sC", m_cc_ag[ab])
            agv = ag_ydo[ab].rearrange("(r p) (o w t) -> r p o w t", p=128, o=8, w=2)

            def em_fetch(e, agv=agv, ab=ab):
                for arm in nc.switch_core_id(engines=[e]):
                    cid = arm.logical % 8
                    q0, q1, tb = cid // 2, 4 + cid // 2, cid % 2
                    for par, q in ((0, q0), (1, q1)):
                        if ab == 0:
                            dst = yd2r[:, 2:10, par, 0:512]
                        else:
                            dst = garBv[:, :, par, :]
                        e.dma_start(out=dst, in_=agv[q, :, :, tb, :]).then_inc(sGs.h, 16)

            B.em("gpsimd", em_fetch)
            sGs.n += 32
            m_fs.append(sGs.n)
        m_f1 = m_fs[-1]

        m_wo = []
        for gidx in range(2):
            B.wait("sync", "sV", m_yd_all)
            B.wait("sync", "sG", conv_m["wout"][gidx])
            src = wbf["wout"][gidx * 4:(gidx + 1) * 4].rearrange("n p k o -> p n k o")
            dst = garena[:, 12288 + gidx * 4096: 12288 + (gidx + 1) * 4096]
            m_wo.append(B.inc("sync", lambda e, o=dst, i=src: e.dma_start(
                out=o, in_=i), "sS", 16))

        def wout_sl(oc, kc):
            g, j = oc // 4, oc % 4
            base = 12288 + g * 4096 + (j * DC + kc) * 128
            return garena[:, base: base + 128]

        def stage4_tile(tile):
            for oc in range(DC):
                bank = (tile * DC + oc) % 4
                bsem, bm = state["bank_free"][bank]
                B.wait("tensor", bsem, bm)
                B.wait("tensor", "sS", m_wo[oc // 4])
                B.wait("tensor", "sG", m_fs[tile // 2])
                for kc in range(DC - 1):
                    B.em("tensor", lambda e, oc=oc, kc=kc, tile=tile, b=bank: e.matmul(
                        ps[b][:, :], wout_sl(oc, kc), yd2_rhs(kc, tile),
                        start=(kc == 0), stop=False))
                m_pe = B.inc("tensor", lambda e, oc=oc, tile=tile, b=bank: e.matmul(
                    ps[b][:, :], wout_sl(oc, DC - 1), yd2_rhs(DC - 1, tile),
                    start=False, stop=True), "sT")
                B.wait("vector", "sT", m_pe)
                m_y = B.inc("vector", lambda e, oc=oc, tile=tile, b=bank: e.scalar_tensor_tensor(
                    resid_int(oc, tile), ps[b][:, :], C["bout"][:, oc:oc + 1], resid_int(oc, tile),
                    ALU.add, ALU.add), "sV")
                state["bank_free"][bank] = ("sV", m_y)

        special = "ffn2" in phases and tap_after is None
        m_ln0_next = None
        B.wait("vector", "sP", state.get("m_adds", 0))
        stage4_tile(0)
        if special:
            m_ln0_next = emit_ln(0, "ffn", scratch=(xb0_ap, x2_ap, xb3_0),
                                 xn_fn=xn0_ap)[:2]
        stage4_tile(1)
        state["harena_free"] = ("sT", sT.n)
        for k in [("k4", 2), ("k4", 3), ("k2", 4), ("k2", 5), ("k2", 6), ("k2", 7)]:
            state["slot_free"][k] = ("sV", m_scan2)

        def finish_stage4():
            stage4_tile(2)
            stage4_tile(3)
            state["garena_free"] = ("sT", sT.n)
            state["resid_ready"] = ("sV", sV.n)

        if not special:
            state["garena_free"] = ("sT", sT.n)
            finish_stage4()
            return None, None
        return m_ln0_next, finish_stage4

    def maybe_tap(ph):
        if tap_after == ph:
            B.wait("gpsimd", *state["resid_ready"])
            gdma(dbg_tap.rearrange("(c p) t -> p c t", p=128), resid3[:, :, :])

    m_ln0_ffn2, fs4 = None, None
    if "ffn1" in phases:
        emit_ffn(after_tile=ssm_partA if "ssm" in phases else None)
        maybe_tap("ffn1")
    if "ssm" in phases:
        m_ln0_ffn2, fs4 = emit_ssm(m_acol)
        maybe_tap("ssm")
    if "ffn2" in phases:
        if fs4 is not None:
            emit_ffn2(m_ln0_ffn2, fs4)
        else:
            emit_ffn()

    B.wait("gpsimd", *state["resid_ready"])
    m_out = gdma(yout.rearrange("(c p) t -> p c t", p=128), resid3[:, :, :])
    B.wait("gpsimd", "sG", m_out)

    with nc.Block() as block:
        @block.sync
        def _(e):
            B.run(e, "sync")

        @block.scalar
        def _(e):
            B.run(e, "scalar")

        @block.vector
        def _(e):
            B.run(e, "vector")

        @block.tensor
        def _(e):
            B.run(e, "tensor")

        @block.gpsimd
        def _(e):
            B.run(e, "gpsimd")

    stack.close()
    return nc


def _mk_in_maps(inputs):
    x = np.asarray(inputs["L0_attn_x"], dtype=np.float32)
    b_in = np.asarray(inputs["L0_attn_b_in"], dtype=np.float32)
    base = dict(
        d_s1=_coltile(inputs["L0_ffn_w1_s"], 32),
        d_s2=_coltile(inputs["L0_ffn_w2_s"], 32),
        d_sh1=_coltile(inputs["L0_ffn_h1_s"], 32),
        d_sh2=_coltile(inputs["L0_ffn_h2_s"], 32),
        d_s3=_coltile(inputs["L0_ffn_w3_s"], 8),
        d_bout=_coltile(inputs["L0_attn_b_out"], 8),
        d_gam=_coltile(inputs["L0_attn_gamma"], 8),
        d_bet=_coltile(inputs["L0_attn_beta"], 8),
        d_binx=_coltile(b_in[:1024], 8),
        d_binz=_coltile(b_in[1024:], 8),
        d_alog=np.ascontiguousarray(np.asarray(inputs["L0_attn_A_log"], dtype=np.float32).reshape(64, 1)),
        d_bxp=np.ascontiguousarray(np.asarray(inputs["L0_attn_b_xp"], dtype=np.float32).reshape(64, 1)),
        d_wxp=np.ascontiguousarray(
            np.asarray(inputs["L0_attn_W_xp"], dtype=np.float32).T.reshape(8, 128, 64)
            .transpose(1, 0, 2).reshape(128, 512)),
        d_dmat=np.ascontiguousarray(np.asarray(inputs["L0_attn_Dmat"], dtype=np.float32)),
        d_eye8=np.eye(8, dtype=np.float32),
        wf_w1=_panels(inputs["L0_ffn_w1_w"]),
        wf_w2=_panels(inputs["L0_ffn_w2_w"]),
        wf_h1=_panels(inputs["L0_ffn_h1_w"]),
        wf_h2=_panels(inputs["L0_ffn_h2_w"]),
        wf_w3=_panels(inputs["L0_ffn_w3_w"]),
        wf_win=_panels(inputs["L0_attn_W_in"]),
        wf_wout=_panels(inputs["L0_attn_W_out"]),
    )
    in_maps = []
    for c in range(N_CORES):
        b, r = c // 4, c % 4
        m = dict(base)
        m["xin"] = np.ascontiguousarray(x[b, r * TOK:(r + 1) * TOK, :].T)
        kv = np.zeros((64, 8), dtype=np.float32)
        for q in range(8):
            qq = q - 4 * b
            kv[:, q] = 2048.0 * (r - 1 - qq) if (0 <= qq < r) else 1e9
        m["d_kvec"] = kv
        in_maps.append(m)
    return in_maps


def kernel(**inputs):
    in_maps = _mk_in_maps(inputs)
    nc = build_nc()
    res = run_bass_kernel_spmd(nc, in_maps, core_ids=list(range(N_CORES)))
    out = np.empty((2, 8192, 1024), dtype=np.float32)
    for c in range(N_CORES):
        b, r = c // 4, c % 4
        out[b, r * TOK:(r + 1) * TOK, :] = np.asarray(res.results[c]["yout"]).T
    return out
